# version 09
# baseline (speedup 1.0000x reference)
import contextlib
import math

import numpy as np
import ml_dtypes

import concourse.bass as bass
import concourse.mybir as mybir
from concourse.bass_utils import run_bass_kernel_spmd

F32 = mybir.dt.float32
BF16 = mybir.dt.bfloat16
I32 = mybir.dt.int32
AF = mybir.ActivationFunctionType
ALU = mybir.AluOpType
AX = mybir.AxisListType

DBG_D = False
DBGF = []
E1_STOP = 0
T = 2048
NT = T // 128
D = 1024
DC = D // 128
NH = 16
HD = 64
MAGIC = 12582912.0
QMAX = 127.0


class Sem:
    def __init__(self, h):
        self.h = h
        self.n = 0


class Ctx:
    def __init__(self, name):
        self.nc = bass.Bass("TRN2", target_bir_lowering=False)
        self.st = contextlib.ExitStack()
        self.name = name
        self.sec = ""
        self.io = {}
        self.s = {}
        self.bufs = {}
        self._scopes = {}
        self._local = {}
        self._alloc_log = []

    def section(self, sec):
        self.sec = sec

    def push(self):
        return len(self._alloc_log)

    def pop(self, mark):
        for nm in reversed(self._alloc_log[mark:]):
            if nm in self._scopes:
                self.free_u(nm)
        del self._alloc_log[mark:]

    def start(self):
        for nm in ["LD", "GD", "TW", "VW", "SW", "GW", "ST", "PL"]:
            self.s[nm] = Sem(self.st.enter_context(self.nc.semaphore(nm)))
        pfA = self.ps("pfA", [128, 2048], F32)
        for i in range(4):
            self.bufs[f"pf{i}"] = pfA.ap()[:, i * 512:(i + 1) * 512]
        for i in range(4, 6):
            self.ps(f"pf{i}", [128, 512], F32)
        for i in range(2):
            self.ps(f"pb{i}", [128, 1024], BF16)
        self.blk = self.st.enter_context(self.nc.Block())
        return self

    def close(self):
        self.st.close()

    def sb(self, name, shape, dt, side=None):
        es = contextlib.ExitStack()
        uname = f"{self.sec}_{name}" if self.sec else name
        t = es.enter_context(self.nc.sbuf_tensor(uname, shape, dt, side=side))
        self.bufs[name] = t
        self._local[name] = uname
        self._scopes[uname] = es
        self._alloc_log.append(uname)
        return t

    def sbc(self, name, shape, dt, side=None):
        if name in self.bufs:
            return self.bufs[name]
        return self.sb(name, shape, dt, side=side)

    def free(self, *names):
        for n in names:
            un = self._local.get(n, n) if n in self._local or n in self._scopes \
                else n
            if un not in self._scopes and n in self._scopes:
                un = n
            self._scopes.pop(un).close()
            self.bufs.pop(n, None)

    def free_u(self, uname):
        self._scopes.pop(uname).close()
        for k, v in list(self._local.items()):
            if v == uname:
                self._local.pop(k)
                self.bufs.pop(k, None)

    def ps(self, name, shape, dt=F32):
        t = self.st.enter_context(self.nc.psum_tensor(name, shape, dt))
        self.bufs[name] = t
        return t

    def din(self, name, shape, dt):
        if name in self.io:
            return self.io[name]
        return self.nc.dram_tensor("i_" + name, list(shape), dt,
                                   kind="ExternalInput").ap()

    def dout(self, name, shape, dt):
        if name in self.io:
            return self.io[name]
        return self.nc.dram_tensor("o_" + name, list(shape), dt,
                                   kind="ExternalOutput").ap()

    def T(self, f):
        self.blk.tensor(f)

    def V(self, f):
        self.blk.vector(f)

    def S(self, f):
        self.blk.scalar(f)

    def G(self, f):
        self.blk.gpsimd(f)

    def Y(self, f):
        self.blk.sync(f)

    def inc(self, sem, instr, n=1):
        instr.then_inc(self.s[sem].h, n)
        self.s[sem].n += n
        return instr

    def edge(self, eng, sem, instr, n=1):
        self.inc(sem, instr, n)
        eng.wait_ge(self.s[sem].h, self.s[sem].n)
        return instr

    def waits(self, eng, *sems):
        for nm in sems:
            if self.s[nm].n > 0:
                eng.wait_ge(self.s[nm].h, self.s[nm].n)

    def barrier_load(self, eng):
        self.waits(eng, "LD", "GD")

    def barrier_all(self, eng):
        self.waits(eng, "LD", "GD", "TW", "VW", "SW", "GW", "ST")


def emit_identity(C):
    if "ident_bf" in C.bufs:
        return C.bufs["ident_bf"][:, :], C.bufs["ident_f32"][:, :]
    sec = C.sec
    C.sec = ""
    cols = C.sb("id_cols", [128, 128], F32)
    rowv = C.sb("id_rowv", [128, 1], F32)
    idb = C.sb("ident_bf", [128, 128], BF16)
    idf = C.sb("ident_f32", [128, 128], F32)
    C.bufs["ident_bf"] = idb
    C.bufs["ident_f32"] = idf

    def _g(g):
        g.iota(cols[:], [[1, 128]], channel_multiplier=0,
               allow_small_or_imprecise_dtypes=True)
        C.inc("GW", g.iota(rowv[:], [[1, 1]], channel_multiplier=1,
                           allow_small_or_imprecise_dtypes=True))
    C.G(_g)

    def _v(v):
        C.waits(v, "GW")
        v.tensor_scalar(idb[:], cols[:], rowv[:, 0:1], None, ALU.is_equal)
        C.inc("VW", v.tensor_scalar(idf[:], cols[:], rowv[:, 0:1], None,
                                    ALU.is_equal))
    C.V(_v)
    C.sec = sec
    for nm in ("id_cols", "id_rowv", "ident_bf", "ident_f32"):
        if nm in C._alloc_log:
            C._alloc_log.remove(nm)
    return idb[:, :], idf[:, :]


def emit_transposes(C, pairs, idb, ps_name="pb", dt=BF16):
    nbank = 2
    pss = [C.bufs[f"{ps_name}{i}"] for i in range(nbank)]

    def _t(t):
        C.barrier_all(t)
        for i, (src, dst) in enumerate(pairs):
            if i >= nbank:
                t.wait_ge(C.s["VW"].h, C.s["VW"].n + (i - nbank + 1))
            C.inc("TW", t.transpose(pss[i % nbank][0:src.shape[-1], 0:src.shape[0]],
                                    src, idb))
    C.T(_t)

    def _v(v):
        base = C.s["TW"].n - len(pairs)
        for i, (src, dst) in enumerate(pairs):
            v.wait_ge(C.s["TW"].h, base + i + 1)
            C.inc("VW", v.tensor_copy(dst, pss[i % nbank][0:src.shape[-1],
                                                          0:src.shape[0]]))
    C.V(_v)


def emit_ln_tiles(C, x_src_tiles, xn_dst_tiles, stats_dst, eps=1e-5):
    n = len(x_src_tiles)
    rs = C.sbc("ln_rs", [128, 1], F32)
    sq = C.sbc("ln_sq", [128, 1024], F32)
    ssq = C.sbc("ln_ssq", [128, 1], F32)
    var = C.sbc("ln_var", [128, 1], F32)
    std = C.sbc("ln_std", [128, 1], F32)

    for i in range(n):
        x = x_src_tiles[i]
        xn = xn_dst_tiles[i]
        mu = stats_dst[:, 2 * i:2 * i + 1]
        rstd = stats_dst[:, 2 * i + 1:2 * i + 2]

        def _v(v, x=x, xn=xn, mu=mu, rstd=rstd):
            C.barrier_all(v)
            C.edge(v, "VW", v.reduce_sum(rs[:], x, AX.X))
            C.edge(v, "VW", v.tensor_scalar(mu[:], rs[:], 1.0 / 1024.0, None,
                                            ALU.mult))
            C.inc("VW", v.tensor_scalar(xn, x, mu[:, 0:1], None, ALU.subtract))
        C.V(_v)

        def _s(s, xn=xn):
            C.waits(s, "VW")
            C.inc("SW", s.activation(sq[:], xn, AF.Square,
                                     accum_out=ssq[:, 0:1]))
        C.S(_s)

        def _v2(v, xn=xn, rstd=rstd):
            C.waits(v, "SW")
            C.inc("VW", v.tensor_scalar(var[:], ssq[:], 1.0 / 1024.0, eps,
                                        ALU.mult, ALU.add))
        C.V(_v2)

        def _s2(s):
            C.waits(s, "VW")
            C.inc("SW", s.activation(std[:], var[:], AF.Sqrt))
        C.S(_s2)

        def _v3(v, xn=xn, rstd=rstd):
            C.waits(v, "SW")
            C.edge(v, "VW", v.reciprocal(rstd[:], std[:]))
            C.inc("VW", v.tensor_scalar(xn, xn, rstd[:, 0:1], None, ALU.mult))
        C.V(_v3)


_AM_CNT = [0]


def emit_absmax_tiles(C, tiles, out11):
    n = len(tiles)
    _AM_CNT[0] += 1
    u = _AM_CNT[0]
    pm = C.sb(f"am_part{u}", [128, max(n, 1)], F32)
    pmo = C.sb(f"am_parto{u}", [128, 1], F32)
    pmt = C.bufs["pf4"]
    row = C.sb(f"am_row{u}", [1, 128], F32)
    idf = C.bufs["ident_f32"][:, :]

    def _v(v):
        C.barrier_all(v)
        for i, tl in enumerate(tiles):
            v.reduce_max(pm[:, i:i + 1], tl, AX.X, apply_absolute_value=True)
        C.edge(v, "VW", v.engine_nop())
        C.inc("VW", v.reduce_max(pmo[:, 0:1], pm[:, 0:n], AX.X))
    C.V(_v)

    def _t(t):
        C.waits(t, "VW")
        C.inc("TW", t.transpose(pmt[0:1, 0:128], pmo[:, 0:1], idf))
    C.T(_t)

    def _v2(v):
        C.waits(v, "TW")
        C.edge(v, "VW", v.tensor_copy(row[:], pmt[0:1, 0:128]))
        C.inc("VW", v.reduce_max(out11, row[0:1, :], AX.X,
                                 apply_absolute_value=True))
    C.V(_v2)


def load_weight_T(C, name, w_ap, d_out, d_in, idb, keep_nat=False, side=None):
    ncd = d_in // 128
    nco = d_out // 128
    wt = C.sb(name + "_T", [128, ncd * d_out], BF16, side=side)
    nat = C.sb(name + "_nat", [128, nco * d_in], BF16, side="right")

    def _g(g):
        C.barrier_all(g)
        for oc in range(nco):
            C.inc("GD", g.dma_start(
                out=nat[:, oc * d_in:(oc + 1) * d_in],
                in_=w_ap[oc * 128:(oc + 1) * 128, :]), 16)
    C.G(_g)

    pairs = []
    for oc in range(nco):
        for dc in range(ncd):
            src = nat[:, oc * d_in + dc * 128: oc * d_in + (dc + 1) * 128]
            dst = wt[:, dc * d_out + oc * 128: dc * d_out + (oc + 1) * 128]
            pairs.append((src, dst))
    emit_transposes(C, pairs, idb)
    if not keep_nat:
        C.free(name + "_nat")
        nat = None
    return wt, nat


def emit_bias_cols(C, b_ap, n, name):
    ncb = n // 128
    row = C.sb(name + "_row", [1, n], F32, side="right")
    col = C.sb(name + "_col", [128, ncb], F32)
    idf = C.bufs["ident_f32"][:, :]
    pst = C.bufs["pf4"]

    def _y(y):
        C.barrier_all(y)
        C.inc("LD", y.dma_start(out=row[:], in_=b_ap.rearrange("n -> 1 n") if len(b_ap.shape) == 1 else b_ap), 16)
    C.Y(_y)

    for c in range(ncb):
        def _t(t, c=c):
            C.barrier_all(t)
            C.inc("TW", t.transpose(pst[0:128, 0:1], row[0:1, c * 128:(c + 1) * 128], idf[0:1, 0:1]))
        C.T(_t)

        def _v(v, c=c):
            C.waits(v, "TW")
            C.inc("VW", v.tensor_copy(col[:, c:c + 1], pst[0:128, 0:1]))
        C.V(_v)
    C.free(name + "_row")
    return col, None


def emit_proj_token_tile(C, ps4, xnT_lhs_tile_views, wT, brow, ones1, dst_bf16,
                         d_in=D):
    ncd = d_in // 128

    def _t(t):
        C.barrier_all(t)
        for half in range(2):
            ps = ps4[half]
            for dc in range(ncd):
                t.matmul(ps[:, 0:512], xnT_lhs_tile_views[dc],
                         wT[:, dc * D + half * 512: dc * D + half * 512 + 512],
                         start=(dc == 0), stop=False)
            mm = t.matmul(ps[:, 0:512], ones1[0:1, 0:128],
                          brow[0:1, half * 512: half * 512 + 512],
                          start=False, stop=True)
        C.inc("TW", mm)
    C.T(_t)

    def _s(s):
        C.waits(s, "TW")
        s.copy(dst_bf16[:, 0:512], ps4[0][:, 0:512])
        C.inc("SW", s.copy(dst_bf16[:, 512:1024], ps4[1][:, 0:512]))
    C.S(_s)


def build_A(C=None):
    own = C is None
    if own:
        C = Ctx("A").start()
    x0 = C.din("x0", [T, D], F32)
    wk = C.din("Wk", [D, D], F32)
    bk = C.din("bk", [1, D], F32)
    wv = C.din("Wv", [D, D], F32)
    bv = C.din("bv", [1, D], F32)

    o_xn = C.dout("xn", [T, D], F32)
    o_max = C.dout("maxx", [1, 1], F32)
    o_S = C.dout("S", [D, HD], F32)

    idb, idf = emit_identity(C)

    xn = C.sb("xn", [128, NT * D], F32)
    stats = C.sb("stats", [128, NT * 2], F32)
    xnT = C.sb("xnT", [128, DC * T], BF16)

    def _y(y):
        for i in range(NT):
            C.inc("LD", y.dma_start(out=xn[:, i * D:(i + 1) * D],
                                    in_=x0[i * 128:(i + 1) * 128, :]), 16)
    C.Y(_y)

    tiles = [xn[:, i * D:(i + 1) * D] for i in range(NT)]
    emit_ln_tiles(C, tiles, tiles, stats)
    gmax = C.sb("gmax", [1, 1], F32)
    emit_absmax_tiles(C, tiles, gmax[0:1, 0:1])

    def _y2(y):
        C.barrier_all(y)
        C.inc("ST", y.dma_start(out=o_max, in_=gmax[0:1, 0:1]), 16)
        for i in range(NT):
            C.inc("ST", y.dma_start(out=o_xn[i * 128:(i + 1) * 128, :],
                                    in_=xn[:, i * D:(i + 1) * D]), 16)
    C.Y(_y2)

    xnb = C.sb("xnb", [128, NT * D], BF16, side="right")

    def _s(s):
        C.barrier_all(s)
        for i in range(NT):
            C.inc("SW", s.copy(xnb[:, i * D:(i + 1) * D],
                               xn[:, i * D:(i + 1) * D]))
    C.S(_s)

    pairs = []
    for i in range(NT):
        for dc in range(DC):
            src = xnb[:, i * D + dc * 128: i * D + (dc + 1) * 128]
            dst = xnT[:, dc * T + i * 128: dc * T + (i + 1) * 128]
            pairs.append((src, dst))
    emit_transposes(C, pairs, idb)
    C.free("xnb")

    wkT, _ = load_weight_T(C, "wk", wk, D, D, idb)
    wvT, _ = load_weight_T(C, "wv", wv, D, D, idb)

    browk = C.sb("brow_k", [1, D], F32)
    browv = C.sb("brow_v", [1, D], F32)
    ones1 = C.sb("ones1", [1, 128], F32)

    def _y3(y):
        C.inc("LD", y.dma_start(out=browk[:], in_=bk), 16)
        C.inc("LD", y.dma_start(out=browv[:], in_=bv), 16)
    C.Y(_y3)

    ktl = C.sb("ktl", [128, D], BF16)
    vtl = C.sb("vtl", [128, D], BF16)
    S = C.sb("Sacc", [128, 8 * 128], F32)
    psA, psB, psC, psD = (C.bufs[f"pf{i}"] for i in range(4))
    pS0, pS1 = C.bufs["pf5"], C.bufs["pf5"]

    def _v0(v):
        v.memset(ones1[:], 1.0)
        C.inc("VW", v.memset(S[:], 0.0))
    C.V(_v0)

    for tt in range(NT):
        xviews = [xnT[:, dc * T + tt * 128: dc * T + (tt + 1) * 128]
                  for dc in range(DC)]
        emit_proj_token_tile(C, [psA, psB], xviews, wkT, browk, ones1, ktl)
        emit_proj_token_tile(C, [psC, psD], xviews, wvT, browv, ones1, vtl)

        for hp in range(8):
            ps = pS0 if hp % 2 == 0 else pS1

            def _t(t, hp=hp, ps=ps):
                C.barrier_all(t)
                C.inc("TW", t.matmul(ps[:, 0:128],
                                     ktl[:, hp * 128:(hp + 1) * 128],
                                     vtl[:, hp * 128:(hp + 1) * 128],
                                     start=True, stop=True))
            C.T(_t)

            def _v(v, hp=hp, ps=ps):
                C.waits(v, "TW")
                sl = S[:, hp * 128:(hp + 1) * 128]
                C.inc("VW", v.tensor_add(sl, sl, ps[:, 0:128]))
            C.V(_v)

    def _y4(y):
        C.barrier_all(y)
        for hp in range(8):
            C.inc("ST", y.dma_start(
                out=o_S[(2 * hp) * 64:(2 * hp + 1) * 64, :],
                in_=S[0:64, hp * 128: hp * 128 + 64]), 16)
            C.inc("ST", y.dma_start(
                out=o_S[(2 * hp + 1) * 64:(2 * hp + 2) * 64, :],
                in_=S[64:128, hp * 128 + 64: hp * 128 + 128]), 16)
        y.wait_ge(C.s["ST"].h, C.s["ST"].n)
    C.Y(_y4)

    if own:
        C.close()
    return C


def emit_scalar_broadcast(C, src11, dst_col, ones_row):
    ps = C.bufs["pf4"]

    def _t(t):
        C.barrier_all(t)
        C.inc("TW", t.matmul(ps[:, 0:1], ones_row[0:1, 0:128], src11,
                             start=True, stop=True))
    C.T(_t)

    def _v(v):
        C.waits(v, "TW")
        C.inc("VW", v.tensor_copy(dst_col, ps[:, 0:1]))
    C.V(_v)


def build_B(C=None):
    own = C is None
    if own:
        C = Ctx("B").start()
    xn_in = C.din("xn", [T, D], F32)
    Sp = C.din("Sprev", [D, HD], F32)
    maxes = C.din("maxes", [1, 8], F32)
    clipx = C.din("clipx", [1, 1], F32)
    idx = C.din("idx", [1, T], F32)
    gam = C.din("gamma", [1, NH], F32)
    wq = C.din("Wq", [D, D], F32)
    bq = C.din("bq", [1, D], F32)
    wk = C.din("Wk", [D, D], F32)
    bk = C.din("bk", [1, D], F32)
    wv = C.din("Wv", [D, D], F32)
    bv = C.din("bv", [1, D], F32)
    wo = C.din("Wo", [D, D], F32)
    bo = C.din("bo", [1, D], F32)
    wgate = C.din("wgate", [2816, D], F32)
    wup = C.din("wup", [2816, D], F32)

    o_a0 = C.dout("a0", [T, D], F32)
    o_xq = C.dout("xq", [T, D], BF16)
    o_hmax = C.dout("hmax", [1, 1], F32)

    idb, idf = emit_identity(C)
    ones1 = C.sb("ones1", [1, 128], F32)
    trilm = C.sb("trilm", [128, 128], BF16)

    def _v0(v):
        C.inc("VW", v.memset(ones1[:], 1.0))
    C.V(_v0)

    def _g0(g):
        C.waits(g, "GW")
        C.inc("GW", g.tensor_scalar(trilm[:], C.bufs["id_cols"][:, :],
                                    C.bufs["id_rowv"][:, 0:1], None, ALU.is_ge))
    C.G(_g0)

    xnT = C.sb("xnT", [128, DC * T], BF16, side="right")
    qT = C.sb("qT", [128, DC * T], BF16, side="right")
    kT = C.sb("kT", [128, DC * T], BF16, side="right")
    vtok = C.sb("vtok", [128, NT * D], BF16, side="right")
    dec = C.sb("dec", [128, NT * NH], F32)
    Sacc = C.sb("Sacc", [128, 8 * 128], F32)
    Sbf = C.sb("Sbf", [128, 8 * 128], BF16)
    lgam = C.sb("lgam", [1, NH], F32)
    idxr = C.sb("idxr", [1, T], F32)
    gamr = C.sb("gamr", [1, NH], F32)

    xtile = C.sb("xtile", [128, D], F32)
    xbtile = C.sb("xbtile", [128, D], BF16)

    def _vS0(v):
        C.inc("VW", v.memset(Sacc[:], 0.0))
    C.V(_vS0)

    def _yl(y):
        C.inc("LD", y.dma_start(out=idxr[:], in_=idx), 16)
        C.inc("LD", y.dma_start(out=gamr[:], in_=gam), 16)
        C.waits(y, "VW")
        for hp in range(8):
            C.inc("LD", y.dma_start(
                out=Sacc[0:64, hp * 128: hp * 128 + 64],
                in_=Sp[(2 * hp) * 64:(2 * hp + 1) * 64, :]), 16)
            C.inc("LD", y.dma_start(
                out=Sacc[64:128, hp * 128 + 64: hp * 128 + 128],
                in_=Sp[(2 * hp + 1) * 64:(2 * hp + 2) * 64, :]), 16)
    C.Y(_yl)

    useScB = C.din("useSc", [1, 1], F32)
    usec = C.sb("usec", [128, 1], F32)
    urow = C.sb("urow", [1, 1], F32)

    def _yu(y):
        C.inc("LD", y.dma_start(out=urow[:], in_=useScB), 16)
    C.Y(_yu)
    emit_scalar_broadcast(C, urow[0:1, 0:1], usec[:, 0:1], ones1)

    def _vS(v):
        C.barrier_all(v)
        C.edge(v, "VW", v.tensor_scalar(Sacc[:], Sacc[:], usec[:, 0:1], None,
                                        ALU.mult))
        C.inc("VW", v.tensor_copy(Sbf[:], Sacc[:]))
    C.V(_vS)

    for tt in range(NT):
        def _y(y, tt=tt):
            C.waits(y, "VW", "ST", "GD")
            C.inc("LD", y.dma_start(out=xtile[:, :],
                                    in_=xn_in[tt * 128:(tt + 1) * 128, :]), 16)
        C.Y(_y)

        def _v(v, tt=tt):
            C.waits(v, "LD", "TW")
            C.inc("VW", v.tensor_copy(xbtile[:, :], xtile[:, :]))
        C.V(_v)

        pairs = []
        for dc in range(DC):
            src = xbtile[:, dc * 128:(dc + 1) * 128]
            dst = xnT[:, dc * T + tt * 128: dc * T + (tt + 1) * 128]
            pairs.append((src, dst))
        emit_transposes(C, pairs, idb)

    def _s0(s):
        C.barrier_load(s)
        C.inc("SW", s.activation(lgam[:], gamr[:], AF.Ln))
    C.S(_s0)

    for tt in range(NT):
        def _t(t, tt=tt):
            C.barrier_all(t)
            C.inc("TW", t.matmul(C.bufs["pf4"][:, 0:NH],
                                 idxr[0:1, tt * 128:(tt + 1) * 128],
                                 lgam[0:1, :], start=True, stop=True))
        C.T(_t)

        def _s(s, tt=tt):
            C.waits(s, "TW")
            C.inc("SW", s.activation(dec[:, tt * NH:(tt + 1) * NH],
                                     C.bufs["pf4"][:, 0:NH], AF.Exp))
        C.S(_s)

    browv = C.sb("brow_v", [1, D], F32)
    browo = C.sb("brow_o", [1, D], F32)

    def _yb(y):
        for brow, src in ((browv, bv), (browo, bo)):
            C.inc("LD", y.dma_start(out=brow[:], in_=src), 16)
    C.Y(_yb)

    bqc, _ = emit_bias_cols(C, bq, D, "bq")
    bkc, _ = emit_bias_cols(C, bk, D, "bk")

    for wname, w_ap, dstT, bcol in (("wq", wq, qT, bqc), ("wk", wk, kT, bkc)):
        wt, _ = load_weight_T(C, wname, w_ap, D, D, idb, side="right")
        pj_ev = [0, 0]
        pj_tw = [0, 0]
        for jc in range(DC):
            for tc in range(4):
                bk = tc % 2
                ps = C.bufs[f"pf{bk}"]

                def _t(t, jc=jc, tc=tc, bk=bk, ps=ps, wt=wt):
                    if jc == 0 and tc == 0:
                        C.barrier_all(t)
                    elif pj_ev[bk]:
                        t.wait_ge(C.s["VW"].h, pj_ev[bk])
                    for dc in range(DC):
                        mm = t.matmul(
                            ps[:, 0:512],
                            wt[:, dc * D + jc * 128:dc * D + (jc + 1) * 128],
                            xnT[:, dc * T + tc * 512:dc * T + (tc + 1) * 512],
                            start=(dc == 0), stop=(dc == DC - 1))
                    C.inc("TW", mm)
                    pj_tw[bk] = C.s["TW"].n
                C.T(_t)

                def _s(s, jc=jc, tc=tc, bk=bk, ps=ps, dstT=dstT, bcol=bcol):
                    s.wait_ge(C.s["TW"].h, pj_tw[bk])
                    C.inc("VW", s.tensor_scalar(
                        dstT[:, jc * T + tc * 512: jc * T + (tc + 1) * 512],
                        ps[:, 0:512], bcol[:, jc:jc + 1], None, ALU.add))
                    pj_ev[bk] = C.s["VW"].n
                C.V(_s)
        C.free(wname + "_T")

    wvT, _ = load_weight_T(C, "wv", wv, D, D, idb, side="right")
    for tt in range(NT):
        xviews = [xnT[:, dc * T + tt * 128: dc * T + (tt + 1) * 128]
                  for dc in range(DC)]
        emit_proj_token_tile(C, [C.bufs["pf0"], C.bufs["pf1"]], xviews, wvT,
                             browv, ones1, vtok[:, tt * D:(tt + 1) * D])
    C.free("wv_T")
    woT, _ = load_weight_T(C, "wo", wo, D, D, idb, side="right")

    atm = C.sb("atm", [128, 2 * 128], BF16)
    ktokt = C.sb("ktokt", [128, D], BF16)
    aotl = C.sb("aotl", [128, D], BF16)
    aoT = C.sb("aoT", [128, DC * 128], BF16)
    a0t = C.sb("a0t", [128, D], F32)

    for tt in range(NT):
        a_tw = [0] * NH
        m_vw = [0] * NH
        t2_tw = [0] * NH
        for h in range(NH):
            hp, r = h // 2, h % 2
            pout = C.bufs["pf0"] if h < 8 else C.bufs["pf1"]
            osl = pout[:, (h % 8) * 64:(h % 8) * 64 + 64]
            rows = slice(r * 64, r * 64 + 64)
            pa = C.bufs["pf4"] if h % 2 == 0 else C.bufs["pf5"]

            def _t(t, tt=tt, h=h, hp=hp, rows=rows, pa=pa):
                if h == 0:
                    C.barrier_all(t)
                elif h >= 2:
                    t.wait_ge(C.s["VW"].h, m_vw[h - 2])
                C.inc("TW", t.matmul(
                    pa[:, 0:128],
                    kT[rows, hp * T + tt * 128: hp * T + (tt + 1) * 128],
                    qT[rows, hp * T + tt * 128: hp * T + (tt + 1) * 128],
                    start=True, stop=True))
                a_tw[h] = C.s["TW"].n
            C.T(_t)

            def _v(v, h=h, pa=pa):
                v.wait_ge(C.s["TW"].h, a_tw[h])
                if h >= 2:
                    v.wait_ge(C.s["TW"].h, t2_tw[h - 2])
                C.inc("VW", v.tensor_tensor(
                    atm[:, (h % 2) * 128:(h % 2) * 128 + 128],
                    pa[:, 0:128], trilm[:], ALU.mult))
                m_vw[h] = C.s["VW"].n
            C.V(_v)

            def _t2(t, tt=tt, h=h, hp=hp, rows=rows, osl=osl):
                t.wait_ge(C.s["VW"].h, m_vw[h])
                t.matmul(osl, qT[rows, hp * T + tt * 128: hp * T + (tt + 1) * 128],
                         Sbf[rows, hp * 128 + r * 64: hp * 128 + r * 64 + 64],
                         start=True, stop=False)
                C.inc("TW", t.matmul(
                    osl, atm[:, (h % 2) * 128:(h % 2) * 128 + 128],
                    vtok[:, tt * D + h * 64: tt * D + (h + 1) * 64],
                    start=False, stop=True))
                t2_tw[h] = C.s["TW"].n
            C.T(_t2)

        kprs = [(kT[:, jc * T + tt * 128: jc * T + (tt + 1) * 128],
                 C.bufs["ktokt"][:, jc * 128:(jc + 1) * 128])
                for jc in range(DC)]
        emit_transposes(C, kprs, idb)

        for hp in range(8):
            def _t3(t, tt=tt, hp=hp):
                C.barrier_all(t)
                C.inc("TW", t.matmul(
                    C.bufs["pf5"][:, 0:128],
                    kT_tok_view(C, tt, hp),
                    vtok[:, tt * D + hp * 128: tt * D + (hp + 1) * 128],
                    start=True, stop=True))
            C.T(_t3)

            def _v3(v, hp=hp):
                C.waits(v, "TW")
                sl = Sacc[:, hp * 128:(hp + 1) * 128]
                C.inc("VW", v.tensor_add(sl, sl, C.bufs["pf5"][:, 0:128]))
            C.V(_v3)

        def _v4(v):
            C.waits(v, "VW")
            C.inc("VW", v.tensor_copy(Sbf[:], Sacc[:]))
        C.V(_v4)

        def _v5(v, tt=tt):
            C.barrier_all(v)
            for h in range(NH):
                pout = C.bufs["pf0"] if h < 8 else C.bufs["pf1"]
                osl = pout[:, (h % 8) * 64:(h % 8) * 64 + 64]
                mm = v.tensor_scalar(aotl[:, h * 64:(h + 1) * 64], osl,
                                     dec[:, tt * NH + h: tt * NH + h + 1],
                                     None, ALU.mult)
            C.inc("VW", mm)
        C.V(_v5)

        pairs = [(aotl[:, dc * 128:(dc + 1) * 128], aoT[:, dc * 128:(dc + 1) * 128])
                 for dc in range(DC)]
        emit_transposes(C, pairs, idb)

        for mh in range(2):
            ps = C.bufs["pf2"] if mh == 0 else C.bufs["pf3"]

            def _t4(t, tt=tt, mh=mh, ps=ps):
                C.barrier_all(t)
                for jc in range(DC):
                    t.matmul(ps[:, 0:512], aoT[:, jc * 128:(jc + 1) * 128],
                             woT[:, jc * D + mh * 512: jc * D + mh * 512 + 512],
                             start=(jc == 0), stop=False)
                C.inc("TW", t.matmul(ps[:, 0:512], ones1[0:1, 0:128],
                                     browo[0:1, mh * 512: mh * 512 + 512],
                                     start=False, stop=True))
            C.T(_t4)

            def _s4(s, mh=mh, ps=ps):
                C.waits(s, "TW", "ST")
                C.inc("SW", s.copy(a0t[:, mh * 512:mh * 512 + 512], ps[:, 0:512]))
            C.S(_s4)

        def _y5(y, tt=tt):
            C.waits(y, "SW")
            C.inc("ST", y.dma_start(out=o_a0[tt * 128:(tt + 1) * 128, :],
                                    in_=a0t[:, :]), 16)
        C.Y(_y5)

    C.free("wo_T")
    C.free("vtok", "kT", "qT", "xnT")
    maxrow = C.sb("maxrow", [1, 8], F32)
    g11 = C.sb("g11", [1, 4], F32)
    s1bc = C.sb("s1bc", [128, 1], F32)
    r1bc = C.sb("r1bc", [128, 1], F32)
    cxr = C.sb("cxr", [1, 1], F32)

    def _ym(y):
        C.inc("LD", y.dma_start(out=maxrow[:], in_=maxes), 16)
        C.inc("LD", y.dma_start(out=cxr[:], in_=clipx), 16)
    C.Y(_ym)

    def _sm(s):
        C.barrier_load(s)
        C.inc("SW", s.activation(g11[0:1, 3:4], cxr[0:1, 0:1], AF.Sigmoid))
    C.S(_sm)

    def _vm(v):
        C.barrier_all(v)
        C.edge(v, "VW", v.reduce_max(g11[0:1, 0:1], maxrow[0:1, :], AX.X))
        C.edge(v, "VW", v.tensor_tensor(g11[0:1, 1:2], g11[0:1, 0:1],
                                        g11[0:1, 3:4], ALU.mult))
        C.edge(v, "VW", v.tensor_scalar(g11[0:1, 1:2], g11[0:1, 1:2],
                                        1.0 / QMAX, None, ALU.mult))
        C.inc("VW", v.reciprocal(g11[0:1, 2:3], g11[0:1, 1:2]))
    C.V(_vm)

    emit_scalar_broadcast(C, g11[0:1, 1:2], s1bc[:, 0:1], ones1)
    emit_scalar_broadcast(C, g11[0:1, 2:3], r1bc[:, 0:1], ones1)

    xqT = C.sb("xqT", [128, DC * T], BF16)
    xqt = C.sb("xqt", [128, D], F32)
    xqb = C.sb("xqb", [128, D], BF16)

    for tt in range(NT):
        def _y(y, tt=tt):
            C.barrier_all(y)
            C.inc("LD", y.dma_start(out=xtile[:, 0:D],
                                    in_=xn_in[tt * 128:(tt + 1) * 128, :]), 16)
        C.Y(_y)

        def _v(v, tt=tt):
            C.waits(v, "LD", "ST")
            C.edge(v, "VW", v.tensor_scalar(xqt[:, :], xtile[:, 0:D],
                                            r1bc[:, 0:1], None, ALU.mult))
            C.edge(v, "VW", v.tensor_scalar(xqt[:, :], xqt[:, :], MAGIC,
                                            -MAGIC, ALU.add, ALU.add))
            C.edge(v, "VW", v.tensor_scalar(xqt[:, :], xqt[:, :], QMAX,
                                            -QMAX - 1.0, ALU.min, ALU.max))
            C.inc("VW", v.tensor_scalar(xqb[:, :], xqt[:, :], s1bc[:, 0:1],
                                        None, ALU.mult))
        C.V(_v)

        def _y2(y, tt=tt):
            C.waits(y, "VW")
            C.inc("ST", y.dma_start(out=o_xq[tt * 128:(tt + 1) * 128, :],
                                    in_=xqb[:, :]), 16)
        C.Y(_y2)

        pairs = []
        for dc in range(DC):
            src = xqb[:, dc * 128:(dc + 1) * 128]
            dst = xqT[:, dc * T + tt * 128: dc * T + (tt + 1) * 128]
            pairs.append((src, dst))
        emit_transposes(C, pairs, idb)

    NI = 2816
    NIC = NI // 128
    wgqT = C.sb("wgqT", [128, DC * NI], BF16)
    wuqT = C.sb("wuqT", [128, DC * NI], BF16)
    wchunk = C.sb("wchunk", [128, D], F32, side="right")
    wqb = C.sb("wqb", [128, D], BF16, side="right")
    wrm = C.sb("wrm", [128, 3], F32, side="right")

    for wap, dstT in ((wgate, wgqT), (wup, wuqT)):
        for ic in range(NIC):
            def _y(y, wap=wap, ic=ic):
                C.barrier_all(y)
                C.inc("LD", y.dma_start(out=wchunk[:, :],
                                        in_=wap[ic * 128:(ic + 1) * 128, :]), 16)
            C.Y(_y)

            def _v(v, ic=ic):
                C.waits(v, "LD")
                C.edge(v, "VW", v.reduce_max(wrm[:, 0:1], wchunk[:, :], AX.X,
                                             apply_absolute_value=True))
                C.edge(v, "VW", v.tensor_scalar(wrm[:, 1:2], wrm[:, 0:1],
                                                1.0 / QMAX, None, ALU.mult))
                C.edge(v, "VW", v.reciprocal(wrm[:, 2:3], wrm[:, 1:2]))
                C.edge(v, "VW", v.tensor_scalar(wchunk[:, :], wchunk[:, :],
                                                wrm[:, 2:3], None, ALU.mult))
                C.edge(v, "VW", v.tensor_scalar(wchunk[:, :], wchunk[:, :],
                                                MAGIC, -MAGIC, ALU.add, ALU.add))
                C.inc("VW", v.tensor_scalar(wqb[:, :], wchunk[:, :],
                                            wrm[:, 1:2], None, ALU.mult))
            C.V(_v)

            pairs = []
            for hc in range(DC):
                src = wqb[:, hc * 128:(hc + 1) * 128]
                dst = dstT[:, hc * NI + ic * 128: hc * NI + (ic + 1) * 128]
                pairs.append((src, dst))
            emit_transposes(C, pairs, idb)

    gch = C.sb("gch", [128, T], F32)
    uch = C.sb("uch", [128, T], F32)
    hmx = C.sb("hmx", [128, NIC], F32)

    evac_sw2 = [0, 0]
    g_tw2 = [0, 0]
    chain_vw2 = [0]
    for ic in range(NIC):
        for tc in range(4):
            for wi, (wT, dst) in enumerate(((wgqT, gch), (wuqT, uch))):
                ps = C.bufs[f"pf{wi}"]

                def _t(t, ic=ic, tc=tc, wi=wi, wT=wT, ps=ps):
                    if ic == 0 and tc == 0 and wi == 0:
                        C.barrier_all(t)
                    elif evac_sw2[wi]:
                        t.wait_ge(C.s["SW"].h, evac_sw2[wi])
                    for hc in range(DC):
                        mm = t.matmul(
                            ps[:, 0:512],
                            wT[:, hc * NI + ic * 128: hc * NI + (ic + 1) * 128],
                            xqT[:, hc * T + tc * 512: hc * T + (tc + 1) * 512],
                            start=(hc == 0), stop=(hc == DC - 1))
                    C.inc("TW", mm)
                    g_tw2[wi] = C.s["TW"].n
                C.T(_t)

                def _s(s, tc=tc, wi=wi, ps=ps, dst=dst):
                    s.wait_ge(C.s["TW"].h, g_tw2[wi])
                    if chain_vw2[0]:
                        s.wait_ge(C.s["VW"].h, chain_vw2[0])
                    C.inc("SW", s.tensor_copy(dst[:, tc * 512:(tc + 1) * 512],
                                              ps[:, 0:512]))
                    evac_sw2[wi] = C.s["SW"].n
                C.V(_s)

        def _s2(s, ic=ic):
            C.waits(s, "SW")
            C.inc("SW", s.activation(gch[:, :], gch[:, :], AF.Silu))
        C.S(_s2)

        def _v2(v, ic=ic):
            C.waits(v, "SW")
            C.edge(v, "VW", v.tensor_mul(gch[:, :], gch[:, :], uch[:, :]))
            C.inc("VW", v.reduce_max(hmx[:, ic:ic + 1], gch[:, :], AX.X,
                                     apply_absolute_value=True))
            chain_vw2[0] = C.s["VW"].n
        C.V(_v2)

    emit_absmax_tiles(C, [hmx[:, 0:NIC]], C.sb("hgm", [1, 1], F32)[0:1, 0:1])

    def _yh(y):
        C.barrier_all(y)
        C.inc("ST", y.dma_start(out=o_hmax, in_=C.bufs["hgm"][0:1, 0:1]), 16)
        y.wait_ge(C.s["ST"].h, C.s["ST"].n)
    C.Y(_yh)

    if own:
        C.close()
    return C


def kT_tok_view(C, tt, hp):
    return C.bufs["ktokt"][:, hp * 128:(hp + 1) * 128]


def shard_x(x):
    out = []
    for b in range(4):
        for h in range(2):
            out.append(np.ascontiguousarray(x[b, h * T:(h + 1) * T, :]))
    return out


def run_A(inputs):
    C = build_A()
    xs = shard_x(inputs["L0_attn_x"])
    im = []
    for c in range(8):
        im.append({
            "i_x0": xs[c],
            "i_Wk": inputs["L0_attn_Wk"],
            "i_bk": inputs["L0_attn_bk"].reshape(1, D),
            "i_Wv": inputs["L0_attn_Wv"],
            "i_bv": inputs["L0_attn_bv"].reshape(1, D),
        })
    res = run_bass_kernel_spmd(C.nc, im, core_ids=list(range(8)))
    return res.results


def build_C(C=None):
    own = C is None
    if own:
        C = Ctx("C").start()
    NI = 2816
    NIC = NI // 128
    x0 = C.din("x0", [T, D], F32)
    a0 = C.din("a0", [T, D], F32)
    xq_in = C.din("xq", [T, D], BF16)
    hmaxes = C.din("hmaxes", [1, 8], F32)
    clipd = C.din("clipd", [1, 1], F32)
    wgate = C.din("wgate", [NI, D], F32)
    wup = C.din("wup", [NI, D], F32)
    wdown = C.din("wdown", [D, NI], F32)
    wk1 = C.din("Wk1", [D, D], F32)
    bk1 = C.din("bk1", [1, D], F32)

    o_x1 = C.dout("x1", [T, D], F32)
    o_xn1 = C.dout("xn1", [T, D], F32)
    o_k1 = C.dout("k1", [T, D], BF16)

    wdqT_dram = C.nc.dram_tensor("wdqT_dram", [NI, D], BF16).ap()

    idb, idf = emit_identity(C)
    ones1 = C.sb("ones1", [1, 128], F32)
    x0t = C.sb("x0t", [128, 512], F32)
    a0t = C.sb("a0t", [128, 512], F32)
    x1t = C.sb("x1t", [128, D], F32)

    def _v0(v):
        C.inc("VW", v.memset(ones1[:], 1.0))
    C.V(_v0)

    wchunk = C.sb("wchunk", [128, max(D, NI)], F32, side="right")
    wqb = C.sb("wqb", [128, max(D, NI)], BF16, side="right")
    wrm = C.sb("wrm", [128, 3], F32, side="right")
    wgqT = C.sb("wgqT", [128, DC * NI], BF16)
    wuqT = C.sb("wuqT", [128, DC * NI], BF16)

    def emit_wquant(wap, width, dst_pairs_fn, nrows):
        for rc in range(nrows // 128):
            def _y(y, rc=rc):
                C.barrier_all(y)
                C.inc("LD", y.dma_start(out=wchunk[:, 0:width],
                                        in_=wap[rc * 128:(rc + 1) * 128, :]), 16)
            C.Y(_y)

            def _v(v, rc=rc):
                C.waits(v, "LD")
                C.edge(v, "VW", v.reduce_max(wrm[:, 0:1], wchunk[:, 0:width],
                                             AX.X, apply_absolute_value=True))
                C.edge(v, "VW", v.tensor_scalar(wrm[:, 1:2], wrm[:, 0:1],
                                                1.0 / QMAX, None, ALU.mult))
                C.edge(v, "VW", v.reciprocal(wrm[:, 2:3], wrm[:, 1:2]))
                C.edge(v, "VW", v.tensor_scalar(wchunk[:, 0:width],
                                                wchunk[:, 0:width],
                                                wrm[:, 2:3], None, ALU.mult))
                C.edge(v, "VW", v.tensor_scalar(wchunk[:, 0:width],
                                                wchunk[:, 0:width],
                                                MAGIC, -MAGIC, ALU.add, ALU.add))
                C.inc("VW", v.tensor_scalar(wqb[:, 0:width], wchunk[:, 0:width],
                                            wrm[:, 1:2], None, ALU.mult))
            C.V(_v)
            emit_transposes(C, dst_pairs_fn(rc), idb)

    for wap, dstT in ((wgate, wgqT), (wup, wuqT)):
        def mkpairs(rc, dstT=dstT):
            return [(wqb[:, hc * 128:(hc + 1) * 128],
                     dstT[:, hc * NI + rc * 128: hc * NI + (rc + 1) * 128])
                    for hc in range(DC)]
        emit_wquant(wap, D, mkpairs, NI)

    wdT = C.sb("wdT", [128, D], BF16, side="right")

    for rc in range(DC):
        def _y(y, rc=rc):
            C.barrier_all(y)
            C.inc("LD", y.dma_start(out=wchunk[:, 0:NI],
                                    in_=wdown[rc * 128:(rc + 1) * 128, :]), 16)
        C.Y(_y)

        def _v(v, rc=rc):
            C.waits(v, "LD")
            C.edge(v, "VW", v.reduce_max(wrm[:, 0:1], wchunk[:, 0:NI],
                                         AX.X, apply_absolute_value=True))
            C.edge(v, "VW", v.tensor_scalar(wrm[:, 1:2], wrm[:, 0:1],
                                            1.0 / QMAX, None, ALU.mult))
            C.edge(v, "VW", v.reciprocal(wrm[:, 2:3], wrm[:, 1:2]))
            C.edge(v, "VW", v.tensor_scalar(wchunk[:, 0:NI], wchunk[:, 0:NI],
                                            wrm[:, 2:3], None, ALU.mult))
            C.edge(v, "VW", v.tensor_scalar(wchunk[:, 0:NI], wchunk[:, 0:NI],
                                            MAGIC, -MAGIC, ALU.add, ALU.add))
            C.inc("VW", v.tensor_scalar(wqb[:, 0:NI], wchunk[:, 0:NI],
                                        wrm[:, 1:2], None, ALU.mult))
        C.V(_v)

        for ic in range(NIC):
            emit_transposes(C, [(wqb[:, ic * 128:(ic + 1) * 128],
                                 wdT[:, rc * 128:(rc + 1) * 128])], idb)

            def _y2(y, ic=ic, rc=rc):
                C.waits(y, "VW")
                C.inc("ST", y.dma_start(
                    out=wdqT_dram[ic * 128:(ic + 1) * 128,
                                  rc * 128:(rc + 1) * 128],
                    in_=wdT[:, rc * 128:(rc + 1) * 128]), 16)
            C.Y(_y2)

    C.free("wdT", "wrm", "wqb", "wchunk")

    xqT = C.sb("xqT", [128, DC * T], BF16)
    xqb = C.sb("xqb", [128, D], BF16)

    for tt in range(NT):
        def _y(y, tt=tt):
            C.barrier_all(y)
            C.inc("LD", y.dma_start(out=xqb[:, :],
                                    in_=xq_in[tt * 128:(tt + 1) * 128, :]), 16)
        C.Y(_y)
        pairs = [(xqb[:, dc * 128:(dc + 1) * 128],
                  xqT[:, dc * T + tt * 128: dc * T + (tt + 1) * 128])
                 for dc in range(DC)]
        emit_transposes(C, pairs, idb)

    maxrow = C.sb("maxrow", [1, 8], F32)
    g11 = C.sb("g11", [1, 4], F32)
    s2bc = C.sb("s2bc", [128, 1], F32)
    r2bc = C.sb("r2bc", [128, 1], F32)
    cdr = C.sb("cdr", [1, 1], F32)

    def _ym(y):
        C.inc("LD", y.dma_start(out=maxrow[:], in_=hmaxes), 16)
        C.inc("LD", y.dma_start(out=cdr[:], in_=clipd), 16)
    C.Y(_ym)

    def _sm(s):
        C.barrier_load(s)
        C.inc("SW", s.activation(g11[0:1, 3:4], cdr[0:1, 0:1], AF.Sigmoid))
    C.S(_sm)

    def _vm(v):
        C.barrier_all(v)
        C.edge(v, "VW", v.reduce_max(g11[0:1, 0:1], maxrow[0:1, :], AX.X))
        C.edge(v, "VW", v.tensor_tensor(g11[0:1, 1:2], g11[0:1, 0:1],
                                        g11[0:1, 3:4], ALU.mult))
        C.edge(v, "VW", v.tensor_scalar(g11[0:1, 1:2], g11[0:1, 1:2],
                                        1.0 / QMAX, None, ALU.mult))
        C.inc("VW", v.reciprocal(g11[0:1, 2:3], g11[0:1, 1:2]))
    C.V(_vm)

    emit_scalar_broadcast(C, g11[0:1, 1:2], s2bc[:, 0:1], ones1)
    emit_scalar_broadcast(C, g11[0:1, 2:3], r2bc[:, 0:1], ones1)

    hq = C.sb("hq", [128, NIC * 512], BF16)
    gchk = C.sb("gchk", [128, 512], F32)
    uchk = C.sb("uchk", [128, 512], F32)
    wdh = C.sb("wdh", [128, NIC * 512], BF16)

    for tc in range(4):
        evac_sw = [0, 0]
        chain_vw = [0]
        g_tw = [0, 0]

        for ic in range(NIC):
            for wi, (wT, dst) in enumerate(((wgqT, gchk), (wuqT, uchk))):
                ps = C.bufs[f"pf{wi}"]

                def _t(t, tc=tc, ic=ic, wi=wi, wT=wT, ps=ps):
                    if tc == 0 and ic == 0 and wi == 0:
                        C.barrier_all(t)
                    elif evac_sw[wi]:
                        t.wait_ge(C.s["SW"].h, evac_sw[wi])
                    for hc in range(DC):
                        mm = t.matmul(
                            ps[:, 0:512],
                            wT[:, hc * NI + ic * 128: hc * NI + (ic + 1) * 128],
                            xqT[:, hc * T + tc * 512: hc * T + (tc + 1) * 512],
                            start=(hc == 0), stop=(hc == DC - 1))
                    C.inc("TW", mm)
                    g_tw[wi] = C.s["TW"].n
                C.T(_t)

                def _s(s, wi=wi, ps=ps, dst=dst):
                    s.wait_ge(C.s["TW"].h, g_tw[wi])
                    if chain_vw[0]:
                        s.wait_ge(C.s["VW"].h, chain_vw[0])
                    C.inc("SW", s.tensor_copy(dst[:, :], ps[:, 0:512]))
                    evac_sw[wi] = C.s["SW"].n
                C.V(_s)

            def _s2(s):
                C.waits(s, "SW")
                C.inc("SW", s.activation(gchk[:, :], gchk[:, :], AF.Silu))
            C.S(_s2)

            def _v2(v, ic=ic):
                C.waits(v, "SW", "TW")
                C.edge(v, "VW", v.tensor_mul(gchk[:, :], gchk[:, :], uchk[:, :]))
                C.edge(v, "VW", v.tensor_scalar(gchk[:, :], gchk[:, :],
                                                r2bc[:, 0:1], None, ALU.mult))
                C.edge(v, "VW", v.tensor_scalar(gchk[:, :], gchk[:, :],
                                                MAGIC, -MAGIC, ALU.add, ALU.add))
                C.edge(v, "VW", v.tensor_scalar(gchk[:, :], gchk[:, :],
                                                QMAX, -QMAX - 1.0, ALU.min,
                                                ALU.max))
                C.inc("VW", v.tensor_scalar(hq[:, ic * 512:(ic + 1) * 512],
                                            gchk[:, :], s2bc[:, 0:1], None,
                                            ALU.mult))
                chain_vw[0] = C.s["VW"].n
            C.V(_v2)

        for mh in range(2):
            def _yw(y, mh=mh):
                C.barrier_all(y)
                for ic in range(NIC):
                    C.inc("LD", y.dma_start(
                        out=wdh[:, ic * 512:(ic + 1) * 512],
                        in_=wdqT_dram[ic * 128:(ic + 1) * 128,
                                      mh * 512: mh * 512 + 512]), 16)
            C.Y(_yw)

            for lt in range(4):
                tt = tc * 4 + lt
                ps = C.bufs[f"pf{2 + lt % 2}"]

                def _yx(y, tt=tt, mh=mh):
                    C.barrier_all(y)
                    C.inc("LD", y.dma_start(
                        out=x0t[:, :],
                        in_=x0[tt * 128:(tt + 1) * 128, mh * 512:mh * 512 + 512]), 16)
                    C.inc("LD", y.dma_start(
                        out=a0t[:, :],
                        in_=a0[tt * 128:(tt + 1) * 128, mh * 512:mh * 512 + 512]), 16)
                C.Y(_yx)

                def _t2(t, lt=lt, ps=ps):
                    C.barrier_all(t)
                    for ic in range(NIC):
                        mm = t.matmul(
                            ps[:, 0:512],
                            hq[:, ic * 512 + lt * 128: ic * 512 + (lt + 1) * 128],
                            wdh[:, ic * 512:(ic + 1) * 512],
                            start=(ic == 0), stop=(ic == NIC - 1))
                    C.inc("TW", mm)
                C.T(_t2)

                def _v3(v, ps=ps):
                    C.waits(v, "TW", "LD", "ST")
                    C.edge(v, "VW", v.tensor_add(x1t[:, 0:512], x0t[:, :],
                                                 ps[:, 0:512]))
                    C.inc("VW", v.tensor_add(x1t[:, 0:512], x1t[:, 0:512],
                                             a0t[:, :]))
                C.V(_v3)

                def _yo(y, tt=tt, mh=mh):
                    C.waits(y, "VW")
                    C.inc("ST", y.dma_start(
                        out=o_x1[tt * 128:(tt + 1) * 128, mh * 512:mh * 512 + 512],
                        in_=x1t[:, 0:512]), 16)
                C.Y(_yo)

    C.free("wdh", "uchk", "gchk", "hq", "cdr", "r2bc", "s2bc", "g11",
           "maxrow", "xqb", "xqT", "wuqT", "wgqT")
    xn1T = C.sb("xn1T", [128, DC * T], BF16)
    stats = C.sb("stats", [128, NT * 2], F32)
    xbt = C.sb("xbt", [128, D], BF16)
    k1t = C.sb("k1t", [128, D], BF16)
    browk1 = C.sb("brow_k1", [1, D], F32)

    def _yk(y):
        C.inc("LD", y.dma_start(out=browk1[:], in_=bk1), 16)
    C.Y(_yk)

    wk1n = C.sb("wk1n", [128, DC * D], BF16)

    def _gk(g):
        C.barrier_all(g)
        for dc in range(DC):
            C.inc("GD", g.dma_start(out=wk1n[:, dc * D:(dc + 1) * D],
                                    in_=wk1[dc * 128:(dc + 1) * 128, :]), 16)
    C.G(_gk)

    for tt in range(NT):
        def _yx(y, tt=tt):
            C.barrier_all(y)
            C.inc("LD", y.dma_start(out=x1t[:, :],
                                    in_=o_x1[tt * 128:(tt + 1) * 128, :]), 16)
        C.Y(_yx)
        emit_ln_tiles(C, [x1t[:, :]], [x1t[:, :]], stats[:, 2 * tt:2 * tt + 2])

        def _yo(y, tt=tt):
            C.waits(y, "VW")
            C.inc("ST", y.dma_start(out=o_xn1[tt * 128:(tt + 1) * 128, :],
                                    in_=x1t[:, :]), 16)
        C.Y(_yo)

        def _vb(v, tt=tt):
            C.waits(v, "VW", "TW", "ST")
            C.inc("VW", v.tensor_copy(xbt[:, :], x1t[:, :]))
        C.V(_vb)

        pairs = [(xbt[:, dc * 128:(dc + 1) * 128],
                  xn1T[:, dc * T + tt * 128: dc * T + (tt + 1) * 128])
                 for dc in range(DC)]
        emit_transposes(C, pairs, idb)

    for tt in range(NT):
        for mh in range(2):
            ps = C.bufs[f"pf{mh}"]

            def _t(t, tt=tt, mh=mh, ps=ps):
                C.barrier_all(t)
                for dc in range(DC):
                    t.matmul(ps[:, 0:512],
                             xn1T[:, dc * T + tt * 128: dc * T + (tt + 1) * 128],
                             wk1n[:, dc * D + mh * 512: dc * D + mh * 512 + 512],
                             start=(dc == 0), stop=False)
                C.inc("TW", t.matmul(ps[:, 0:512], ones1[0:1, 0:128],
                                     browk1[0:1, mh * 512: mh * 512 + 512],
                                     start=False, stop=True))
            C.T(_t)

            def _s(s, mh=mh, ps=ps):
                C.waits(s, "TW", "ST")
                C.inc("SW", s.copy(k1t[:, mh * 512:mh * 512 + 512], ps[:, 0:512]))
            C.S(_s)

        def _yk2(y, tt=tt):
            C.waits(y, "SW")
            C.inc("ST", y.dma_start(out=o_k1[tt * 128:(tt + 1) * 128, :],
                                    in_=k1t[:, :]), 16)
        C.Y(_yk2)

    if own:
        def _yfin(y):
            y.wait_ge(C.s["ST"].h, C.s["ST"].n)
        C.Y(_yfin)

    if own:
        C.close()
    return C


def build_D(C=None):
    own = C is None
    if own:
        C = Ctx("D").start()
    KTOT = 2 * T
    NKT = KTOT // 128
    xn1 = C.din("xn1", [T, D], F32)
    k_own = C.din("k_own", [T, D], BF16)
    k_part = C.din("k_part", [T, D], BF16)
    first = C.din("first", [1, 1], F32)
    wq1 = C.din("Wq1", [D, D], F32)
    bq1 = C.din("bq1", [1, D], F32)

    o_a1 = C.dout("a1", [T, D], F32)
    o_dbg = C.dout("dbg", [128, 1024 + 512], F32) if DBG_D else None

    idb, idf = emit_identity(C)
    ones1 = C.sb("ones1", [1, 128], F32)
    onecol = C.sb("onecol", [128, 1], BF16)

    def _v0(v):
        v.memset(ones1[:], 1.0)
        C.inc("VW", v.memset(onecol[:], 1.0))
    C.V(_v0)

    qT = C.sb("qT", [128, DC * T], BF16)
    xbt = C.sb("xbt", [128, D], BF16)
    xtile = C.sb("xtile", [128, D], F32)

    xn1T = C.sb("xn1T", [128, DC * T], BF16, side="right")
    wq1n = C.sb("wq1n", [128, DC * D], BF16, side="right")
    bqc, _ = emit_bias_cols(C, bq1, D, "bq1")

    for tt in range(NT):
        def _y(y, tt=tt):
            C.waits(y, "VW", "ST", "GD")
            C.inc("LD", y.dma_start(out=xtile[:, :],
                                    in_=xn1[tt * 128:(tt + 1) * 128, :]), 16)
        C.Y(_y)

        def _v(v, tt=tt):
            C.waits(v, "LD", "TW")
            C.inc("VW", v.tensor_copy(xbt[:, :], xtile[:, :]))
        C.V(_v)

        pairs = [(xbt[:, dc * 128:(dc + 1) * 128],
                  xn1T[:, dc * T + tt * 128: dc * T + (tt + 1) * 128])
                 for dc in range(DC)]
        emit_transposes(C, pairs, idb)

    def _g(g):
        C.barrier_all(g)
        for dc in range(DC):
            C.inc("GD", g.dma_start(out=wq1n[:, dc * D:(dc + 1) * D],
                                    in_=wq1[dc * 128:(dc + 1) * 128, :]), 16)
    C.G(_g)

    for jc in range(DC):
        for tc in range(4):
            ps = C.bufs[f"pf{tc % 2}"]

            def _t(t, jc=jc, tc=tc, ps=ps):
                C.barrier_all(t)
                for dc in range(DC):
                    mm = t.matmul(
                        ps[:, 0:512],
                        wq1n[:, dc * D + jc * 128: dc * D + (jc + 1) * 128],
                        xn1T[:, dc * T + tc * 512: dc * T + (tc + 1) * 512],
                        start=(dc == 0), stop=(dc == DC - 1))
                C.inc("TW", mm)
            C.T(_t)

            def _s(s, jc=jc, tc=tc, ps=ps):
                C.waits(s, "TW")
                C.inc("VW", s.tensor_scalar(
                    qT[:, jc * T + tc * 512: jc * T + (tc + 1) * 512],
                    ps[:, 0:512], bqc[:, jc:jc + 1], None, ALU.add))
            C.V(_s)
    C.free("wq1n", "xn1T")

    ktokf = C.sb("ktokf", [128, NKT * D], BF16)
    kTf = C.sb("kTf", [128, DC * KTOT], BF16)

    def _yk(y):
        C.barrier_all(y)
        for tt in range(NT):
            C.inc("LD", y.dma_start(
                out=ktokf[:, tt * D:(tt + 1) * D],
                in_=k_own[tt * 128:(tt + 1) * 128, :]), 16)
            C.inc("LD", y.dma_start(
                out=ktokf[:, (NT + tt) * D:(NT + tt + 1) * D],
                in_=k_part[tt * 128:(tt + 1) * 128, :]), 16)
    C.Y(_yk)

    for kt in range(NKT):
        pairs = [(ktokf[:, kt * D + dc * 128: kt * D + (dc + 1) * 128],
                  kTf[:, dc * KTOT + kt * 128: dc * KTOT + (kt + 1) * 128])
                 for dc in range(DC)]
        emit_transposes(C, pairs, idb)

    if DBG_D:
        qdump = C.sb("qdump", [128, 1024], F32)

        def _vq(v):
            C.barrier_all(v)
            C.inc("VW", v.tensor_copy(qdump[:, :], qT[:, 0:1024]))
        C.V(_vq)

        def _yq(y):
            C.waits(y, "VW")
            C.inc("ST", y.dma_start(out=o_dbg[:, 0:1024], in_=qdump[:, :]), 16)
        C.Y(_yq)

    expt = C.sb("expt", [128, 4 * 512], BF16)
    obuf = C.sb("obuf", [128, 512], F32)
    rcol = C.sb("rcol", [128, 1], F32)
    a1buf = C.sb("a1buf", [128, 4 * D], F32)
    SCALE = 1.0 / math.sqrt(1024.0)

    pfA = C.bufs["pfA"]
    for qc in range(4):
        for h in range(NH):
            hp, r = h // 2, h % 2
            rows = slice(r * 64, r * 64 + 64)
            pout = C.bufs["pf4"]
            NG2 = NKT // 2
            sim_tw = [0] * NG2
            v_tw = [0] * NKT
            exp_sw = [0] * NG2

            def _t_sim(t, g):
                gb = g % 2
                if g == 0:
                    C.barrier_all(t)
                if g >= 2:
                    t.wait_ge(C.s["SW"].h, exp_sw[g - 2])
                for kk in range(2):
                    kt = 2 * g + kk
                    C.inc("TW", t.matmul(
                        pfA[:, gb * 1024 + kk * 512: gb * 1024 + kk * 512 + 512],
                        kTf[rows, hp * KTOT + kt * 128: hp * KTOT + (kt + 1) * 128],
                        qT[rows, hp * T + qc * 512: hp * T + (qc + 1) * 512],
                        start=True, stop=True))
                sim_tw[g] = C.s["TW"].n

            def _s_exp(s, g):
                gb = g % 2
                s.wait_ge(C.s["TW"].h, sim_tw[g])
                if g >= 2:
                    s.wait_ge(C.s["TW"].h, v_tw[2 * (g - 2) + 1])
                C.inc("SW", s.activation(
                    expt[:, gb * 1024: gb * 1024 + 1024],
                    pfA[:, gb * 1024: gb * 1024 + 1024], AF.Exp, scale=SCALE))
                exp_sw[g] = C.s["SW"].n

            def _t_v(t, kt):
                g = kt // 2
                t.wait_ge(C.s["SW"].h, exp_sw[g])
                esl = expt[:, (kt % 4) * 512:(kt % 4) * 512 + 512]
                t.matmul(pout[0:64, 0:512],
                         ktokf[:, kt * D + h * 64: kt * D + (h + 1) * 64],
                         esl, start=(kt == 0), stop=(kt == NKT - 1))
                C.inc("TW", t.matmul(C.bufs["pf5"][0:1, 0:512],
                                     onecol[:, 0:1], esl,
                                     start=(kt == 0), stop=(kt == NKT - 1)))
                v_tw[kt] = C.s["TW"].n

            for g in range(NG2 + 1):
                if g < NG2:
                    C.T(lambda t, g=g: _t_sim(t, g))
                    C.S(lambda s, g=g: _s_exp(s, g))
                if g >= 1:
                    C.T(lambda t, g=g: _t_v(t, 2 * (g - 1)))
                    C.T(lambda t, g=g: _t_v(t, 2 * (g - 1) + 1))

            def _s2(s):
                C.waits(s, "TW", "ST")
                s.copy(obuf[0:64, 0:512], pout[0:64, 0:512])
                C.inc("SW", s.copy(obuf[64:65, 0:512],
                                   C.bufs["pf5"][0:1, 0:512]))
            C.S(_s2)

            for ci in range(4):
                def _t3(t, ci=ci):
                    C.waits(t, "SW", "VW")
                    C.inc("TW", t.transpose(
                        C.bufs["pf5"][0:128, 0:65],
                        obuf[0:65, ci * 128:(ci + 1) * 128], idf[0:65, 0:65]))
                C.T(_t3)

                def _v3(v, ci=ci, h=h):
                    C.waits(v, "TW")
                    C.edge(v, "VW", v.reciprocal(rcol[:, 0:1],
                                                 C.bufs["pf5"][:, 64:65]))
                    C.inc("VW", v.tensor_scalar(
                        a1buf[:, ci * D + h * 64: ci * D + (h + 1) * 64],
                        C.bufs["pf5"][:, 0:64], rcol[:, 0:1], None, ALU.mult))
                C.V(_v3)

        def _yo(y, qc=qc):
            C.waits(y, "VW")
            for ci in range(4):
                C.inc("ST", y.dma_start(
                    out=o_a1[(qc * 4 + ci) * 128:(qc * 4 + ci + 1) * 128, :],
                    in_=a1buf[:, ci * D:(ci + 1) * D]), 16)
        C.Y(_yo)

    if own:
        def _yfin(y):
            y.wait_ge(C.s["ST"].h, C.s["ST"].n)
        C.Y(_yfin)

    if own:
        C.close()
    return C


def run_D(inputs, resC):
    C = build_D()
    im = []
    for c in range(8):
        b, h = c // 2, c % 2
        part = 2 * b + (1 - h)
        im.append({
            "i_xn1": resC[c]["o_xn1"],
            "i_k_own": resC[c]["o_k1"],
            "i_k_part": resC[part]["o_k1"],
            "i_first": np.array([[1.0 - h]], np.float32),
            "i_Wq1": inputs["L1_attn_Wq"],
            "i_bq1": inputs["L1_attn_bq"].reshape(1, D),
        })
    res = run_bass_kernel_spmd(C.nc, im, core_ids=list(range(8)))
    return res.results


WIN = 512
IHEADS = 8
INNER = 512
NG = 20


def build_E1(C=None):
    own = C is None
    if own:
        C = Ctx("E1").start()
    z_in = C.din("z", [T, D], F32)
    ztail = C.din("ztail", [640, D], F32)
    wqkv = C.din("wqkv", [D, 3 * INNER], F32)
    qsr = C.din("qs", [1, HD], F32)
    ksr = C.din("ks", [1, HD], F32)
    wo_in = C.din("wo", [INNER, D], F32)
    tR1 = C.din("tR1", [T, INNER], F32)
    tR2 = C.din("tR2", [T, INNER], F32)
    tR3 = C.din("tR3", [T, INNER], F32)
    tR4 = C.din("tR4", [T, INNER], F32)
    tR3t = C.din("tR3t", [WIN, INNER], F32)
    tR4t = C.din("tR4t", [WIN, INNER], F32)
    mask0 = C.din("mask0", [1024, WIN], BF16)
    maskr = C.din("maskr", [1024, WIN], BF16)

    o_zp = C.dout("zp", [T, D], F32)

    idb, idf = emit_identity(C)
    ones1 = C.sb("ones1", [1, 128], F32)
    onecol = C.sb("onecol", [128, 1], BF16)

    def _v0(v):
        v.memset(ones1[:], 1.0)
        C.inc("VW", v.memset(onecol[:], 1.0))
    C.V(_v0)

    zT = C.sb("zT", [128, DC * (T + 1)], BF16, side="right")
    zTt = C.sb("zTt", [128, DC * 640], BF16, side="right")
    ztile = C.sb("ztile", [128, D], F32)
    zbt = C.sb("zbt", [128, D], BF16)

    for tt in range(NT):
        def _y(y, tt=tt):
            C.waits(y, "VW", "ST", "GD")
            C.inc("LD", y.dma_start(out=ztile[:, :],
                                    in_=z_in[tt * 128:(tt + 1) * 128, :]), 16)
        C.Y(_y)

        def _v(v):
            C.waits(v, "LD", "TW")
            C.inc("VW", v.tensor_copy(zbt[:, :], ztile[:, :]))
        C.V(_v)

        pairs = [(zbt[:, dc * 128:(dc + 1) * 128],
                  zT[:, dc * (T + 1) + 1 + tt * 128: dc * (T + 1) + 1 + (tt + 1) * 128])
                 for dc in range(DC)]
        emit_transposes(C, pairs, idb)

    useScE = C.din("useSc", [1, 1], F32)
    uscE = C.sb("uscE", [128, 1], F32)
    urowE = C.sb("urowE", [1, 1], F32)

    def _yu(y):
        C.inc("LD", y.dma_start(out=urowE[:], in_=useScE), 16)
    C.Y(_yu)
    emit_scalar_broadcast(C, urowE[0:1, 0:1], uscE[:, 0:1], ones1)

    for tt in range(5):
        def _y(y, tt=tt):
            C.waits(y, "VW", "ST", "GD")
            C.inc("LD", y.dma_start(out=ztile[:, :],
                                    in_=ztail[tt * 128:(tt + 1) * 128, :]), 16)
        C.Y(_y)

        def _v(v):
            C.waits(v, "LD", "TW")
            C.edge(v, "VW", v.tensor_scalar(ztile[:, :], ztile[:, :],
                                            uscE[:, 0:1], None, ALU.mult))
            C.inc("VW", v.tensor_copy(zbt[:, :], ztile[:, :]))
        C.V(_v)

        pairs = [(zbt[:, dc * 128:(dc + 1) * 128],
                  zTt[:, dc * 640 + tt * 128: dc * 640 + (tt + 1) * 128])
                 for dc in range(DC)]
        emit_transposes(C, pairs, idb)

    def _vh(v):
        C.barrier_all(v)
        for dc in range(DC):
            mm = v.tensor_copy(zT[:, dc * (T + 1): dc * (T + 1) + 1],
                               zTt[:, dc * 640 + 639: dc * 640 + 640])
        C.inc("VW", mm)
    C.V(_vh)

    def hT_view(dc, lo, n):
        base = dc * (T + 1) + 1
        if dc >= 4:
            base -= 1
        return zT[:, base + lo: base + lo + n]

    def hTt_view(dc, lo, n):
        base = dc * 640 + 128
        if dc >= 4:
            base -= 1
        return zTt[:, base + lo: base + lo + n]

    if E1_STOP == 1:
        return C
    wqkvn = C.sb("wqkvn", [128, DC * 3 * INNER], BF16, side="right")

    def _gw(g):
        C.barrier_all(g)
        for dc in range(DC):
            C.inc("GD", g.dma_start(
                out=wqkvn[:, dc * 3 * INNER:(dc + 1) * 3 * INNER],
                in_=wqkv[dc * 128:(dc + 1) * 128, :]), 16)
    C.G(_gw)

    qb = C.sb("qb", [128, NT * INNER], BF16)
    kb = C.sb("kb", [128, NG * INNER], BF16)
    vb = C.sb("vb", [128, NG * INNER], BF16)

    for g in range(NG):
        is_own = g >= 4
        tt = g - 4
        outs = ((qb, 0), (kb, INNER), (vb, 2 * INNER)) if is_own else                ((kb, INNER), (vb, 2 * INNER))
        for buf, joff in outs:
            ps = C.bufs["pf0"] if joff != 2 * INNER else C.bufs["pf1"]

            def _t(t, g=g, is_own=is_own, tt=tt, joff=joff, ps=ps):
                C.barrier_all(t)
                for dc in range(DC):
                    view = hT_view(dc, tt * 128, 128) if is_own else                         hTt_view(dc, (g % 4) * 128, 128)
                    mm = t.matmul(ps[:, 0:512], view,
                                  wqkvn[:, dc * 3 * INNER + joff:
                                        dc * 3 * INNER + joff + INNER],
                                  start=(dc == 0), stop=(dc == DC - 1))
                C.inc("TW", mm)
            C.T(_t)

            def _s(s, g=g, tt=tt, buf=buf, joff=joff, ps=ps):
                C.waits(s, "TW")
                dst = buf[:, (tt if buf is qb else g) * INNER:
                          (tt if buf is qb else g) * INNER + INNER]
                C.inc("SW", s.copy(dst, ps[:, 0:512]))
            C.S(_s)

    if E1_STOP == 2:
        return C
    C.free("wqkvn", "zTt", "zT")

    qsb = C.sb("qsb", [128, INNER], F32)
    ksb = C.sb("ksb", [128, INNER], F32)
    qsrow = C.sb("qsrow", [1, INNER], F32)
    ksrow = C.sb("ksrow", [1, INNER], F32)

    def _yq(y):
        for h8 in range(IHEADS):
            C.inc("LD", y.dma_start(out=qsrow[0:1, h8 * HD:(h8 + 1) * HD],
                                    in_=qsr), 16)
            C.inc("LD", y.dma_start(out=ksrow[0:1, h8 * HD:(h8 + 1) * HD],
                                    in_=ksr), 16)
    C.Y(_yq)

    def _tb(t):
        C.barrier_all(t)
        t.matmul(C.bufs["pf4"][:, 0:INNER], ones1[0:1, 0:128], qsrow[0:1, :],
                 start=True, stop=True)
        C.inc("TW", t.matmul(C.bufs["pf5"][:, 0:INNER], ones1[0:1, 0:128],
                             ksrow[0:1, :], start=True, stop=True))
    C.T(_tb)

    def _vb(v):
        C.waits(v, "TW")
        v.tensor_copy(qsb[:, :], C.bufs["pf4"][:, 0:INNER])
        C.inc("VW", v.tensor_copy(ksb[:, :], C.bufs["pf5"][:, 0:INNER]))
    C.V(_vb)

    nrm = C.sb("nrm", [128, IHEADS], F32)
    sqs = C.sb("sqs", [128, HD], F32)
    rtile = C.sb("rtile", [128, 2 * INNER], F32)
    tmp1 = C.sb("tmp1", [128, INNER], F32)
    tmp2 = C.sb("tmp2", [128, INNER], F32)

    def emit_rotnorm(buf, g, sb_scale, t1_ap, t2_ap):
        tile = buf[:, g * INNER:(g + 1) * INNER]

        def _y(y, t1_ap=t1_ap, t2_ap=t2_ap):
            C.barrier_all(y)
            C.inc("LD", y.dma_start(out=rtile[:, 0:INNER], in_=t1_ap), 16)
            C.inc("LD", y.dma_start(out=rtile[:, INNER:2 * INNER], in_=t2_ap), 16)
        C.Y(_y)

        for h8 in range(IHEADS):
            def _s(s, h8=h8, tile=tile):
                C.waits(s, "VW", "SW")
                C.inc("SW", s.activation(sqs[:, :],
                                         tile[:, h8 * HD:(h8 + 1) * HD],
                                         AF.Square,
                                         accum_out=nrm[:, h8:h8 + 1]))
            C.S(_s)

        def _s2(s):
            C.waits(s, "SW")
            C.inc("SW", s.activation(nrm[:, :], nrm[:, :], AF.Sqrt))
        C.S(_s2)

        def _v(v, tile=tile):
            C.waits(v, "SW")
            C.edge(v, "VW", v.tensor_scalar(nrm[:, :], nrm[:, :], 1e-12, None,
                                            ALU.max))
            C.edge(v, "VW", v.reciprocal(nrm[:, :], nrm[:, :]))
            for h8 in range(IHEADS):
                mm = v.tensor_scalar(tile[:, h8 * HD:(h8 + 1) * HD],
                                     tile[:, h8 * HD:(h8 + 1) * HD],
                                     nrm[:, h8:h8 + 1], None, ALU.mult)
            C.edge(v, "VW", mm)
            C.edge(v, "VW", v.tensor_mul(tmp1[:, :], tile, sb_scale[:, :]))
            C.edge(v, "VW", v.tensor_scalar(
                tmp2.ap().rearrange("p (h x) -> p h x", x=HD)[:, :, 0:32],
                tmp1.ap().rearrange("p (h x) -> p h x", x=HD)[:, :, 32:64],
                -1.0, None, ALU.mult))
            C.edge(v, "VW", v.tensor_copy(
                tmp2.ap().rearrange("p (h x) -> p h x", x=HD)[:, :, 32:64],
                tmp1.ap().rearrange("p (h x) -> p h x", x=HD)[:, :, 0:32]))
            C.edge(v, "VW", v.tensor_mul(tmp1[:, :], tmp1[:, :],
                                         rtile[:, 0:INNER]))
            C.edge(v, "VW", v.tensor_mul(tmp2[:, :], tmp2[:, :],
                                         rtile[:, INNER:2 * INNER]))
            C.inc("VW", v.tensor_add(tile, tmp1[:, :], tmp2[:, :]))
        C.V(_v)

    for tt in range(NT):
        emit_rotnorm(qb, tt, qsb, tR1[tt * 128:(tt + 1) * 128, :],
                     tR2[tt * 128:(tt + 1) * 128, :])
    for g in range(NG):
        if g < 4:
            t3 = tR3t[(g % 4) * 128:((g % 4) + 1) * 128, :]
            t4 = tR4t[(g % 4) * 128:((g % 4) + 1) * 128, :]
        else:
            tt = g - 4
            t3 = tR3[tt * 128:(tt + 1) * 128, :]
            t4 = tR4[tt * 128:(tt + 1) * 128, :]
        emit_rotnorm(kb, g, ksb, t3, t4)

    if E1_STOP == 3:
        return C
    qTl = C.sb("qTl", [128, NT * INNER], BF16)
    kTl = C.sb("kTl", [128, NG * INNER], BF16)
    qbb = C.sb("qbb", [128, INNER], BF16)

    for tt in range(NT):
        def _v(v, tt=tt):
            C.waits(v, "VW", "TW")
            C.inc("VW", v.tensor_copy(qbb[:, :], qb[:, tt * INNER:(tt + 1) * INNER]))
        C.V(_v)
        pairs = [(qbb[:, jc * 128:(jc + 1) * 128],
                  qTl[:, (jc * NT + tt) * 128:(jc * NT + tt + 1) * 128])
                 for jc in range(4)]
        emit_transposes(C, pairs, idb)
    for g in range(NG):
        def _v(v, g=g):
            C.waits(v, "VW", "TW")
            C.inc("VW", v.tensor_copy(qbb[:, :], kb[:, g * INNER:(g + 1) * INNER]))
        C.V(_v)
        pairs = [(qbb[:, jc * 128:(jc + 1) * 128],
                  kTl[:, (jc * NG + g) * 128:(jc * NG + g + 1) * 128])
                 for jc in range(4)]
        emit_transposes(C, pairs, idb)

    if E1_STOP == 4:
        return C
    mbuf = C.sb("mbuf", [128, 2 * 8 * WIN], BF16)

    def _ym(y):
        for kt in range(8):
            C.inc("LD", y.dma_start(
                out=mbuf[:, kt * WIN:(kt + 1) * WIN],
                in_=mask0[kt * 128:(kt + 1) * 128, :]), 16)
            C.inc("LD", y.dma_start(
                out=mbuf[:, (8 + kt) * WIN:(8 + kt + 1) * WIN],
                in_=maskr[kt * 128:(kt + 1) * 128, :]), 16)
    C.Y(_ym)

    expt = C.sb("expt", [128, 4 * WIN], BF16)
    simb = C.sb("simb", [128, 4 * WIN], F32)
    obuf = C.sb("obuf", [128, WIN], F32)
    rcol = C.sb("rcol", [128, 1], F32)
    ob = C.sb("ob", [128, NT * INNER], BF16)

    for w in range(4):
        for h8 in range(IHEADS):
            hp, r = h8 // 2, h8 % 2
            rows = slice(r * 64, r * 64 + 64)
            pout = C.bufs["pf2"]
            sim_tw = [0] * 8
            mask_vw = [0] * 8
            exp_sw = [0] * 8
            v_tw = [0] * 8

            def _t_sim(t, w, kt, g, psim):
                if kt == 0:
                    C.barrier_all(t)
                if kt >= 2:
                    t.wait_ge(C.s["VW"].h, mask_vw[kt - 2])

                C.inc("TW", t.matmul(
                    psim[:, 0:WIN],
                    kTl[rows, (hp * NG + g) * 128:(hp * NG + g + 1) * 128],
                    qTl[rows, (hp * NT + 4 * w) * 128:
                        (hp * NT + 4 * w) * 128 + WIN],
                    start=True, stop=True))
                sim_tw[kt] = C.s["TW"].n

            def _v_mask(v, w, kt, psim):
                v.wait_ge(C.s["TW"].h, sim_tw[kt])
                if kt >= 4:
                    g = kt // 2 - 2
                    v.wait_ge(C.s["SW"].h, exp_sw[2 * g + 1])
                moff = (0 if w == 0 else 8 * WIN) + kt * WIN
                C.inc("VW", v.tensor_add(
                    simb[:, (kt % 4) * WIN:(kt % 4) * WIN + WIN],
                    psim[:, 0:WIN], mbuf[:, moff:moff + WIN]))
                mask_vw[kt] = C.s["VW"].n

            def _s_exp(s, kt):
                s.wait_ge(C.s["VW"].h, mask_vw[kt])
                if kt >= 4:
                    s.wait_ge(C.s["TW"].h, v_tw[kt - 4])
                gb = ((kt - 1) % 4) * WIN
                C.inc("SW", s.activation(
                    expt[:, gb: gb + 2 * WIN],
                    simb[:, gb: gb + 2 * WIN],
                    AF.Exp, scale=8.0))
                exp_sw[kt - 1] = C.s["SW"].n
                exp_sw[kt] = C.s["SW"].n

            def _t_v(t, kt, g, h8):
                t.wait_ge(C.s["SW"].h, exp_sw[kt])
                esl = expt[:, (kt % 4) * WIN:(kt % 4) * WIN + WIN]
                t.matmul(pout[0:64, 0:WIN],
                         vb[:, g * INNER + h8 * HD: g * INNER + (h8 + 1) * HD],
                         esl, start=(kt == 0), stop=(kt == 7))
                C.inc("TW", t.matmul(C.bufs["pf3"][0:1, 0:WIN],
                                     onecol[:, 0:1], esl,
                                     start=(kt == 0), stop=(kt == 7)))
                v_tw[kt] = C.s["TW"].n

            for kt in range(10):
                if kt < 8:
                    g = 4 * w + kt
                    psim = C.bufs[f"pf{kt % 2}"]
                    C.T(lambda t, w=w, kt=kt, g=g, psim=psim:
                        _t_sim(t, w, kt, g, psim))
                    C.V(lambda v, w=w, kt=kt, psim=psim:
                        _v_mask(v, w, kt, psim))
                    if kt % 2 == 1:
                        C.S(lambda s, kt=kt: _s_exp(s, kt))
                if kt >= 2:
                    g = 4 * w + kt - 2
                    C.T(lambda t, kt=kt, g=g, h8=h8: _t_v(t, kt - 2, g, h8))

            def _s2(s):
                C.waits(s, "TW", "ST")
                s.copy(obuf[0:64, 0:WIN], pout[0:64, 0:WIN])
                C.inc("SW", s.copy(obuf[64:65, 0:WIN], C.bufs["pf3"][0:1, 0:WIN]))
            C.S(_s2)

            for ci in range(4):
                def _t3(t, ci=ci):
                    C.waits(t, "SW", "VW")
                    C.inc("TW", t.transpose(
                        C.bufs["pf4"][0:128, 0:65],
                        obuf[0:65, ci * 128:(ci + 1) * 128], idf[0:65, 0:65]))
                C.T(_t3)

                def _v3(v, w=w, ci=ci, h8=h8):
                    C.waits(v, "TW")
                    C.edge(v, "VW", v.reciprocal(rcol[:, 0:1],
                                                 C.bufs["pf4"][:, 64:65]))
                    C.inc("VW", v.tensor_scalar(
                        ob[:, (w * 4 + ci) * INNER + h8 * HD:
                           (w * 4 + ci) * INNER + (h8 + 1) * HD],
                        C.bufs["pf4"][:, 0:64], rcol[:, 0:1], None, ALU.mult))
                C.V(_v3)

    won = C.sb("won", [128, 4 * D], BF16)
    oT = C.sb("oT", [128, 4 * 128], BF16)
    zpt = C.sb("zpt", [128, D], F32)

    def _gw2(g):
        C.barrier_all(g)
        for ic in range(4):
            C.inc("GD", g.dma_start(out=won[:, ic * D:(ic + 1) * D],
                                    in_=wo_in[ic * 128:(ic + 1) * 128, :]), 16)
    C.G(_gw2)
    if E1_STOP == 6:
        return C

    for tt in range(NT):
        pairs = [(ob[:, tt * INNER + jc * 128: tt * INNER + (jc + 1) * 128],
                  oT[:, jc * 128:(jc + 1) * 128]) for jc in range(4)]
        emit_transposes(C, pairs, idb)
        if E1_STOP == 61:
            return C

        def _yz(y, tt=tt):
            C.barrier_all(y)
            C.inc("LD", y.dma_start(out=ztile[:, :],
                                    in_=z_in[tt * 128:(tt + 1) * 128, :]), 16)
        C.Y(_yz)
        if E1_STOP == 611:
            return C

        for mh in range(2):
            ps = C.bufs[f"pf{mh}"]

            def _t(t, mh=mh, ps=ps):
                C.barrier_all(t)
                for ic in range(4):
                    mm = t.matmul(ps[:, 0:512], oT[:, ic * 128:(ic + 1) * 128],
                                  won[:, ic * D + mh * 512: ic * D + mh * 512 + 512],
                                  start=(ic == 0), stop=(ic == 3))
                C.inc("TW", mm)
            C.T(_t)
            if E1_STOP == 612:
                return C

            def _v(v, mh=mh, ps=ps):
                C.waits(v, "TW", "LD", "ST")
                C.inc("VW", v.tensor_add(zpt[:, mh * 512:mh * 512 + 512],
                                         ztile[:, mh * 512:mh * 512 + 512],
                                         ps[:, 0:512]))
            C.V(_v)
            if E1_STOP == 613:
                return C
            if E1_STOP == 614 and mh == 1:
                return C

        def _yo(y, tt=tt):
            C.waits(y, "VW")
            C.inc("ST", y.dma_start(out=o_zp[tt * 128:(tt + 1) * 128, :],
                                    in_=zpt[:, :]), 16)
        C.Y(_yo)
        if E1_STOP == 62 and tt == 0:
            return C

    if E1_STOP == 615:
        return C

    if own:
        def _yfin(y):
            y.wait_ge(C.s["ST"].h, C.s["ST"].n)
        C.Y(_yfin)

    if own:
        C.close()
    return C


HID = 2730
NOC = 6
OCW = 455
NIC2 = 22


def build_E2(final, C=None):
    own = C is None
    if own:
        C = Ctx("E2f" if final else "E2").start()
    zp = C.din("zp", [T, D], F32)
    zptail = C.din("zptail", [1, D], F32)
    lg = C.din("lg", [1, D], F32)
    lb = C.din("lb", [1, D], F32)
    w1 = C.din("w1", [D, 2 * HID], F32)
    b1 = C.din("b1", [1, 2 * HID], F32)
    w2 = C.din("w2", [HID, D], F32)
    b2 = C.din("b2", [1, D], F32)
    if final:
        x1 = C.din("x1", [T, D], F32)
        a1 = C.din("a1", [T, D], F32)
    o_z = C.dout("z", [T, D], F32)

    idb, idf = emit_identity(C)
    ones1 = C.sb("ones1", [1, 128], F32)

    def _v0(v):
        C.inc("VW", v.memset(ones1[:], 1.0))
    C.V(_v0)

    zpT = C.sb("zpT", [128, DC * (T + 1)], BF16, side="right")
    ztile = C.sb("ztile", [128, D], F32)
    zbt = C.sb("zbt", [128, D], BF16)
    tailr = C.sb("tailr", [1, D], F32, side="right")
    tailb = C.sb("tailb", [1, D], BF16, side="right")

    def _yt(y):
        C.barrier_all(y)
        C.inc("LD", y.dma_start(out=tailr[:], in_=zptail), 16)
    C.Y(_yt)

    useScF = C.din("useSc", [1, 1], F32)
    urowF = C.sb("urowF", [1, 1], F32)

    def _yu(y):
        C.inc("LD", y.dma_start(out=urowF[:], in_=useScF), 16)
    C.Y(_yu)

    def _vt(v):
        C.waits(v, "LD")
        C.edge(v, "VW", v.tensor_scalar(tailr[:, :], tailr[:, :],
                                        urowF[0:1, 0:1], None, ALU.mult))
        C.inc("VW", v.tensor_copy(tailb[:, :], tailr[:, :]))
    C.V(_vt)

    for dc in range(DC):
        def _t(t, dc=dc):
            C.barrier_all(t)
            C.inc("TW", t.transpose(C.bufs["pb0"][0:128, 0:1],
                                    tailb[0:1, dc * 128:(dc + 1) * 128],
                                    idb[0:1, 0:1]))
        C.T(_t)

        def _v(v, dc=dc):
            C.waits(v, "TW")
            C.inc("VW", v.tensor_copy(zpT[:, dc * (T + 1): dc * (T + 1) + 1],
                                      C.bufs["pb0"][0:128, 0:1]))
        C.V(_v)

    s1b = C.sb("s1b", [128, NT], F32)
    s2b = C.sb("s2b", [128, NT], F32)
    q1b = C.sb("q1b", [128, NT], F32)
    q2b = C.sb("q2b", [128, NT], F32)
    s2s = C.sb("s2s", [128, NT], F32)
    q2s = C.sb("q2s", [128, NT], F32)
    sqs = C.sb("sqs2", [128, 512], F32)
    hstat = C.sb("hstat", [1, 2], F32)

    def _vhs(v):
        C.waits(v, "LD")
        C.inc("VW", v.reduce_sum(hstat[0:1, 0:1], tailr[0:1, 512:1024], AX.X))
    C.V(_vhs)

    def _shs(s):
        C.waits(s, "VW")
        C.inc("SW", s.activation(sqs[0:1, :], tailr[0:1, 512:1024], AF.Square,
                                 accum_out=hstat[0:1, 1:2]))
    C.S(_shs)
    C.free("tailb", "tailr")

    for tt in range(NT):
        def _y(y, tt=tt):
            C.waits(y, "VW", "SW", "TW")
            C.inc("LD", y.dma_start(out=ztile[:, :],
                                    in_=zp[tt * 128:(tt + 1) * 128, :]), 16)
        C.Y(_y)

        def _v(v, tt=tt):
            C.waits(v, "LD", "TW")
            v.reduce_sum(s1b[:, tt:tt + 1], ztile[:, 0:512], AX.X)
            v.reduce_sum(s2b[:, tt:tt + 1], ztile[:, 512:1024], AX.X)
            C.inc("VW", v.tensor_copy(zbt[:, :], ztile[:, :]))
        C.V(_v)

        def _s(s, tt=tt):
            C.waits(s, "VW")
            s.activation(sqs[:, :], ztile[:, 0:512], AF.Square,
                         accum_out=q1b[:, tt:tt + 1])
            C.inc("SW", s.activation(sqs[:, :], ztile[:, 512:1024], AF.Square,
                                     accum_out=q2b[:, tt:tt + 1]))
        C.S(_s)

        pairs = [(zbt[:, dc * 128:(dc + 1) * 128],
                  zpT[:, dc * (T + 1) + 1 + tt * 128:
                      dc * (T + 1) + 1 + (tt + 1) * 128])
                 for dc in range(DC)]
        emit_transposes(C, pairs, idb)

    skd = C.nc.dram_tensor(f"skd{C.sec}" if C.sec else "skd",
                           [128, 2 * NT], F32).ap()

    def _ysko(y):
        C.barrier_all(y)
        C.inc("ST", y.dma_start(out=skd[:, 0:NT], in_=s2b[:, :]), 16)
        C.inc("ST", y.dma_start(out=skd[:, NT:2 * NT], in_=q2b[:, :]), 16)
    C.Y(_ysko)

    def _yski(y):
        C.waits(y, "ST")
        C.inc("LD", y.dma_start(out=s2s[1:128, :], in_=skd[0:127, 0:NT]), 16)
        C.inc("LD", y.dma_start(out=s2s[0:1, 1:NT], in_=skd[127:128, 0:NT - 1]), 16)
        C.inc("LD", y.dma_start(out=q2s[1:128, :], in_=skd[0:127, NT:2 * NT]), 16)
        C.inc("LD", y.dma_start(out=q2s[0:1, 1:NT],
                                in_=skd[127:128, NT:2 * NT - 1]), 16)
    C.Y(_yski)

    def _vsk(v):
        C.waits(v, "LD", "SW")
        v.tensor_copy(s2s[0:1, 0:1], hstat[0:1, 0:1])
        C.inc("VW", v.tensor_copy(q2s[0:1, 0:1], hstat[0:1, 1:2]))
    C.V(_vsk)

    mu = C.sb("mu", [128, NT], F32)
    var = C.sb("var", [128, NT], F32)
    rstd = C.sb("rstdc", [128, NT], F32)
    stdv = C.sb("stdc", [128, NT], F32)
    negmu = C.sb("negmu", [128, NT], F32)

    def _vst(v):
        C.waits(v, "VW", "LD", "SW")
        C.edge(v, "VW", v.tensor_add(mu[:, :], s1b[:, :], s2s[:, :]))
        C.edge(v, "VW", v.tensor_scalar(mu[:, :], mu[:, :], 1.0 / 1024.0, None,
                                        ALU.mult))
        C.edge(v, "VW", v.tensor_add(var[:, :], q1b[:, :], q2s[:, :]))
        C.edge(v, "VW", v.tensor_scalar(var[:, :], var[:, :], 1.0 / 1024.0,
                                        None, ALU.mult))
        C.edge(v, "VW", v.tensor_mul(s1b[:, :], mu[:, :], mu[:, :]))
        C.edge(v, "VW", v.tensor_sub(var[:, :], var[:, :], s1b[:, :]))
        C.edge(v, "VW", v.tensor_scalar(var[:, :], var[:, :], 1e-5, None,
                                        ALU.add))
        C.inc("VW", v.tensor_scalar(negmu[:, :], mu[:, :], -1.0, None,
                                    ALU.mult))
    C.V(_vst)

    def _sst(s):
        C.waits(s, "VW")
        C.inc("SW", s.activation(stdv[:, :], var[:, :], AF.Sqrt))
    C.S(_sst)

    def _vst2(v):
        C.waits(v, "SW")
        C.inc("VW", v.reciprocal(rstd[:, :], stdv[:, :]))
    C.V(_vst2)

    if DBG_D:
        def _yds(y):
            C.barrier_all(y)
            dbg = C.nc.dram_tensor("o_dbgs", [8, 128, 1], F32,
                                   kind="ExternalOutput").ap()
            for di, src in enumerate((s1b, s2s, q1b, q2s, mu, var, rstd, stdv)):
                C.inc("ST", y.dma_start(out=dbg[di], in_=src[:, 0:1]), 16)
        C.Y(_yds)

    negmurow = C.sb("negmurow", [1, T], BF16)
    stdrow = C.sb("stdrow", [1, T], BF16)

    for tt in range(NT):
        for src, dstrow in ((negmu, negmurow), (stdv, stdrow)):
            def _t(t, tt=tt, src=src):
                C.barrier_all(t)
                C.inc("TW", t.transpose(C.bufs["pf4"][0:1, 0:128],
                                        src[:, tt:tt + 1], idf[:, :]))
            C.T(_t)

            def _v(v, tt=tt, dstrow=dstrow):
                C.waits(v, "TW")
                C.inc("VW", v.tensor_copy(dstrow[0:1, tt * 128:(tt + 1) * 128],
                                          C.bufs["pf4"][0:1, 0:128]))
            C.V(_v)

    w1b = C.sb("w1b", [128, DC * 2 * HID], BF16, side="right")

    def _gw1(g):
        C.barrier_all(g)
        for dc in range(DC):
            C.inc("GD", g.dma_start(out=w1b[:, dc * 2 * HID:(dc + 1) * 2 * HID],
                                    in_=w1[dc * 128:(dc + 1) * 128, :]), 16)
    C.G(_gw1)

    lgc, _ = emit_bias_cols(C, lg, D, "lg")
    lbc, _ = emit_bias_cols(C, lb, D, "lb")
    lbcb = C.sb("lbcb", [128, DC], BF16)
    onecb = C.sb("onecb", [128, 1], BF16)
    c1row = C.sb("c1row", [1, 2 * HID], BF16)
    c2row = C.sb("c2row", [1, 2 * HID], BF16)
    b1row = C.sb("b1row", [1, 2 * HID], F32, side="right")

    def _yb1(y):
        C.inc("LD", y.dma_start(out=b1row[:], in_=b1), 16)
    C.Y(_yb1)

    def _vlb(v):
        C.barrier_all(v)
        v.tensor_copy(lbcb[:, :], lbc[:, :])
        C.inc("VW", v.memset(onecb[:], 1.0))
    C.V(_vlb)

    NSL = (2 * HID + 511) // 512
    for sl in range(NSL):
        o0 = sl * 512
        ow = min(512, 2 * HID - o0)

        def _t(t, o0=o0, ow=ow):
            C.barrier_all(t)
            for dc in range(DC):
                mm = t.matmul(C.bufs["pf4"][0:1, 0:ow], lbcb[:, dc:dc + 1],
                              w1b[:, dc * 2 * HID + o0: dc * 2 * HID + o0 + ow],
                              start=(dc == 0), stop=(dc == DC - 1))
            C.inc("TW", mm)
        C.T(_t)

        def _v(v, o0=o0, ow=ow):
            C.waits(v, "TW")
            C.inc("VW", v.tensor_add(c2row[0:1, o0:o0 + ow],
                                     C.bufs["pf4"][0:1, 0:ow],
                                     b1row[0:1, o0:o0 + ow]))
        C.V(_v)

    C.free("b1row")

    def _vw1(v):
        C.barrier_all(v)
        for dc in range(DC):
            mm = v.tensor_scalar(w1b[:, dc * 2 * HID:(dc + 1) * 2 * HID],
                                 w1b[:, dc * 2 * HID:(dc + 1) * 2 * HID],
                                 lgc[:, dc:dc + 1], None, ALU.mult)
        C.inc("VW", mm)
    C.V(_vw1)

    for sl in range(NSL):
        o0 = sl * 512
        ow = min(512, 2 * HID - o0)

        def _t(t, o0=o0, ow=ow):
            C.barrier_all(t)
            for dc in range(DC):
                mm = t.matmul(C.bufs["pf4"][0:1, 0:ow], onecb[:, 0:1],
                              w1b[:, dc * 2 * HID + o0: dc * 2 * HID + o0 + ow],
                              start=(dc == 0), stop=(dc == DC - 1))
            C.inc("TW", mm)
        C.T(_t)

        def _v(v, o0=o0, ow=ow):
            C.waits(v, "TW")
            C.inc("VW", v.tensor_copy(c1row[0:1, o0:o0 + ow],
                                      C.bufs["pf4"][0:1, 0:ow]))
        C.V(_v)

    def h2T_view(dc, lo, n):
        base = dc * (T + 1) + 1
        if dc >= 4:
            base -= 1
        return zpT[:, base + lo: base + lo + n]

    if DBG_D:
        def _ydo(y):
            C.barrier_all(y)
            dbg2 = C.nc.dram_tensor("o_dbgo", [4, 1, 512], BF16,
                                    kind="ExternalOutput").ap()
            C.inc("ST", y.dma_start(out=dbg2[0], in_=w1b[0:1, 0:512]), 16)
            C.inc("ST", y.dma_start(out=dbg2[1], in_=zpT[0:1, 0:512]), 16)
            C.inc("ST", y.dma_start(out=dbg2[2], in_=c1row[0:1, 0:512]), 16)
            C.inc("ST", y.dma_start(out=dbg2[3], in_=c2row[0:1, 0:512]), 16)
        C.Y(_ydo)

    ya = C.sb("ya", [128, OCW], F32)
    yg = C.sb("yg", [128, OCW], F32)
    h3buf = C.sb("h3buf", [128, HID], BF16)
    h3T = C.sb("h3T", [128, NIC2 * 128], BF16)
    w2r = C.sb("w2r", [128, NIC2 * 512], BF16)
    b2row = C.sb("b2row", [1, D], F32)
    outt = C.sb("outt", [128, D], F32)
    xt1 = C.sb("xt1", [128, 512], F32) if final else None
    at1 = C.sb("at1", [128, 512], F32) if final else None

    def _yb2(y):
        C.inc("LD", y.dma_start(out=b2row[:], in_=b2), 16)
    C.Y(_yb2)

    w1_evac = [0, 0]
    w1_gtw = [0, 0]
    w1_h3vw = [0]
    for tt in range(NT):
        trow = slice(tt * 128, (tt + 1) * 128)
        for i in range(NOC):
            for half, (psn, ybuf) in enumerate((("pf0", ya), ("pf1", yg))):
                o0 = i * OCW + half * HID
                ps = C.bufs[psn]

                def _t(t, tt=tt, i=i, half=half, o0=o0, ps=ps):
                    if tt == 0 and i == 0 and half == 0:
                        C.barrier_all(t)
                    elif w1_evac[half]:
                        t.wait_ge(C.s["VW"].h, w1_evac[half])
                    for dc in range(DC):
                        t.matmul(ps[:, 0:OCW], h2T_view(dc, tt * 128, 128),
                                 w1b[:, dc * 2 * HID + o0: dc * 2 * HID + o0 + OCW],
                                 start=(dc == 0), stop=False)
                    t.matmul(ps[:, 0:OCW],
                             negmurow[0:1, tt * 128:(tt + 1) * 128],
                             c1row[0:1, o0:o0 + OCW], start=False, stop=False)
                    C.inc("TW", t.matmul(
                        ps[:, 0:OCW], stdrow[0:1, tt * 128:(tt + 1) * 128],
                        c2row[0:1, o0:o0 + OCW], start=False, stop=True))
                    w1_gtw[half] = C.s["TW"].n
                C.T(_t)

                def _v(v, tt=tt, half=half, ps=ps, ybuf=ybuf):
                    v.wait_ge(C.s["TW"].h, w1_gtw[half])
                    if w1_h3vw[0]:
                        v.wait_ge(C.s["SW"].h, w1_h3vw[0])
                    C.inc("VW", v.tensor_scalar(ybuf[:, :], ps[:, 0:OCW],
                                                rstd[:, tt:tt + 1], None,
                                                ALU.mult))
                    w1_evac[half] = C.s["VW"].n
                C.V(_v)

            if DBG_D and tt == 0 and i == 0:
                def _yd(y):
                    C.waits(y, "VW")
                    C.inc("ST", y.dma_start(out=C.nc.dram_tensor(
                        "o_dbgy", [128, 2 * OCW], F32,
                        kind="ExternalOutput").ap()[:, 0:OCW], in_=ya[:, :]), 16)
                    C.inc("ST", y.dma_start(
                        out=C.nc.dram_tensor("o_dbgy2", [128, OCW], F32,
                                             kind="ExternalOutput").ap(),
                        in_=yg[:, :]), 16)
                C.Y(_yd)

            def _sg(s):
                C.waits(s, "VW")
                C.inc("SW", s.activation(yg[:, :], yg[:, :], AF.Gelu))
            C.S(_sg)

            def _vh3(v, i=i):
                C.waits(v, "SW", "TW")
                C.inc("VW", v.tensor_mul(h3buf[:, i * OCW:(i + 1) * OCW],
                                         ya[:, :], yg[:, :]))
                w1_h3vw[0] = C.s["SW"].n
            C.V(_vh3)

        pairs = []
        for ic in range(NIC2):
            wdt = min(128, HID - ic * 128)
            pairs.append((h3buf[:, ic * 128: ic * 128 + wdt],
                          h3T[0:wdt, ic * 128:(ic + 1) * 128]))
        emit_transposes(C, pairs, idb)

        def _yz(y, tt=tt, trow=trow):
            C.barrier_all(y)
            C.inc("LD", y.dma_start(out=ztile[:, :], in_=zp[trow, :]), 16)
        C.Y(_yz)

        for mh in range(2):
            if final:
                def _yxa(y, tt=tt, trow=trow, mh=mh):
                    C.barrier_all(y)
                    C.inc("LD", y.dma_start(
                        out=xt1[:, :], in_=x1[trow, mh * 512:mh * 512 + 512]), 16)
                    C.inc("LD", y.dma_start(
                        out=at1[:, :], in_=a1[trow, mh * 512:mh * 512 + 512]), 16)
                C.Y(_yxa)
            def _yw2(y, mh=mh):
                C.waits(y, "TW")
                for ic in range(NIC2):
                    rw = min(128, HID - ic * 128)
                    C.inc("GD", y.dma_start(
                        out=w2r[:rw, ic * 512:(ic + 1) * 512],
                        in_=w2[ic * 128: ic * 128 + rw,
                               mh * 512: mh * 512 + 512]), 16)
            C.G(_yw2)

            ps = C.bufs[f"pf{2 + mh}"]

            def _t2(t, mh=mh, ps=ps):
                C.waits(t, "GD", "VW", "LD")
                for ic in range(NIC2):
                    rw = min(128, HID - ic * 128)
                    t.matmul(ps[:, 0:512], h3T[0:rw, ic * 128:(ic + 1) * 128],
                             w2r[0:rw, ic * 512:(ic + 1) * 512],
                             start=(ic == 0), stop=False)
                C.inc("TW", t.matmul(ps[:, 0:512], ones1[0:1, 0:128],
                                     b2row[0:1, mh * 512: mh * 512 + 512],
                                     start=False, stop=True))
            C.T(_t2)

            def _v2(v, mh=mh, ps=ps):
                C.waits(v, "TW", "LD", "ST")
                C.edge(v, "VW", v.tensor_add(outt[:, mh * 512:mh * 512 + 512],
                                             ztile[:, mh * 512:mh * 512 + 512],
                                             ps[:, 0:512]))
                if final:
                    C.edge(v, "VW", v.tensor_add(
                        outt[:, mh * 512:mh * 512 + 512],
                        outt[:, mh * 512:mh * 512 + 512], xt1[:, :]))
                    C.edge(v, "VW", v.tensor_add(
                        outt[:, mh * 512:mh * 512 + 512],
                        outt[:, mh * 512:mh * 512 + 512], at1[:, :]))
            C.V(_v2)

        def _yo(y, trow=trow):
            C.waits(y, "VW")
            C.inc("ST", y.dma_start(out=o_z[trow, :], in_=outt[:, :]), 16)
        C.Y(_yo)

    if own:
        def _yfin(y):
            y.wait_ge(C.s["ST"].h, C.s["ST"].n)
        C.Y(_yfin)

    if own:
        C.close()
    return C


def run_E2(inputs, layer, zp_shards, final=False, x1_shards=None, a1_shards=None):
    C = build_E2(final)
    im = []
    for c in range(8):
        b, h = c // 2, c % 2
        part = 2 * b + (1 - h)
        zptail = np.zeros((1, D), np.float32)
        if h == 1:
            zptail[0] = zp_shards[part][2047]
        m = {
            "i_zp": zp_shards[c],
            "i_zptail": zptail,
            "i_useSc": np.ones((1, 1), np.float32),
            "i_lg": inputs["L1_ffn_ln_g"][layer].reshape(1, D),
            "i_lb": inputs["L1_ffn_ln_b"][layer].reshape(1, D),
            "i_w1": np.ascontiguousarray(inputs["L1_ffn_w1"][layer]),
            "i_b1": inputs["L1_ffn_b1"][layer].reshape(1, 2 * HID),
            "i_w2": np.ascontiguousarray(inputs["L1_ffn_w2"][layer]),
            "i_b2": inputs["L1_ffn_b2"][layer].reshape(1, D),
        }
        if final:
            m["i_x1"] = x1_shards[c]
            m["i_a1"] = a1_shards[c]
        im.append(m)
    res = run_bass_kernel_spmd(C.nc, im, core_ids=list(range(8)))
    return res.results


def run_B(inputs, resA):
    C = build_B()
    im = []
    for c in range(8):
        b, h = c // 2, c % 2
        part = 2 * b + (1 - h)
        Sprev = resA[part]["o_S"] if h == 1 else np.zeros((D, HD), np.float32)
        maxes = np.concatenate([resA[j]["o_maxx"].reshape(1, 1)
                                for j in range(8)], axis=1)
        idx = np.arange(h * T, (h + 1) * T, dtype=np.float32).reshape(1, T)
        im.append({
            "i_xn": resA[c]["o_xn"],
            "i_Sprev": np.ascontiguousarray(Sprev),
            "i_useSc": np.ones((1, 1), np.float32),
            "i_maxes": maxes,
            "i_clipx": inputs["L0_ffn_clip_x"].reshape(1, 1),
            "i_idx": idx,
            "i_gamma": inputs["L0_attn_gamma"].reshape(1, NH),
            "i_Wq": inputs["L0_attn_Wq"],
            "i_bq": inputs["L0_attn_bq"].reshape(1, D),
            "i_Wk": inputs["L0_attn_Wk"],
            "i_bk": inputs["L0_attn_bk"].reshape(1, D),
            "i_Wv": inputs["L0_attn_Wv"],
            "i_bv": inputs["L0_attn_bv"].reshape(1, D),
            "i_Wo": inputs["L0_attn_Wo"],
            "i_bo": inputs["L0_attn_bo"].reshape(1, D),
            "i_wgate": inputs["L0_ffn_w_gate"],
            "i_wup": inputs["L0_ffn_w_up"],
        })
    res = run_bass_kernel_spmd(C.nc, im, core_ids=list(range(8)))
    return res.results


def run_C(inputs, resB):
    C = build_C()
    xs = shard_x(inputs["L0_attn_x"])
    hmaxes = np.concatenate([resB[j]["o_hmax"].reshape(1, 1)
                             for j in range(8)], axis=1)
    im = []
    for c in range(8):
        im.append({
            "i_x0": xs[c],
            "i_a0": resB[c]["o_a0"],
            "i_xq": resB[c]["o_xq"],
            "i_hmaxes": hmaxes,
            "i_clipd": inputs["L0_ffn_clip_down"].reshape(1, 1),
            "i_wgate": inputs["L0_ffn_w_gate"],
            "i_wup": inputs["L0_ffn_w_up"],
            "i_wdown": inputs["L0_ffn_w_down"],
            "i_Wk1": inputs["L1_attn_Wk"],
            "i_bk1": inputs["L1_attn_bk"].reshape(1, D),
        })
    res = run_bass_kernel_spmd(C.nc, im, core_ids=list(range(8)))
    return res.results


def _rot_tables(pos, kind):
    half = np.arange(0, HD, 2, dtype=np.float32)
    invf = 1.0 / 10000.0 ** (half / HD)
    t = pos.astype(np.float32)
    fr = t[:, None] * invf[None, :]
    fr = np.concatenate([fr, fr], 1)
    base = (half + 0.4 * HD) / (1.4 * HD)
    power = (t - (2 * T) // 2) / float(WIN // 2)
    sc = base[None, :] ** power[:, None]
    sc = np.concatenate([sc, sc], 1)
    cos, sin = np.cos(fr), np.sin(fr)
    if kind == "q":
        a, b = cos * sc, sin * sc
    else:
        a, b = cos / sc, sin / sc
    return (np.tile(a, (1, IHEADS)).astype(np.float32),
            np.tile(b, (1, IHEADS)).astype(np.float32))


def _inner_masks(first_blocked):
    r = np.arange(1024)[:, None]
    qi = np.arange(512)[None, :]
    allowed = (qi >= r - 512) & (qi <= r)
    m = np.where(allowed, 0.0, -30000.0).astype(np.float32)
    if first_blocked:
        m0 = m.copy()
        m0[0:512, :] = -30000.0
        return m0
    return m


def run_E1(inputs, layer, z_shards):
    C = build_E1()
    im = []
    for c in range(8):
        b, h = c // 2, c % 2
        part = 2 * b + (1 - h)
        ztail = np.zeros((640, D), np.float32)
        if h == 1:
            ztail[127:640] = z_shards[part][1535:2048]
        pos = np.arange(h * T, (h + 1) * T)
        tpos = np.arange(1536, 2048) if h == 1 else np.arange(512)
        q1, q2 = _rot_tables(pos, "q")
        k1, k2 = _rot_tables(pos, "k")
        kt1, kt2 = _rot_tables(tpos, "k")
        im.append({
            "i_z": z_shards[c],
            "i_ztail": ztail,
            "i_wqkv": np.ascontiguousarray(inputs["L1_ffn_wqkv"][layer]),
            "i_qs": inputs["L1_ffn_q_scale"][layer].reshape(1, HD),
            "i_ks": inputs["L1_ffn_k_scale"][layer].reshape(1, HD),
            "i_wo": np.ascontiguousarray(inputs["L1_ffn_wo"][layer]),
            "i_tR1": q1, "i_tR2": q2, "i_tR3": k1, "i_tR4": k2,
            "i_tR3t": kt1, "i_tR4t": kt2,
            "i_useSc": np.ones((1, 1), np.float32),
            "i_mask0": _inner_masks(h == 0).astype(ml_dtypes.bfloat16),
            "i_maskr": _inner_masks(False).astype(ml_dtypes.bfloat16),
        })
    res = run_bass_kernel_spmd(C.nc, im, core_ids=list(range(8)))
    return res.results




def emit_pair_send(C, slot, gen, payloads):
    def _g(g):
        C.barrier_all(g)
        with g.register(f"s{slot}h") as rh:
            g.reg_load(rh, C.bufs["myh_sb"][0:1, 0:1])
            for m in range(2):
                guard = g.If_eq(rh, 0) if m == 0 else g.Else()
                with guard:
                    for dram_t, rpm, row_lo, row_n, src in payloads:
                        g.dma_start(
                            out=dram_t.ap()[m * rpm: m * rpm + row_n, :],
                            in_=src).then_inc(C.s["GD"].h, 16)
            C.s["GD"].n += 16 * len(payloads)
            g.wait_ge(C.s["GD"].h, C.s["GD"].n)
            for m in range(2):
                guard = g.If_eq(rh, 0) if m == 0 else g.Else()
                with guard:
                    g.dma_start(
                        out=C.bufs["xflg_dram"].ap()[m:m + 1, slot:slot + 1],
                        in_=C.bufs["gens_sb"][0:1, gen:gen + 1]).then_inc(
                            C.s["GD"].h, 16)
            C.s["GD"].n += 16
            g.wait_ge(C.s["GD"].h, C.s["GD"].n)
    C.G(_g)


def emit_pair_recv(C, slot, gen, payloads):
    def _g(g):
        C.barrier_all(g)
        with g.register(f"r{slot}h") as rh, g.register(f"r{slot}f") as rf, \
             g.register(f"r{slot}v") as rv, g.register(f"r{slot}t") as rt:
            g.reg_load(rh, C.bufs["prh_sb"][0:1, 0:1])
            g.reg_mov(rt, 0)
            g.reg_mov(rf, 1)
            with g.While(rf):
                g.dma_start(
                    C.bufs["pollb"][0:1, 0:2],
                    bass.AP(C.bufs["xflg_dram"], slot,
                            [[64, 2], [1, 1], [1, 1]]),
                    allow_slow_non_contiguous=True).then_inc(
                        C.s["PL"].h, 16)
                g.reg_add(rt, rt, 16)
                g.wait_ge(C.s["PL"].h, rt)
                with g.If_eq(rh, 0):
                    g.reg_load(rv, C.bufs["pollb"][0:1, 0:1])
                with g.Else():
                    g.reg_load(rv, C.bufs["pollb"][0:1, 1:2])
                g.reg_sub(rf, rv, gen)
            for m in range(2):
                guard = g.If_eq(rh, 0) if m == 0 else g.Else()
                with guard:
                    for dst, dram_t, rpm, row_lo, row_n in payloads:
                        g.dma_start(
                            out=dst,
                            in_=dram_t.ap()[m * rpm: m * rpm + row_n, :]
                            ).then_inc(C.s["GD"].h, 16)
            C.s["GD"].n += 16 * len(payloads)
            g.wait_ge(C.s["GD"].h, C.s["GD"].n)
    C.G(_g)

    def _v(v):
        C.waits(v, "GD")
        C.inc("VW", v.engine_nop())
    C.V(_v)


def emit_FA(C, x0f, fstats, gmax_sb, maxesD):
    C.section("FA")
    NF = 48
    ft = C.sb("fa_t", [128, D], F32)
    rs = C.sb("fa_rs", [128, 6], F32)
    sq = C.sb("fa_sq", [128, D], F32)
    ssq = C.sb("fa_ssq", [128, 1], F32)
    fmx = C.sb("fa_fmx", [128, NF], F32)
    half = C.sb("fa_half", [1, 2], F32)

    for i in range(NF):
        def _y(y, i=i):
            C.barrier_all(y)
            C.inc("LD", y.dma_start(out=ft[:, :],
                                    in_=x0f[i * 128:(i + 1) * 128, :]), 16)
        C.Y(_y)

        def _v(v, i=i):
            C.waits(v, "LD")
            C.edge(v, "VW", v.reduce_sum(rs[:, 0:1], ft[:, :], AX.X))
            C.edge(v, "VW", v.reduce_max(rs[:, 1:2], ft[:, :], AX.X))
            C.edge(v, "VW", v.tensor_reduce(rs[:, 2:3], ft[:, :], AX.X,
                                            ALU.min))
            C.inc("VW", v.tensor_scalar(fstats[:, 2 * i:2 * i + 1], rs[:, 0:1],
                                        1.0 / 1024.0, None, ALU.mult))
        C.V(_v)

        def _s(s, i=i):
            C.waits(s, "VW")
            C.inc("SW", s.activation(sq[:, :], ft[:, :], AF.Square,
                                     accum_out=ssq[:, 0:1]))
        C.S(_s)

        def _v2(v, i=i):
            C.waits(v, "SW")
            mu = fstats[:, 2 * i:2 * i + 1]
            rstd = fstats[:, 2 * i + 1:2 * i + 2]
            C.edge(v, "VW", v.tensor_scalar(rs[:, 4:5], ssq[:, 0:1],
                                            1.0 / 1024.0, None, ALU.mult))
            C.edge(v, "VW", v.tensor_mul(rs[:, 5:6], mu, mu))
            C.edge(v, "VW", v.tensor_sub(rs[:, 4:5], rs[:, 4:5], rs[:, 5:6]))
            C.edge(v, "VW", v.tensor_scalar(rs[:, 4:5], rs[:, 4:5], 1e-5,
                                            None, ALU.add))
            C.inc("VW", v.engine_nop())
        C.V(_v2)

        def _s2(s, i=i):
            C.waits(s, "VW")
            C.inc("SW", s.activation(rs[:, 4:5], rs[:, 4:5], AF.Sqrt))
        C.S(_s2)

        def _v3(v, i=i):
            C.waits(v, "SW")
            mu = fstats[:, 2 * i:2 * i + 1]
            rstd = fstats[:, 2 * i + 1:2 * i + 2]
            C.edge(v, "VW", v.reciprocal(rstd, rs[:, 4:5]))
            C.edge(v, "VW", v.tensor_sub(rs[:, 1:2], rs[:, 1:2], mu))
            C.edge(v, "VW", v.tensor_sub(rs[:, 2:3], mu, rs[:, 2:3]))
            C.edge(v, "VW", v.tensor_max(rs[:, 1:2], rs[:, 1:2], rs[:, 2:3]))
            C.inc("VW", v.tensor_mul(fmx[:, i:i + 1], rs[:, 1:2], rstd))
        C.V(_v3)

    emit_absmax_tiles(C, [fmx[:, 0:NF]], half[0:1, 0:1])

    def _vc(v):
        C.barrier_all(v)
        C.inc("VW", v.tensor_max(half[0:1, 0:1], half[0:1, 0:1],
                                 gmax_sb[0:1, 0:1]))
    C.V(_vc)

    hd = C.nc.dram_tensor("fa_halfD", [2, 1], F32)

    def _yh(y):
        C.waits(y, "VW")
        C.inc("ST", y.dma_start(out=C.bufs["scratch1"].ap()[0:1, :],
                                in_=half[0:1, 0:1]), 16)
    C.Y(_yh)
    emit_pair_send(C, 2, 2, [(C.bufs["xMX1"], 1, 0, 1,
                              C.bufs["scratch1"].ap()[0:1, :])])
    emit_pair_recv(C, 2, 2, [(half[0:1, 1:2], C.bufs["xMX1"], 1, 0, 1)])

    def _vg(v):
        C.waits(v, "GD", "GW")
        C.inc("VW", v.tensor_max(half[0:1, 0:1], half[0:1, 0:1],
                                 half[0:1, 1:2]))
    C.V(_vg)

    gm8 = C.sb("fa_gm8", [1, 8], F32)

    def _vb(v):
        C.waits(v, "VW")
        for i in range(8):
            mm = v.tensor_copy(gm8[0:1, i:i + 1], half[0:1, 0:1])
        C.inc("VW", mm)
    C.V(_vb)

    def _yo(y):
        C.waits(y, "VW")
        C.inc("ST", y.dma_start(out=maxesD, in_=gm8[0:1, :]), 16)
    C.Y(_yo)


def emit_FB(C, x0f, fstats, hmaxesD):
    C.section("FB")
    NF = 48
    wgqT = C.bufs["wgqT"]
    wuqT = C.bufs["wuqT"]
    s1bc = C.bufs["s1bc"]
    r1bc = C.bufs["r1bc"]
    hgm = C.bufs["hgm"]
    NI = 2816
    ft = C.sb("fb_t", [128, D], F32)
    xqfb = C.sb("fb_xqb", [128, D], BF16)
    xqfT = C.sb("fb_xqT", [128, DC * 512], BF16)
    gchf = C.bufs["gch"]
    uchf = C.bufs["uch"]
    fmx = C.sb("fb_fmx", [128, 12], F32)
    half = C.sb("fb_half", [1, 2], F32)
    idb = C.bufs["ident_bf"][:, :]

    for fc in range(12):
        for lt in range(4):
            i = fc * 4 + lt

            def _y(y, i=i):
                C.barrier_all(y)
                C.inc("LD", y.dma_start(out=ft[:, :],
                                        in_=x0f[i * 128:(i + 1) * 128, :]), 16)
            C.Y(_y)

            def _v(v, i=i):
                C.waits(v, "LD", "TW")
                mu = fstats[:, 2 * i:2 * i + 1]
                rstd = fstats[:, 2 * i + 1:2 * i + 2]
                C.edge(v, "VW", v.tensor_scalar(ft[:, :], ft[:, :], mu, None,
                                                ALU.subtract))
                C.edge(v, "VW", v.tensor_scalar(ft[:, :], ft[:, :], rstd,
                                                None, ALU.mult))
                C.edge(v, "VW", v.tensor_scalar(ft[:, :], ft[:, :],
                                                r1bc[:, 0:1], None, ALU.mult))
                C.edge(v, "VW", v.tensor_scalar(ft[:, :], ft[:, :], MAGIC,
                                                -MAGIC, ALU.add, ALU.add))
                C.edge(v, "VW", v.tensor_scalar(ft[:, :], ft[:, :], QMAX,
                                                -QMAX - 1.0, ALU.min, ALU.max))
                C.inc("VW", v.tensor_scalar(xqfb[:, :], ft[:, :], s1bc[:, 0:1],
                                            None, ALU.mult))
            C.V(_v)

            pairs = [(xqfb[:, dc * 128:(dc + 1) * 128],
                      xqfT[:, dc * 512 + lt * 128: dc * 512 + (lt + 1) * 128])
                     for dc in range(DC)]
            emit_transposes(C, pairs, idb)

        fb_evac = [0, 0]
        fb_gtw = [0, 0]
        fb_vw = [0]
        for ic in range(NI // 128):
            for wi, (wT, dstn) in enumerate(((wgqT, "gch"), (wuqT, "uch"))):
                ps = C.bufs[f"pf{wi}"]

                def _t(t, ic=ic, wi=wi, wT=wT, ps=ps):
                    if ic == 0 and wi == 0:
                        C.barrier_all(t)
                    elif fb_evac[wi]:
                        t.wait_ge(C.s["SW"].h, fb_evac[wi])
                    for hc in range(DC):
                        mm = t.matmul(
                            ps[:, 0:512],
                            wT[:, hc * NI + ic * 128: hc * NI + (ic + 1) * 128],
                            xqfT[:, hc * 512:(hc + 1) * 512],
                            start=(hc == 0), stop=(hc == DC - 1))
                    C.inc("TW", mm)
                    fb_gtw[wi] = C.s["TW"].n
                C.T(_t)

                def _s(s, wi=wi, ps=ps, dstn=dstn):
                    s.wait_ge(C.s["TW"].h, fb_gtw[wi])
                    if fb_vw[0]:
                        s.wait_ge(C.s["VW"].h, fb_vw[0])
                    C.inc("SW", s.tensor_copy(C.bufs[dstn][:, 0:512],
                                              ps[:, 0:512]))
                    fb_evac[wi] = C.s["SW"].n
                C.V(_s)

            def _s2(s):
                C.waits(s, "SW")
                C.inc("SW", s.activation(gchf[:, 0:512], gchf[:, 0:512],
                                         AF.Silu))
            C.S(_s2)

            def _v2(v, fc=fc, ic=ic):
                C.waits(v, "SW")
                C.edge(v, "VW", v.tensor_mul(gchf[:, 0:512], gchf[:, 0:512],
                                             uchf[:, 0:512]))
                C.edge(v, "VW", v.reduce_max(uchf[:, 0:1], gchf[:, 0:512],
                                             AX.X, apply_absolute_value=True))
                if ic == 0:
                    C.inc("VW", v.tensor_copy(fmx[:, fc:fc + 1],
                                              uchf[:, 0:1]))
                else:
                    C.inc("VW", v.tensor_max(fmx[:, fc:fc + 1],
                                             fmx[:, fc:fc + 1],
                                             uchf[:, 0:1]))
                fb_vw[0] = C.s["VW"].n
            C.V(_v2)

    emit_absmax_tiles(C, [fmx[:, 0:12]], half[0:1, 0:1])

    def _vc(v):
        C.barrier_all(v)
        C.inc("VW", v.tensor_max(half[0:1, 0:1], half[0:1, 0:1],
                                 hgm[0:1, 0:1]))
    C.V(_vc)

    def _yh(y):
        C.waits(y, "VW")
        C.inc("ST", y.dma_start(out=C.bufs["scratch1"].ap()[0:1, :],
                                in_=half[0:1, 0:1]), 16)
    C.Y(_yh)
    emit_pair_send(C, 3, 3, [(C.bufs["xMX2"], 1, 0, 1,
                              C.bufs["scratch1"].ap()[0:1, :])])
    emit_pair_recv(C, 3, 3, [(half[0:1, 1:2], C.bufs["xMX2"], 1, 0, 1)])

    def _vg(v):
        C.waits(v, "GD", "GW")
        C.edge(v, "VW", v.tensor_max(half[0:1, 0:1], half[0:1, 0:1],
                                     half[0:1, 1:2]))
        for i in range(8):
            mm = v.tensor_copy(C.bufs["fb_gm8"][0:1, i:i + 1], half[0:1, 0:1])
        C.inc("VW", mm)
    C.V(_vg)

    def _yo(y):
        C.waits(y, "VW")
        C.inc("ST", y.dma_start(out=hmaxesD, in_=C.bufs["fb_gm8"][0:1, :]), 16)
    C.Y(_yo)


def build_fused():
    C = Ctx("FUSED").start()
    nc = C.nc
    E = {}

    def ein(name, shape, dt):
        E[name] = nc.dram_tensor("i_" + name, list(shape), dt,
                                 kind="ExternalInput").ap()
        return E[name]

    ein("x0", [T, D], F32)
    ein("x0f", [6144, D], F32)
    for nm, shp in (("Wq0", [D, D]), ("Wk0", [D, D]), ("Wv0", [D, D]),
                    ("Wo0", [D, D])):
        ein(nm, shp, F32)
    for nm in ("bq0", "bk0", "bv0", "bo0"):
        ein(nm, [1, D], F32)
    ein("gamma", [1, NH], F32)
    ein("idx", [1, T], F32)
    ein("clipx", [1, 1], F32)
    ein("clipd", [1, 1], F32)
    ein("wgate", [2816, D], F32)
    ein("wup", [2816, D], F32)
    ein("wdown", [D, 2816], F32)
    ein("Wq1", [D, D], F32)
    ein("Wk1", [D, D], F32)
    ein("bq1", [1, D], F32)
    ein("bk1", [1, D], F32)
    for l in range(2):
        ein(f"wqkv{l}", [D, 3 * INNER], F32)
        ein(f"qs{l}", [1, HD], F32)
        ein(f"ks{l}", [1, HD], F32)
        ein(f"wo{l}", [INNER, D], F32)
        ein(f"lg{l}", [1, D], F32)
        ein(f"lb{l}", [1, D], F32)
        ein(f"w1_{l}", [D, 2 * HID], F32)
        ein(f"b1_{l}", [1, 2 * HID], F32)
        ein(f"w2_{l}", [HID, D], F32)
        ein(f"b2_{l}", [1, D], F32)
    ein("tR1", [T, INNER], F32)
    ein("tR2", [T, INNER], F32)
    ein("tR3", [T, INNER], F32)
    ein("tR4", [T, INNER], F32)
    ein("tR3t", [WIN, INNER], F32)
    ein("tR4t", [WIN, INNER], F32)
    ein("mask0", [1024, WIN], BF16)
    ein("maskr", [1024, WIN], BF16)
    ein("myh", [1, 1], I32)
    ein("prh", [1, 1], I32)
    ein("useS", [1, 1], F32)

    o_out = nc.dram_tensor("o_out", [T, D], F32, kind="ExternalOutput").ap()

    def idr(name, shape, dt):
        if name in DBGF:
            return nc.dram_tensor("o_" + name + "D", list(shape), dt,
                                  kind="ExternalOutput").ap()
        return nc.dram_tensor(name + "D", list(shape), dt).ap()

    xnD = idr("xn", [T, D], F32)
    maxxD = idr("maxx", [1, 1], F32)
    SD = idr("S", [D, HD], F32)
    SprevD = idr("Sprev", [D, HD], F32)
    maxesD = idr("maxes", [1, 8], F32)
    a0D = idr("a0", [T, D], F32)
    xqD = idr("xq", [T, D], BF16)
    hmaxD = idr("hmax", [1, 1], F32)
    hmaxesD = idr("hmaxes", [1, 8], F32)
    x1D = idr("x1", [T, D], F32)
    xn1D = idr("xn1", [T, D], F32)
    k1D = idr("k1", [T, D], BF16)
    k1pD = idr("k1p", [T, D], BF16)
    a1D = idr("a1", [T, D], F32)
    zpD = idr("zp", [T, D], F32)
    z1D = idr("z1", [T, D], F32)
    zp1D = idr("zp1", [T, D], F32)
    ztail0D = idr("ztail0", [640, D], F32)
    ztail1D = idr("ztail1", [640, D], F32)
    zpt0D = idr("zpt0", [1, D], F32)
    zpt1D = idr("zpt1", [1, D], F32)

    C.bufs["xflg_dram"] = nc.dram_tensor("xflg", [2, 64], I32,
                                         addr_space="Shared")
    C.bufs["xMX1"] = nc.dram_tensor("xMX1", [2, 1], F32, addr_space="Shared")
    C.bufs["xMX2"] = nc.dram_tensor("xMX2", [2, 1], F32, addr_space="Shared")
    xS = nc.dram_tensor("xS", [2 * D, HD], F32, addr_space="Shared")
    xK = nc.dram_tensor("xK", [2 * T, D], BF16, addr_space="Shared")
    xT1 = nc.dram_tensor("xT1", [2 * 513, D], F32, addr_space="Shared")
    xZ1 = nc.dram_tensor("xZ1", [2 * 1, D], F32, addr_space="Shared")
    xT2 = nc.dram_tensor("xT2", [2 * 513, D], F32, addr_space="Shared")
    xZ2 = nc.dram_tensor("xZ2", [2 * 1, D], F32, addr_space="Shared")

    C.bufs["myh_sb"] = C.sb("myh_sb", [1, 1], I32)
    C.bufs["prh_sb"] = C.sb("prh_sb", [1, 1], I32)
    C.bufs["pollb"] = C.sb("pollb", [1, 2], I32)
    C.bufs["gens_sb"] = C.sb("gens_sb", [1, 64], I32)
    C.bufs["scratch1"] = nc.dram_tensor("scr1D", [1, 1], F32)
    fstats = C.sb("fstats", [128, 96], F32)
    C.bufs["fb_gm8"] = C.sb("fb_gm8", [1, 8], F32)
    zerot = C.sb("zerot", [128, D], F32)

    def _y0(y):
        C.inc("LD", y.dma_start(out=C.bufs["myh_sb"][:, :], in_=E["myh"]), 16)
        C.inc("LD", y.dma_start(out=C.bufs["prh_sb"][:, :], in_=E["prh"]), 16)
    C.Y(_y0)

    def _g0(g):
        C.inc("GW", g.iota(C.bufs["gens_sb"][:, :], [[1, 64]],
                           channel_multiplier=0,
                           allow_small_or_imprecise_dtypes=True))
    C.G(_g0)

    def _vz(v):
        C.inc("VW", v.memset(zerot[:, :], 0.0))
    C.V(_vz)

    def _yz(y):
        C.waits(y, "VW")
        C.inc("ST", y.dma_start(out=ztail0D[0:127, :], in_=zerot[0:127, :]), 16)
        C.inc("ST", y.dma_start(out=ztail1D[0:127, :], in_=zerot[0:127, :]), 16)
        y.wait_ge(C.s["ST"].h, C.s["ST"].n)
    C.Y(_yz)
    C.free("zerot")

    C.section("A")
    mk = C.push()
    C.io = {"x0": E["x0"], "Wk": E["Wk0"], "bk": E["bk0"], "Wv": E["Wv0"],
            "bv": E["bv0"], "xn": xnD, "maxx": maxxD, "S": SD}
    build_A(C)
    emit_pair_send(C, 1, 1, [(xS, D, 0, D, SD)])
    emit_FA(C, E["x0f"], fstats, C.bufs["gmax"], maxesD)
    C.pop(mk)

    emit_pair_recv(C, 1, 1, [(SprevD, xS, D, 0, D)])

    C.section("B")
    mk = C.push()
    C.io = {"xn": xnD, "Sprev": SprevD, "maxes": maxesD, "clipx": E["clipx"],
            "idx": E["idx"], "gamma": E["gamma"], "Wq": E["Wq0"],
            "bq": E["bq0"], "Wk": E["Wk0"], "bk": E["bk0"], "Wv": E["Wv0"],
            "bv": E["bv0"], "Wo": E["Wo0"], "bo": E["bo0"],
            "wgate": E["wgate"], "wup": E["wup"], "useSc": E["useS"],
            "a0": a0D, "xq": xqD, "hmax": hmaxD}
    build_B(C)
    emit_FB(C, E["x0f"], fstats, hmaxesD)
    C.pop(mk)

    C.section("C")
    mk = C.push()
    C.io = {"x0": E["x0"], "a0": a0D, "xq": xqD, "hmaxes": hmaxesD,
            "clipd": E["clipd"], "wgate": E["wgate"], "wup": E["wup"],
            "wdown": E["wdown"], "Wk1": E["Wk1"], "bk1": E["bk1"],
            "x1": x1D, "xn1": xn1D, "k1": k1D}
    build_C(C)
    C.pop(mk)

    emit_pair_send(C, 4, 4, [(xK, T, 0, T, k1D)])
    emit_pair_send(C, 5, 5, [(xT1, 513, 0, 513, xn1D[1535:2048, :])])
    emit_pair_recv(C, 4, 4, [(k1pD, xK, T, 0, T)])
    emit_pair_recv(C, 5, 5, [(ztail0D[127:640, :], xT1, 513, 0, 513)])

    C.section("D")
    mk = C.push()
    C.io = {"xn1": xn1D, "k_own": k1D, "k_part": k1pD, "first": maxxD,
            "Wq1": E["Wq1"], "bq1": E["bq1"], "a1": a1D}
    build_D(C)
    C.pop(mk)

    C.section("E1a")
    mk = C.push()
    C.io = {"z": xn1D, "ztail": ztail0D, "wqkv": E["wqkv0"], "qs": E["qs0"],
            "ks": E["ks0"], "wo": E["wo0"], "tR1": E["tR1"], "tR2": E["tR2"],
            "tR3": E["tR3"], "tR4": E["tR4"], "tR3t": E["tR3t"],
            "tR4t": E["tR4t"], "mask0": E["mask0"], "maskr": E["maskr"],
            "useSc": E["useS"], "zp": zpD}
    build_E1(C)
    C.pop(mk)
    emit_pair_send(C, 6, 6, [(xZ1, 1, 0, 1, zpD[2047:2048, :])])
    emit_pair_recv(C, 6, 6, [(zpt0D, xZ1, 1, 0, 1)])

    C.section("E2a")
    mk = C.push()
    C.io = {"zp": zpD, "zptail": zpt0D, "lg": E["lg0"], "lb": E["lb0"],
            "w1": E["w1_0"], "b1": E["b1_0"], "w2": E["w2_0"],
            "b2": E["b2_0"], "useSc": E["useS"], "z": z1D}
    build_E2(False, C)
    C.pop(mk)
    emit_pair_send(C, 7, 7, [(xT2, 513, 0, 513, z1D[1535:2048, :])])
    emit_pair_recv(C, 7, 7, [(ztail1D[127:640, :], xT2, 513, 0, 513)])

    C.section("F1a")
    mk = C.push()
    C.io = {"z": z1D, "ztail": ztail1D, "wqkv": E["wqkv1"], "qs": E["qs1"],
            "ks": E["ks1"], "wo": E["wo1"], "tR1": E["tR1"], "tR2": E["tR2"],
            "tR3": E["tR3"], "tR4": E["tR4"], "tR3t": E["tR3t"],
            "tR4t": E["tR4t"], "mask0": E["mask0"], "maskr": E["maskr"],
            "useSc": E["useS"], "zp": zp1D}
    build_E1(C)
    C.pop(mk)
    emit_pair_send(C, 8, 8, [(xZ2, 1, 0, 1, zp1D[2047:2048, :])])
    emit_pair_recv(C, 8, 8, [(zpt1D, xZ2, 1, 0, 1)])

    C.section("F2a")
    mk = C.push()
    C.io = {"zp": zp1D, "zptail": zpt1D, "lg": E["lg1"], "lb": E["lb1"],
            "w1": E["w1_1"], "b1": E["b1_1"], "w2": E["w2_1"],
            "b2": E["b2_1"], "useSc": E["useS"], "x1": x1D, "a1": a1D,
            "z": o_out}
    build_E2(True, C)
    C.pop(mk)

    C.io = {}
    C.section("")

    def _yend(y):
        y.wait_ge(C.s["ST"].h, C.s["ST"].n)
    C.Y(_yend)
    C.close()
    return C


FUSED = True


def kernel_fused(inputs):
    C = build_fused()
    im = []
    for c in range(8):
        b, h = c // 2, c % 2
        xs = shard_x(inputs["L0_attn_x"])
        bb = [(b + 1) % 4, (b + 2) % 4, (b + 3) % 4]
        x0 = inputs["L0_attn_x"]
        if h == 0:
            x0f = np.concatenate([x0[bb[0]], x0[bb[1]][:T]], 0)
        else:
            x0f = np.concatenate([x0[bb[1]][T:], x0[bb[2]]], 0)
        pos = np.arange(h * T, (h + 1) * T)
        tpos = np.arange(1536, 2048) if h == 1 else np.arange(512)
        q1, q2 = _rot_tables(pos, "q")
        k1, k2 = _rot_tables(pos, "k")
        kt1, kt2 = _rot_tables(tpos, "k")
        m = {
            "i_x0": xs[c],
            "i_x0f": np.ascontiguousarray(x0f),
            "i_Wq0": inputs["L0_attn_Wq"], "i_bq0": inputs["L0_attn_bq"].reshape(1, D),
            "i_Wk0": inputs["L0_attn_Wk"], "i_bk0": inputs["L0_attn_bk"].reshape(1, D),
            "i_Wv0": inputs["L0_attn_Wv"], "i_bv0": inputs["L0_attn_bv"].reshape(1, D),
            "i_Wo0": inputs["L0_attn_Wo"], "i_bo0": inputs["L0_attn_bo"].reshape(1, D),
            "i_gamma": inputs["L0_attn_gamma"].reshape(1, NH),
            "i_idx": np.arange(h * T, (h + 1) * T, dtype=np.float32).reshape(1, T),
            "i_clipx": inputs["L0_ffn_clip_x"].reshape(1, 1),
            "i_clipd": inputs["L0_ffn_clip_down"].reshape(1, 1),
            "i_wgate": inputs["L0_ffn_w_gate"],
            "i_wup": inputs["L0_ffn_w_up"],
            "i_wdown": inputs["L0_ffn_w_down"],
            "i_Wq1": inputs["L1_attn_Wq"], "i_bq1": inputs["L1_attn_bq"].reshape(1, D),
            "i_Wk1": inputs["L1_attn_Wk"], "i_bk1": inputs["L1_attn_bk"].reshape(1, D),
            "i_tR1": q1, "i_tR2": q2, "i_tR3": k1, "i_tR4": k2,
            "i_tR3t": kt1, "i_tR4t": kt2,
            "i_mask0": _inner_masks(h == 0).astype(ml_dtypes.bfloat16),
            "i_maskr": _inner_masks(False).astype(ml_dtypes.bfloat16),
            "i_myh": np.array([[h]], np.int32),
            "i_prh": np.array([[1 - h]], np.int32),
            "i_useS": np.array([[float(h)]], np.float32),
        }
        for l in range(2):
            m[f"i_wqkv{l}"] = np.ascontiguousarray(inputs["L1_ffn_wqkv"][l])
            m[f"i_qs{l}"] = inputs["L1_ffn_q_scale"][l].reshape(1, HD)
            m[f"i_ks{l}"] = inputs["L1_ffn_k_scale"][l].reshape(1, HD)
            m[f"i_wo{l}"] = np.ascontiguousarray(inputs["L1_ffn_wo"][l])
            m[f"i_lg{l}"] = inputs["L1_ffn_ln_g"][l].reshape(1, D)
            m[f"i_lb{l}"] = inputs["L1_ffn_ln_b"][l].reshape(1, D)
            m[f"i_w1_{l}"] = np.ascontiguousarray(inputs["L1_ffn_w1"][l])
            m[f"i_b1_{l}"] = inputs["L1_ffn_b1"][l].reshape(1, 2 * HID)
            m[f"i_w2_{l}"] = np.ascontiguousarray(inputs["L1_ffn_w2"][l])
            m[f"i_b2_{l}"] = inputs["L1_ffn_b2"][l].reshape(1, D)
        im.append(m)
    res = run_bass_kernel_spmd(C.nc, im, core_ids=list(range(8)))
    return unshard([res.results[c]["o_out"] for c in range(8)])


def unshard(shards):
    out = np.empty((4, 2 * T, D), np.float32)
    for b in range(4):
        for h in range(2):
            out[b, h * T:(h + 1) * T, :] = shards[2 * b + h]
    return out


def kernel(**inputs):
    if FUSED:
        return kernel_fused(inputs)
    resA = run_A(inputs)
    resB = run_B(inputs, resA)
    resC = run_C(inputs, resB)
    resD = run_D(inputs, resC)
    xn1_sh = [resC[c]["o_xn1"] for c in range(8)]
    zp0 = run_E1(inputs, 0, xn1_sh)
    zp0_sh = [zp0[c]["o_zp"] for c in range(8)]
    z1 = run_E2(inputs, 0, zp0_sh)
    z1_sh = [z1[c]["o_z"] for c in range(8)]
    zp1 = run_E1(inputs, 1, z1_sh)
    zp1_sh = [zp1[c]["o_zp"] for c in range(8)]
    fin = run_E2(inputs, 1, zp1_sh, final=True,
                 x1_shards=[resC[c]["o_x1"] for c in range(8)],
                 a1_shards=[resD[c]["o_a1"] for c in range(8)])
    return unshard([fin[c]["o_z"] for c in range(8)])


# version 10 (final)
# speedup vs baseline: 1.6857x; 1.6857x over previous
import contextlib
import math

import numpy as np
import ml_dtypes

import concourse.bass as bass
import concourse.mybir as mybir
from concourse.bass_utils import run_bass_kernel_spmd

F32 = mybir.dt.float32
BF16 = mybir.dt.bfloat16
I32 = mybir.dt.int32
AF = mybir.ActivationFunctionType
ALU = mybir.AluOpType
AX = mybir.AxisListType

DBG_D = False
DBGF = []
E1_STOP = 0
T = 2048
NT = T // 128
D = 1024
DC = D // 128
NH = 16
HD = 64
MAGIC = 12582912.0
QMAX = 127.0


class Sem:
    def __init__(self, h):
        self.h = h
        self.n = 0


class Ctx:
    def __init__(self, name):
        self.nc = bass.Bass("TRN2", target_bir_lowering=False)
        self.st = contextlib.ExitStack()
        self.name = name
        self.sec = ""
        self.io = {}
        self.s = {}
        self.bufs = {}
        self._scopes = {}
        self._local = {}
        self._alloc_log = []

    def section(self, sec):
        self.sec = sec

    def push(self):
        return len(self._alloc_log)

    def pop(self, mark):
        for nm in reversed(self._alloc_log[mark:]):
            if nm in self._scopes:
                self.free_u(nm)
        del self._alloc_log[mark:]

    def start(self):
        for nm in ["LD", "GD", "TW", "VW", "SW", "GW", "ST", "PL"]:
            self.s[nm] = Sem(self.st.enter_context(self.nc.semaphore(nm)))
        pfA = self.ps("pfA", [128, 2048], F32)
        for i in range(4):
            self.bufs[f"pf{i}"] = pfA.ap()[:, i * 512:(i + 1) * 512]
        for i in range(4, 6):
            self.ps(f"pf{i}", [128, 512], F32)
        for i in range(2):
            self.ps(f"pb{i}", [128, 1024], BF16)
        self.blk = self.st.enter_context(self.nc.Block())
        return self

    def close(self):
        self.st.close()

    def sb(self, name, shape, dt, side=None):
        es = contextlib.ExitStack()
        uname = f"{self.sec}_{name}" if self.sec else name
        t = es.enter_context(self.nc.sbuf_tensor(uname, shape, dt, side=side))
        self.bufs[name] = t
        self._local[name] = uname
        self._scopes[uname] = es
        self._alloc_log.append(uname)
        return t

    def sbc(self, name, shape, dt, side=None):
        if name in self.bufs:
            return self.bufs[name]
        return self.sb(name, shape, dt, side=side)

    def free(self, *names):
        for n in names:
            un = self._local.get(n, n) if n in self._local or n in self._scopes \
                else n
            if un not in self._scopes and n in self._scopes:
                un = n
            self._scopes.pop(un).close()
            self.bufs.pop(n, None)

    def free_u(self, uname):
        self._scopes.pop(uname).close()
        for k, v in list(self._local.items()):
            if v == uname:
                self._local.pop(k)
                self.bufs.pop(k, None)

    def ps(self, name, shape, dt=F32):
        t = self.st.enter_context(self.nc.psum_tensor(name, shape, dt))
        self.bufs[name] = t
        return t

    def din(self, name, shape, dt):
        if name in self.io:
            return self.io[name]
        return self.nc.dram_tensor("i_" + name, list(shape), dt,
                                   kind="ExternalInput").ap()

    def dout(self, name, shape, dt):
        if name in self.io:
            return self.io[name]
        return self.nc.dram_tensor("o_" + name, list(shape), dt,
                                   kind="ExternalOutput").ap()

    def T(self, f):
        self.blk.tensor(f)

    def V(self, f):
        self.blk.vector(f)

    def S(self, f):
        self.blk.scalar(f)

    def G(self, f):
        self.blk.gpsimd(f)

    def Y(self, f):
        self.blk.sync(f)

    def inc(self, sem, instr, n=1):
        instr.then_inc(self.s[sem].h, n)
        self.s[sem].n += n
        return instr

    def edge(self, eng, sem, instr, n=1):
        self.inc(sem, instr, n)
        eng.wait_ge(self.s[sem].h, self.s[sem].n)
        return instr

    def waits(self, eng, *sems):
        for nm in sems:
            if self.s[nm].n > 0:
                eng.wait_ge(self.s[nm].h, self.s[nm].n)

    def barrier_load(self, eng):
        self.waits(eng, "LD", "GD")

    def barrier_all(self, eng):
        self.waits(eng, "LD", "GD", "TW", "VW", "SW", "GW", "ST")


def emit_identity(C):
    if "ident_bf" in C.bufs:
        return C.bufs["ident_bf"][:, :], C.bufs["ident_f32"][:, :]
    sec = C.sec
    C.sec = ""
    cols = C.sb("id_cols", [128, 128], F32)
    rowv = C.sb("id_rowv", [128, 1], F32)
    idb = C.sb("ident_bf", [128, 128], BF16)
    idf = C.sb("ident_f32", [128, 128], F32)
    C.bufs["ident_bf"] = idb
    C.bufs["ident_f32"] = idf

    def _g(g):
        g.iota(cols[:], [[1, 128]], channel_multiplier=0,
               allow_small_or_imprecise_dtypes=True)
        C.inc("GW", g.iota(rowv[:], [[1, 1]], channel_multiplier=1,
                           allow_small_or_imprecise_dtypes=True))
    C.G(_g)

    def _v(v):
        C.waits(v, "GW")
        v.tensor_scalar(idb[:], cols[:], rowv[:, 0:1], None, ALU.is_equal)
        C.inc("VW", v.tensor_scalar(idf[:], cols[:], rowv[:, 0:1], None,
                                    ALU.is_equal))
    C.V(_v)
    C.sec = sec
    for nm in ("id_cols", "id_rowv", "ident_bf", "ident_f32"):
        if nm in C._alloc_log:
            C._alloc_log.remove(nm)
    return idb[:, :], idf[:, :]


def emit_transposes(C, pairs, idb, ps_name="pb", dt=BF16):
    nbank = 2
    pss = [C.bufs[f"{ps_name}{i}"] for i in range(nbank)]

    def _t(t):
        C.barrier_all(t)
        for i, (src, dst) in enumerate(pairs):
            if i >= nbank:
                t.wait_ge(C.s["VW"].h, C.s["VW"].n + (i - nbank + 1))
            C.inc("TW", t.transpose(pss[i % nbank][0:src.shape[-1], 0:src.shape[0]],
                                    src, idb))
    C.T(_t)

    def _v(v):
        base = C.s["TW"].n - len(pairs)
        for i, (src, dst) in enumerate(pairs):
            v.wait_ge(C.s["TW"].h, base + i + 1)
            C.inc("VW", v.tensor_copy(dst, pss[i % nbank][0:src.shape[-1],
                                                          0:src.shape[0]]))
    C.V(_v)


def emit_ln_tiles(C, x_src_tiles, xn_dst_tiles, stats_dst, eps=1e-5):
    n = len(x_src_tiles)
    rs = C.sbc("ln_rs", [128, 1], F32)
    sq = C.sbc("ln_sq", [128, 1024], F32)
    ssq = C.sbc("ln_ssq", [128, 1], F32)
    var = C.sbc("ln_var", [128, 1], F32)
    std = C.sbc("ln_std", [128, 1], F32)

    for i in range(n):
        x = x_src_tiles[i]
        xn = xn_dst_tiles[i]
        mu = stats_dst[:, 2 * i:2 * i + 1]
        rstd = stats_dst[:, 2 * i + 1:2 * i + 2]

        def _v(v, x=x, xn=xn, mu=mu, rstd=rstd):
            C.barrier_all(v)
            C.edge(v, "VW", v.reduce_sum(rs[:], x, AX.X))
            C.edge(v, "VW", v.tensor_scalar(mu[:], rs[:], 1.0 / 1024.0, None,
                                            ALU.mult))
            C.inc("VW", v.tensor_scalar(xn, x, mu[:, 0:1], None, ALU.subtract))
        C.V(_v)

        def _s(s, xn=xn):
            C.waits(s, "VW")
            C.inc("SW", s.activation(sq[:], xn, AF.Square,
                                     accum_out=ssq[:, 0:1]))
        C.S(_s)

        def _v2(v, xn=xn, rstd=rstd):
            C.waits(v, "SW")
            C.inc("VW", v.tensor_scalar(var[:], ssq[:], 1.0 / 1024.0, eps,
                                        ALU.mult, ALU.add))
        C.V(_v2)

        def _s2(s):
            C.waits(s, "VW")
            C.inc("SW", s.activation(std[:], var[:], AF.Sqrt))
        C.S(_s2)

        def _v3(v, xn=xn, rstd=rstd):
            C.waits(v, "SW")
            C.edge(v, "VW", v.reciprocal(rstd[:], std[:]))
            C.inc("VW", v.tensor_scalar(xn, xn, rstd[:, 0:1], None, ALU.mult))
        C.V(_v3)


_AM_CNT = [0]


def emit_absmax_tiles(C, tiles, out11):
    n = len(tiles)
    _AM_CNT[0] += 1
    u = _AM_CNT[0]
    pm = C.sb(f"am_part{u}", [128, max(n, 1)], F32)
    pmo = C.sb(f"am_parto{u}", [128, 1], F32)
    pmt = C.bufs["pf4"]
    row = C.sb(f"am_row{u}", [1, 128], F32)
    idf = C.bufs["ident_f32"][:, :]

    def _v(v):
        C.barrier_all(v)
        for i, tl in enumerate(tiles):
            v.reduce_max(pm[:, i:i + 1], tl, AX.X, apply_absolute_value=True)
        C.edge(v, "VW", v.engine_nop())
        C.inc("VW", v.reduce_max(pmo[:, 0:1], pm[:, 0:n], AX.X))
    C.V(_v)

    def _t(t):
        C.waits(t, "VW")
        C.inc("TW", t.transpose(pmt[0:1, 0:128], pmo[:, 0:1], idf))
    C.T(_t)

    def _v2(v):
        C.waits(v, "TW")
        C.edge(v, "VW", v.tensor_copy(row[:], pmt[0:1, 0:128]))
        C.inc("VW", v.reduce_max(out11, row[0:1, :], AX.X,
                                 apply_absolute_value=True))
    C.V(_v2)


def load_weight_T(C, name, w_ap, d_out, d_in, idb, keep_nat=False, side=None):
    ncd = d_in // 128
    nco = d_out // 128
    wt = C.sb(name + "_T", [128, ncd * d_out], BF16, side=side)
    nat = C.sb(name + "_nat", [128, nco * d_in], BF16, side="right")

    def _g(g):
        C.barrier_all(g)
        for oc in range(nco):
            C.inc("GD", g.dma_start(
                out=nat[:, oc * d_in:(oc + 1) * d_in],
                in_=w_ap[oc * 128:(oc + 1) * 128, :]), 16)
    C.G(_g)

    pairs = []
    for oc in range(nco):
        for dc in range(ncd):
            src = nat[:, oc * d_in + dc * 128: oc * d_in + (dc + 1) * 128]
            dst = wt[:, dc * d_out + oc * 128: dc * d_out + (oc + 1) * 128]
            pairs.append((src, dst))
    emit_transposes(C, pairs, idb)
    if not keep_nat:
        C.free(name + "_nat")
        nat = None
    return wt, nat


def emit_bias_cols(C, b_ap, n, name):
    ncb = n // 128
    row = C.sb(name + "_row", [1, n], F32, side="right")
    col = C.sb(name + "_col", [128, ncb], F32)
    idf = C.bufs["ident_f32"][:, :]
    pst = C.bufs["pf4"]

    def _y(y):
        C.barrier_all(y)
        C.inc("LD", y.dma_start(out=row[:], in_=b_ap.rearrange("n -> 1 n") if len(b_ap.shape) == 1 else b_ap), 16)
    C.Y(_y)

    for c in range(ncb):
        def _t(t, c=c):
            C.barrier_all(t)
            C.inc("TW", t.transpose(pst[0:128, 0:1], row[0:1, c * 128:(c + 1) * 128], idf[0:1, 0:1]))
        C.T(_t)

        def _v(v, c=c):
            C.waits(v, "TW")
            C.inc("VW", v.tensor_copy(col[:, c:c + 1], pst[0:128, 0:1]))
        C.V(_v)
    C.free(name + "_row")
    return col, None


def emit_proj_token_tile(C, ps4, xnT_lhs_tile_views, wT, brow, ones1, dst_bf16,
                         d_in=D):
    ncd = d_in // 128

    def _t(t):
        C.barrier_all(t)
        for half in range(2):
            ps = ps4[half]
            for dc in range(ncd):
                t.matmul(ps[:, 0:512], xnT_lhs_tile_views[dc],
                         wT[:, dc * D + half * 512: dc * D + half * 512 + 512],
                         start=(dc == 0), stop=False)
            mm = t.matmul(ps[:, 0:512], ones1[0:1, 0:128],
                          brow[0:1, half * 512: half * 512 + 512],
                          start=False, stop=True)
        C.inc("TW", mm)
    C.T(_t)

    def _s(s):
        C.waits(s, "TW")
        s.copy(dst_bf16[:, 0:512], ps4[0][:, 0:512])
        C.inc("SW", s.copy(dst_bf16[:, 512:1024], ps4[1][:, 0:512]))
    C.S(_s)


def build_A(C=None):
    own = C is None
    if own:
        C = Ctx("A").start()
    x0 = C.din("x0", [T, D], F32)
    wk = C.din("Wk", [D, D], F32)
    bk = C.din("bk", [1, D], F32)
    wv = C.din("Wv", [D, D], F32)
    bv = C.din("bv", [1, D], F32)

    o_xn = C.dout("xn", [T, D], F32)
    o_max = C.dout("maxx", [1, 1], F32)
    o_S = C.dout("S", [D, HD], F32)

    idb, idf = emit_identity(C)

    xn = C.sb("xn", [128, NT * D], F32)
    stats = C.sb("stats", [128, NT * 2], F32)
    xnT = C.sb("xnT", [128, DC * T], BF16)

    def _y(y):
        for i in range(NT):
            C.inc("LD", y.dma_start(out=xn[:, i * D:(i + 1) * D],
                                    in_=x0[i * 128:(i + 1) * 128, :]), 16)
    C.Y(_y)

    tiles = [xn[:, i * D:(i + 1) * D] for i in range(NT)]
    emit_ln_tiles(C, tiles, tiles, stats)
    gmax = C.sb("gmax", [1, 1], F32)
    emit_absmax_tiles(C, tiles, gmax[0:1, 0:1])

    def _y2(y):
        C.barrier_all(y)
        C.inc("ST", y.dma_start(out=o_max, in_=gmax[0:1, 0:1]), 16)
        for i in range(NT):
            C.inc("ST", y.dma_start(out=o_xn[i * 128:(i + 1) * 128, :],
                                    in_=xn[:, i * D:(i + 1) * D]), 16)
    C.Y(_y2)

    xnb = C.sb("xnb", [128, NT * D], BF16, side="right")

    def _s(s):
        C.barrier_all(s)
        for i in range(NT):
            C.inc("SW", s.copy(xnb[:, i * D:(i + 1) * D],
                               xn[:, i * D:(i + 1) * D]))
    C.S(_s)

    pairs = []
    for i in range(NT):
        for dc in range(DC):
            src = xnb[:, i * D + dc * 128: i * D + (dc + 1) * 128]
            dst = xnT[:, dc * T + i * 128: dc * T + (i + 1) * 128]
            pairs.append((src, dst))
    emit_transposes(C, pairs, idb)
    C.free("xnb")

    wkT, _ = load_weight_T(C, "wk", wk, D, D, idb)
    wvT, _ = load_weight_T(C, "wv", wv, D, D, idb)

    browk = C.sb("brow_k", [1, D], F32)
    browv = C.sb("brow_v", [1, D], F32)
    ones1 = C.sb("ones1", [1, 128], F32)

    def _y3(y):
        C.inc("LD", y.dma_start(out=browk[:], in_=bk), 16)
        C.inc("LD", y.dma_start(out=browv[:], in_=bv), 16)
    C.Y(_y3)

    ktl = C.sb("ktl", [128, D], BF16)
    vtl = C.sb("vtl", [128, D], BF16)
    S = C.sb("Sacc", [128, 8 * 128], F32)
    psA, psB, psC, psD = (C.bufs[f"pf{i}"] for i in range(4))
    pS0, pS1 = C.bufs["pf5"], C.bufs["pf5"]

    def _v0(v):
        v.memset(ones1[:], 1.0)
        C.inc("VW", v.memset(S[:], 0.0))
    C.V(_v0)

    for tt in range(NT):
        xviews = [xnT[:, dc * T + tt * 128: dc * T + (tt + 1) * 128]
                  for dc in range(DC)]
        emit_proj_token_tile(C, [psA, psB], xviews, wkT, browk, ones1, ktl)
        emit_proj_token_tile(C, [psC, psD], xviews, wvT, browv, ones1, vtl)

        for hp in range(8):
            ps = pS0 if hp % 2 == 0 else pS1

            def _t(t, hp=hp, ps=ps):
                C.barrier_all(t)
                C.inc("TW", t.matmul(ps[:, 0:128],
                                     ktl[:, hp * 128:(hp + 1) * 128],
                                     vtl[:, hp * 128:(hp + 1) * 128],
                                     start=True, stop=True))
            C.T(_t)

            def _v(v, hp=hp, ps=ps):
                C.waits(v, "TW")
                sl = S[:, hp * 128:(hp + 1) * 128]
                C.inc("VW", v.tensor_add(sl, sl, ps[:, 0:128]))
            C.V(_v)

    def _y4(y):
        C.barrier_all(y)
        for hp in range(8):
            C.inc("ST", y.dma_start(
                out=o_S[(2 * hp) * 64:(2 * hp + 1) * 64, :],
                in_=S[0:64, hp * 128: hp * 128 + 64]), 16)
            C.inc("ST", y.dma_start(
                out=o_S[(2 * hp + 1) * 64:(2 * hp + 2) * 64, :],
                in_=S[64:128, hp * 128 + 64: hp * 128 + 128]), 16)
        y.wait_ge(C.s["ST"].h, C.s["ST"].n)
    C.Y(_y4)

    if own:
        C.close()
    return C


def emit_scalar_broadcast(C, src11, dst_col, ones_row):
    ps = C.bufs["pf4"]

    def _t(t):
        C.barrier_all(t)
        C.inc("TW", t.matmul(ps[:, 0:1], ones_row[0:1, 0:128], src11,
                             start=True, stop=True))
    C.T(_t)

    def _v(v):
        C.waits(v, "TW")
        C.inc("VW", v.tensor_copy(dst_col, ps[:, 0:1]))
    C.V(_v)


def build_B(C=None):
    own = C is None
    if own:
        C = Ctx("B").start()
    xn_in = C.din("xn", [T, D], F32)
    Sp = C.din("Sprev", [D, HD], F32)
    maxes = C.din("maxes", [1, 8], F32)
    clipx = C.din("clipx", [1, 1], F32)
    idx = C.din("idx", [1, T], F32)
    gam = C.din("gamma", [1, NH], F32)
    wq = C.din("Wq", [D, D], F32)
    bq = C.din("bq", [1, D], F32)
    wk = C.din("Wk", [D, D], F32)
    bk = C.din("bk", [1, D], F32)
    wv = C.din("Wv", [D, D], F32)
    bv = C.din("bv", [1, D], F32)
    wo = C.din("Wo", [D, D], F32)
    bo = C.din("bo", [1, D], F32)
    wgate = C.din("wgate", [2816, D], F32)
    wup = C.din("wup", [2816, D], F32)

    o_a0 = C.dout("a0", [T, D], F32)
    o_xq = C.dout("xq", [T, D], BF16)
    o_hmax = C.dout("hmax", [1, 1], F32)

    idb, idf = emit_identity(C)
    ones1 = C.sb("ones1", [1, 128], F32)
    trilm = C.sb("trilm", [128, 128], BF16)

    def _v0(v):
        C.inc("VW", v.memset(ones1[:], 1.0))
    C.V(_v0)

    def _g0(g):
        C.waits(g, "GW")
        C.inc("GW", g.tensor_scalar(trilm[:], C.bufs["id_cols"][:, :],
                                    C.bufs["id_rowv"][:, 0:1], None, ALU.is_ge))
    C.G(_g0)

    xnT = C.sb("xnT", [128, DC * T], BF16, side="right")
    qT = C.sb("qT", [128, DC * T], BF16, side="right")
    kT = C.sb("kT", [128, DC * T], BF16, side="right")
    vtok = C.sb("vtok", [128, NT * D], BF16, side="right")
    dec = C.sb("dec", [128, NT * NH], F32)
    Sacc = C.sb("Sacc", [128, 8 * 128], F32)
    Sbf = C.sb("Sbf", [128, 8 * 128], BF16)
    lgam = C.sb("lgam", [1, NH], F32)
    idxr = C.sb("idxr", [1, T], F32)
    gamr = C.sb("gamr", [1, NH], F32)

    xtile = C.sb("xtile", [128, D], F32)
    xbtile = C.sb("xbtile", [128, D], BF16)

    def _vS0(v):
        C.inc("VW", v.memset(Sacc[:], 0.0))
    C.V(_vS0)

    def _yl(y):
        C.inc("LD", y.dma_start(out=idxr[:], in_=idx), 16)
        C.inc("LD", y.dma_start(out=gamr[:], in_=gam), 16)
        C.waits(y, "VW")
        for hp in range(8):
            C.inc("LD", y.dma_start(
                out=Sacc[0:64, hp * 128: hp * 128 + 64],
                in_=Sp[(2 * hp) * 64:(2 * hp + 1) * 64, :]), 16)
            C.inc("LD", y.dma_start(
                out=Sacc[64:128, hp * 128 + 64: hp * 128 + 128],
                in_=Sp[(2 * hp + 1) * 64:(2 * hp + 2) * 64, :]), 16)
    C.Y(_yl)

    useScB = C.din("useSc", [1, 1], F32)
    usec = C.sb("usec", [128, 1], F32)
    urow = C.sb("urow", [1, 1], F32)

    def _yu(y):
        C.inc("LD", y.dma_start(out=urow[:], in_=useScB), 16)
    C.Y(_yu)
    emit_scalar_broadcast(C, urow[0:1, 0:1], usec[:, 0:1], ones1)

    def _vS(v):
        C.barrier_all(v)
        C.edge(v, "VW", v.tensor_scalar(Sacc[:], Sacc[:], usec[:, 0:1], None,
                                        ALU.mult))
        C.inc("VW", v.tensor_copy(Sbf[:], Sacc[:]))
    C.V(_vS)

    for tt in range(NT):
        def _y(y, tt=tt):
            C.waits(y, "VW", "ST", "GD")
            C.inc("LD", y.dma_start(out=xtile[:, :],
                                    in_=xn_in[tt * 128:(tt + 1) * 128, :]), 16)
        C.Y(_y)

        def _v(v, tt=tt):
            C.waits(v, "LD", "TW")
            C.inc("VW", v.tensor_copy(xbtile[:, :], xtile[:, :]))
        C.V(_v)

        pairs = []
        for dc in range(DC):
            src = xbtile[:, dc * 128:(dc + 1) * 128]
            dst = xnT[:, dc * T + tt * 128: dc * T + (tt + 1) * 128]
            pairs.append((src, dst))
        emit_transposes(C, pairs, idb)

    def _s0(s):
        C.barrier_load(s)
        C.inc("SW", s.activation(lgam[:], gamr[:], AF.Ln))
    C.S(_s0)

    for tt in range(NT):
        def _t(t, tt=tt):
            C.barrier_all(t)
            C.inc("TW", t.matmul(C.bufs["pf4"][:, 0:NH],
                                 idxr[0:1, tt * 128:(tt + 1) * 128],
                                 lgam[0:1, :], start=True, stop=True))
        C.T(_t)

        def _s(s, tt=tt):
            C.waits(s, "TW")
            C.inc("SW", s.activation(dec[:, tt * NH:(tt + 1) * NH],
                                     C.bufs["pf4"][:, 0:NH], AF.Exp))
        C.S(_s)

    browv = C.sb("brow_v", [1, D], F32)
    browo = C.sb("brow_o", [1, D], F32)

    def _yb(y):
        for brow, src in ((browv, bv), (browo, bo)):
            C.inc("LD", y.dma_start(out=brow[:], in_=src), 16)
    C.Y(_yb)

    bqc, _ = emit_bias_cols(C, bq, D, "bq")
    bkc, _ = emit_bias_cols(C, bk, D, "bk")

    for wname, w_ap, dstT, bcol in (("wq", wq, qT, bqc), ("wk", wk, kT, bkc)):
        wt, _ = load_weight_T(C, wname, w_ap, D, D, idb, side="right")
        pj_ev = [0, 0]
        pj_tw = [0, 0]
        for jc in range(DC):
            for tc in range(4):
                bk = tc % 2
                ps = C.bufs[f"pf{bk}"]

                def _t(t, jc=jc, tc=tc, bk=bk, ps=ps, wt=wt):
                    if jc == 0 and tc == 0:
                        C.barrier_all(t)
                    elif pj_ev[bk]:
                        t.wait_ge(C.s["VW"].h, pj_ev[bk])
                    for dc in range(DC):
                        mm = t.matmul(
                            ps[:, 0:512],
                            wt[:, dc * D + jc * 128:dc * D + (jc + 1) * 128],
                            xnT[:, dc * T + tc * 512:dc * T + (tc + 1) * 512],
                            start=(dc == 0), stop=(dc == DC - 1))
                    C.inc("TW", mm)
                    pj_tw[bk] = C.s["TW"].n
                C.T(_t)

                def _s(s, jc=jc, tc=tc, bk=bk, ps=ps, dstT=dstT, bcol=bcol):
                    s.wait_ge(C.s["TW"].h, pj_tw[bk])
                    C.inc("VW", s.tensor_scalar(
                        dstT[:, jc * T + tc * 512: jc * T + (tc + 1) * 512],
                        ps[:, 0:512], bcol[:, jc:jc + 1], None, ALU.add))
                    pj_ev[bk] = C.s["VW"].n
                C.V(_s)
        C.free(wname + "_T")

    wvT, _ = load_weight_T(C, "wv", wv, D, D, idb, side="right")
    for tt in range(NT):
        xviews = [xnT[:, dc * T + tt * 128: dc * T + (tt + 1) * 128]
                  for dc in range(DC)]
        emit_proj_token_tile(C, [C.bufs["pf0"], C.bufs["pf1"]], xviews, wvT,
                             browv, ones1, vtok[:, tt * D:(tt + 1) * D])
    C.free("wv_T")
    woT, _ = load_weight_T(C, "wo", wo, D, D, idb, side="right")

    atm = C.sb("atm", [128, 2 * 128], BF16)
    ktokt = C.sb("ktokt", [128, D], BF16)
    aotl = C.sb("aotl", [128, D], BF16)
    aoT = C.sb("aoT", [128, DC * 128], BF16)
    a0t = C.sb("a0t", [128, D], F32)

    for tt in range(NT):
        a_tw = [0] * NH
        m_vw = [0] * NH
        t2_tw = [0] * NH
        for h in range(NH):
            hp, r = h // 2, h % 2
            pout = C.bufs["pf0"] if h < 8 else C.bufs["pf1"]
            osl = pout[:, (h % 8) * 64:(h % 8) * 64 + 64]
            rows = slice(r * 64, r * 64 + 64)
            pa = C.bufs["pf4"] if h % 2 == 0 else C.bufs["pf5"]

            def _t(t, tt=tt, h=h, hp=hp, rows=rows, pa=pa):
                if h == 0:
                    C.barrier_all(t)
                elif h >= 2:
                    t.wait_ge(C.s["VW"].h, m_vw[h - 2])
                C.inc("TW", t.matmul(
                    pa[:, 0:128],
                    kT[rows, hp * T + tt * 128: hp * T + (tt + 1) * 128],
                    qT[rows, hp * T + tt * 128: hp * T + (tt + 1) * 128],
                    start=True, stop=True))
                a_tw[h] = C.s["TW"].n
            C.T(_t)

            def _v(v, h=h, pa=pa):
                v.wait_ge(C.s["TW"].h, a_tw[h])
                if h >= 2:
                    v.wait_ge(C.s["TW"].h, t2_tw[h - 2])
                C.inc("VW", v.tensor_tensor(
                    atm[:, (h % 2) * 128:(h % 2) * 128 + 128],
                    pa[:, 0:128], trilm[:], ALU.mult))
                m_vw[h] = C.s["VW"].n
            C.V(_v)

            def _t2(t, tt=tt, h=h, hp=hp, rows=rows, osl=osl):
                t.wait_ge(C.s["VW"].h, m_vw[h])
                t.matmul(osl, qT[rows, hp * T + tt * 128: hp * T + (tt + 1) * 128],
                         Sbf[rows, hp * 128 + r * 64: hp * 128 + r * 64 + 64],
                         start=True, stop=False)
                C.inc("TW", t.matmul(
                    osl, atm[:, (h % 2) * 128:(h % 2) * 128 + 128],
                    vtok[:, tt * D + h * 64: tt * D + (h + 1) * 64],
                    start=False, stop=True))
                t2_tw[h] = C.s["TW"].n
            C.T(_t2)

        kprs = [(kT[:, jc * T + tt * 128: jc * T + (tt + 1) * 128],
                 C.bufs["ktokt"][:, jc * 128:(jc + 1) * 128])
                for jc in range(DC)]
        emit_transposes(C, kprs, idb)

        for hp in range(8):
            def _t3(t, tt=tt, hp=hp):
                C.barrier_all(t)
                C.inc("TW", t.matmul(
                    C.bufs["pf5"][:, 0:128],
                    kT_tok_view(C, tt, hp),
                    vtok[:, tt * D + hp * 128: tt * D + (hp + 1) * 128],
                    start=True, stop=True))
            C.T(_t3)

            def _v3(v, hp=hp):
                C.waits(v, "TW")
                sl = Sacc[:, hp * 128:(hp + 1) * 128]
                C.inc("VW", v.tensor_add(sl, sl, C.bufs["pf5"][:, 0:128]))
            C.V(_v3)

        def _v4(v):
            C.waits(v, "VW")
            C.inc("VW", v.tensor_copy(Sbf[:], Sacc[:]))
        C.V(_v4)

        def _v5(v, tt=tt):
            C.barrier_all(v)
            for h in range(NH):
                pout = C.bufs["pf0"] if h < 8 else C.bufs["pf1"]
                osl = pout[:, (h % 8) * 64:(h % 8) * 64 + 64]
                mm = v.tensor_scalar(aotl[:, h * 64:(h + 1) * 64], osl,
                                     dec[:, tt * NH + h: tt * NH + h + 1],
                                     None, ALU.mult)
            C.inc("VW", mm)
        C.V(_v5)

        pairs = [(aotl[:, dc * 128:(dc + 1) * 128], aoT[:, dc * 128:(dc + 1) * 128])
                 for dc in range(DC)]
        emit_transposes(C, pairs, idb)

        for mh in range(2):
            ps = C.bufs["pf2"] if mh == 0 else C.bufs["pf3"]

            def _t4(t, tt=tt, mh=mh, ps=ps):
                C.barrier_all(t)
                for jc in range(DC):
                    t.matmul(ps[:, 0:512], aoT[:, jc * 128:(jc + 1) * 128],
                             woT[:, jc * D + mh * 512: jc * D + mh * 512 + 512],
                             start=(jc == 0), stop=False)
                C.inc("TW", t.matmul(ps[:, 0:512], ones1[0:1, 0:128],
                                     browo[0:1, mh * 512: mh * 512 + 512],
                                     start=False, stop=True))
            C.T(_t4)

            def _s4(s, mh=mh, ps=ps):
                C.waits(s, "TW", "ST")
                C.inc("SW", s.copy(a0t[:, mh * 512:mh * 512 + 512], ps[:, 0:512]))
            C.S(_s4)

        def _y5(y, tt=tt):
            C.waits(y, "SW")
            C.inc("ST", y.dma_start(out=o_a0[tt * 128:(tt + 1) * 128, :],
                                    in_=a0t[:, :]), 16)
        C.Y(_y5)

    C.free("wo_T")
    C.free("vtok", "kT", "qT", "xnT")
    maxrow = C.sb("maxrow", [1, 8], F32)
    g11 = C.sb("g11", [1, 4], F32)
    s1bc = C.sb("s1bc", [128, 1], F32)
    r1bc = C.sb("r1bc", [128, 1], F32)
    cxr = C.sb("cxr", [1, 1], F32)

    def _ym(y):
        C.inc("LD", y.dma_start(out=maxrow[:], in_=maxes), 16)
        C.inc("LD", y.dma_start(out=cxr[:], in_=clipx), 16)
    C.Y(_ym)

    def _sm(s):
        C.barrier_load(s)
        C.inc("SW", s.activation(g11[0:1, 3:4], cxr[0:1, 0:1], AF.Sigmoid))
    C.S(_sm)

    def _vm(v):
        C.barrier_all(v)
        C.edge(v, "VW", v.reduce_max(g11[0:1, 0:1], maxrow[0:1, :], AX.X))
        C.edge(v, "VW", v.tensor_tensor(g11[0:1, 1:2], g11[0:1, 0:1],
                                        g11[0:1, 3:4], ALU.mult))
        C.edge(v, "VW", v.tensor_scalar(g11[0:1, 1:2], g11[0:1, 1:2],
                                        1.0 / QMAX, None, ALU.mult))
        C.inc("VW", v.reciprocal(g11[0:1, 2:3], g11[0:1, 1:2]))
    C.V(_vm)

    emit_scalar_broadcast(C, g11[0:1, 1:2], s1bc[:, 0:1], ones1)
    emit_scalar_broadcast(C, g11[0:1, 2:3], r1bc[:, 0:1], ones1)

    xqT = C.sb("xqT", [128, DC * T], BF16)
    xqt = C.sb("xqt", [128, D], F32)
    xqb = C.sb("xqb", [128, D], BF16)

    for tt in range(NT):
        def _y(y, tt=tt):
            C.barrier_all(y)
            C.inc("LD", y.dma_start(out=xtile[:, 0:D],
                                    in_=xn_in[tt * 128:(tt + 1) * 128, :]), 16)
        C.Y(_y)

        def _v(v, tt=tt):
            C.waits(v, "LD", "ST")
            C.edge(v, "VW", v.tensor_scalar(xqt[:, :], xtile[:, 0:D],
                                            r1bc[:, 0:1], None, ALU.mult))
            C.edge(v, "VW", v.tensor_scalar(xqt[:, :], xqt[:, :], MAGIC,
                                            -MAGIC, ALU.add, ALU.add))
            C.edge(v, "VW", v.tensor_scalar(xqt[:, :], xqt[:, :], QMAX,
                                            -QMAX - 1.0, ALU.min, ALU.max))
            C.inc("VW", v.tensor_scalar(xqb[:, :], xqt[:, :], s1bc[:, 0:1],
                                        None, ALU.mult))
        C.V(_v)

        def _y2(y, tt=tt):
            C.waits(y, "VW")
            C.inc("ST", y.dma_start(out=o_xq[tt * 128:(tt + 1) * 128, :],
                                    in_=xqb[:, :]), 16)
        C.Y(_y2)

        pairs = []
        for dc in range(DC):
            src = xqb[:, dc * 128:(dc + 1) * 128]
            dst = xqT[:, dc * T + tt * 128: dc * T + (tt + 1) * 128]
            pairs.append((src, dst))
        emit_transposes(C, pairs, idb)

    NI = 2816
    NIC = NI // 128
    wgqT = C.sb("wgqT", [128, DC * NI], BF16)
    wuqT = C.sb("wuqT", [128, DC * NI], BF16)
    wchunk = C.sb("wchunk", [128, D], F32, side="right")
    wqb = C.sb("wqb", [128, D], BF16, side="right")
    wrm = C.sb("wrm", [128, 3], F32, side="right")

    for wap, dstT in ((wgate, wgqT), (wup, wuqT)):
        for ic in range(NIC):
            def _y(y, wap=wap, ic=ic):
                C.barrier_all(y)
                C.inc("LD", y.dma_start(out=wchunk[:, :],
                                        in_=wap[ic * 128:(ic + 1) * 128, :]), 16)
            C.Y(_y)

            def _v(v, ic=ic):
                C.waits(v, "LD")
                C.edge(v, "VW", v.reduce_max(wrm[:, 0:1], wchunk[:, :], AX.X,
                                             apply_absolute_value=True))
                C.edge(v, "VW", v.tensor_scalar(wrm[:, 1:2], wrm[:, 0:1],
                                                1.0 / QMAX, None, ALU.mult))
                C.edge(v, "VW", v.reciprocal(wrm[:, 2:3], wrm[:, 1:2]))
                C.edge(v, "VW", v.tensor_scalar(wchunk[:, :], wchunk[:, :],
                                                wrm[:, 2:3], None, ALU.mult))
                C.edge(v, "VW", v.tensor_scalar(wchunk[:, :], wchunk[:, :],
                                                MAGIC, -MAGIC, ALU.add, ALU.add))
                C.inc("VW", v.tensor_scalar(wqb[:, :], wchunk[:, :],
                                            wrm[:, 1:2], None, ALU.mult))
            C.V(_v)

            pairs = []
            for hc in range(DC):
                src = wqb[:, hc * 128:(hc + 1) * 128]
                dst = dstT[:, hc * NI + ic * 128: hc * NI + (ic + 1) * 128]
                pairs.append((src, dst))
            emit_transposes(C, pairs, idb)

    gch = C.sb("gch", [128, T], F32)
    uch = C.sb("uch", [128, T], F32)
    hmx = C.sb("hmx", [128, NIC], F32)

    evac_sw2 = [0, 0]
    g_tw2 = [0, 0]
    chain_vw2 = [0]
    for ic in range(NIC):
        for tc in range(4):
            for wi, (wT, dst) in enumerate(((wgqT, gch), (wuqT, uch))):
                ps = C.bufs[f"pf{wi}"]

                def _t(t, ic=ic, tc=tc, wi=wi, wT=wT, ps=ps):
                    if ic == 0 and tc == 0 and wi == 0:
                        C.barrier_all(t)
                    elif evac_sw2[wi]:
                        t.wait_ge(C.s["SW"].h, evac_sw2[wi])
                    for hc in range(DC):
                        mm = t.matmul(
                            ps[:, 0:512],
                            wT[:, hc * NI + ic * 128: hc * NI + (ic + 1) * 128],
                            xqT[:, hc * T + tc * 512: hc * T + (tc + 1) * 512],
                            start=(hc == 0), stop=(hc == DC - 1))
                    C.inc("TW", mm)
                    g_tw2[wi] = C.s["TW"].n
                C.T(_t)

                def _s(s, tc=tc, wi=wi, ps=ps, dst=dst):
                    s.wait_ge(C.s["TW"].h, g_tw2[wi])
                    if chain_vw2[0]:
                        s.wait_ge(C.s["VW"].h, chain_vw2[0])
                    C.inc("SW", s.tensor_copy(dst[:, tc * 512:(tc + 1) * 512],
                                              ps[:, 0:512]))
                    evac_sw2[wi] = C.s["SW"].n
                C.V(_s)

        def _s2(s, ic=ic):
            C.waits(s, "SW")
            C.inc("SW", s.activation(gch[:, :], gch[:, :], AF.Silu))
        C.S(_s2)

        def _v2(v, ic=ic):
            C.waits(v, "SW")
            C.edge(v, "VW", v.tensor_mul(gch[:, :], gch[:, :], uch[:, :]))
            C.inc("VW", v.reduce_max(hmx[:, ic:ic + 1], gch[:, :], AX.X,
                                     apply_absolute_value=True))
            chain_vw2[0] = C.s["VW"].n
        C.V(_v2)

    emit_absmax_tiles(C, [hmx[:, 0:NIC]], C.sb("hgm", [1, 1], F32)[0:1, 0:1])

    def _yh(y):
        C.barrier_all(y)
        C.inc("ST", y.dma_start(out=o_hmax, in_=C.bufs["hgm"][0:1, 0:1]), 16)
        y.wait_ge(C.s["ST"].h, C.s["ST"].n)
    C.Y(_yh)

    if own:
        C.close()
    return C


def kT_tok_view(C, tt, hp):
    return C.bufs["ktokt"][:, hp * 128:(hp + 1) * 128]


def shard_x(x):
    out = []
    for b in range(4):
        for h in range(2):
            out.append(np.ascontiguousarray(x[b, h * T:(h + 1) * T, :]))
    return out


def run_A(inputs):
    C = build_A()
    xs = shard_x(inputs["L0_attn_x"])
    im = []
    for c in range(8):
        im.append({
            "i_x0": xs[c],
            "i_Wk": inputs["L0_attn_Wk"],
            "i_bk": inputs["L0_attn_bk"].reshape(1, D),
            "i_Wv": inputs["L0_attn_Wv"],
            "i_bv": inputs["L0_attn_bv"].reshape(1, D),
        })
    res = run_bass_kernel_spmd(C.nc, im, core_ids=list(range(8)))
    return res.results


def build_C(C=None):
    own = C is None
    if own:
        C = Ctx("C").start()
    NI = 2816
    NIC = NI // 128
    x0 = C.din("x0", [T, D], F32)
    a0 = C.din("a0", [T, D], F32)
    xq_in = C.din("xq", [T, D], BF16)
    hmaxes = C.din("hmaxes", [1, 8], F32)
    clipd = C.din("clipd", [1, 1], F32)
    wgate = C.din("wgate", [NI, D], F32)
    wup = C.din("wup", [NI, D], F32)
    wdown = C.din("wdown", [D, NI], F32)
    wk1 = C.din("Wk1", [D, D], F32)
    bk1 = C.din("bk1", [1, D], F32)

    o_x1 = C.dout("x1", [T, D], F32)
    o_xn1 = C.dout("xn1", [T, D], F32)
    o_k1 = C.dout("k1", [T, D], BF16)

    wdqT_dram = C.nc.dram_tensor("wdqT_dram", [NI, D], BF16).ap()

    idb, idf = emit_identity(C)
    ones1 = C.sb("ones1", [1, 128], F32)
    x0t = C.sb("x0t", [128, 512], F32)
    a0t = C.sb("a0t", [128, 512], F32)
    x1t = C.sb("x1t", [128, D], F32)

    def _v0(v):
        C.inc("VW", v.memset(ones1[:], 1.0))
    C.V(_v0)

    wchunk = C.sb("wchunk", [128, max(D, NI)], F32, side="right")
    wqb = C.sb("wqb", [128, max(D, NI)], BF16, side="right")
    wrm = C.sb("wrm", [128, 3], F32, side="right")
    wgqT = C.sb("wgqT", [128, DC * NI], BF16)
    wuqT = C.sb("wuqT", [128, DC * NI], BF16)

    def emit_wquant(wap, width, dst_pairs_fn, nrows):
        for rc in range(nrows // 128):
            def _y(y, rc=rc):
                C.barrier_all(y)
                C.inc("LD", y.dma_start(out=wchunk[:, 0:width],
                                        in_=wap[rc * 128:(rc + 1) * 128, :]), 16)
            C.Y(_y)

            def _v(v, rc=rc):
                C.waits(v, "LD")
                C.edge(v, "VW", v.reduce_max(wrm[:, 0:1], wchunk[:, 0:width],
                                             AX.X, apply_absolute_value=True))
                C.edge(v, "VW", v.tensor_scalar(wrm[:, 1:2], wrm[:, 0:1],
                                                1.0 / QMAX, None, ALU.mult))
                C.edge(v, "VW", v.reciprocal(wrm[:, 2:3], wrm[:, 1:2]))
                C.edge(v, "VW", v.tensor_scalar(wchunk[:, 0:width],
                                                wchunk[:, 0:width],
                                                wrm[:, 2:3], None, ALU.mult))
                C.edge(v, "VW", v.tensor_scalar(wchunk[:, 0:width],
                                                wchunk[:, 0:width],
                                                MAGIC, -MAGIC, ALU.add, ALU.add))
                C.inc("VW", v.tensor_scalar(wqb[:, 0:width], wchunk[:, 0:width],
                                            wrm[:, 1:2], None, ALU.mult))
            C.V(_v)
            emit_transposes(C, dst_pairs_fn(rc), idb)

    for wap, dstT in ((wgate, wgqT), (wup, wuqT)):
        def mkpairs(rc, dstT=dstT):
            return [(wqb[:, hc * 128:(hc + 1) * 128],
                     dstT[:, hc * NI + rc * 128: hc * NI + (rc + 1) * 128])
                    for hc in range(DC)]
        emit_wquant(wap, D, mkpairs, NI)

    wdT = C.sb("wdT", [128, D], BF16, side="right")

    for rc in range(DC):
        def _y(y, rc=rc):
            C.barrier_all(y)
            C.inc("LD", y.dma_start(out=wchunk[:, 0:NI],
                                    in_=wdown[rc * 128:(rc + 1) * 128, :]), 16)
        C.Y(_y)

        def _v(v, rc=rc):
            C.waits(v, "LD")
            C.edge(v, "VW", v.reduce_max(wrm[:, 0:1], wchunk[:, 0:NI],
                                         AX.X, apply_absolute_value=True))
            C.edge(v, "VW", v.tensor_scalar(wrm[:, 1:2], wrm[:, 0:1],
                                            1.0 / QMAX, None, ALU.mult))
            C.edge(v, "VW", v.reciprocal(wrm[:, 2:3], wrm[:, 1:2]))
            C.edge(v, "VW", v.tensor_scalar(wchunk[:, 0:NI], wchunk[:, 0:NI],
                                            wrm[:, 2:3], None, ALU.mult))
            C.edge(v, "VW", v.tensor_scalar(wchunk[:, 0:NI], wchunk[:, 0:NI],
                                            MAGIC, -MAGIC, ALU.add, ALU.add))
            C.inc("VW", v.tensor_scalar(wqb[:, 0:NI], wchunk[:, 0:NI],
                                        wrm[:, 1:2], None, ALU.mult))
        C.V(_v)

        for ic in range(NIC):
            emit_transposes(C, [(wqb[:, ic * 128:(ic + 1) * 128],
                                 wdT[:, rc * 128:(rc + 1) * 128])], idb)

            def _y2(y, ic=ic, rc=rc):
                C.waits(y, "VW")
                C.inc("ST", y.dma_start(
                    out=wdqT_dram[ic * 128:(ic + 1) * 128,
                                  rc * 128:(rc + 1) * 128],
                    in_=wdT[:, rc * 128:(rc + 1) * 128]), 16)
            C.Y(_y2)

    C.free("wdT", "wrm", "wqb", "wchunk")

    xqT = C.sb("xqT", [128, DC * T], BF16)
    xqb = C.sb("xqb", [128, D], BF16)

    for tt in range(NT):
        def _y(y, tt=tt):
            C.barrier_all(y)
            C.inc("LD", y.dma_start(out=xqb[:, :],
                                    in_=xq_in[tt * 128:(tt + 1) * 128, :]), 16)
        C.Y(_y)
        pairs = [(xqb[:, dc * 128:(dc + 1) * 128],
                  xqT[:, dc * T + tt * 128: dc * T + (tt + 1) * 128])
                 for dc in range(DC)]
        emit_transposes(C, pairs, idb)

    maxrow = C.sb("maxrow", [1, 8], F32)
    g11 = C.sb("g11", [1, 4], F32)
    s2bc = C.sb("s2bc", [128, 1], F32)
    r2bc = C.sb("r2bc", [128, 1], F32)
    cdr = C.sb("cdr", [1, 1], F32)

    def _ym(y):
        C.inc("LD", y.dma_start(out=maxrow[:], in_=hmaxes), 16)
        C.inc("LD", y.dma_start(out=cdr[:], in_=clipd), 16)
    C.Y(_ym)

    def _sm(s):
        C.barrier_load(s)
        C.inc("SW", s.activation(g11[0:1, 3:4], cdr[0:1, 0:1], AF.Sigmoid))
    C.S(_sm)

    def _vm(v):
        C.barrier_all(v)
        C.edge(v, "VW", v.reduce_max(g11[0:1, 0:1], maxrow[0:1, :], AX.X))
        C.edge(v, "VW", v.tensor_tensor(g11[0:1, 1:2], g11[0:1, 0:1],
                                        g11[0:1, 3:4], ALU.mult))
        C.edge(v, "VW", v.tensor_scalar(g11[0:1, 1:2], g11[0:1, 1:2],
                                        1.0 / QMAX, None, ALU.mult))
        C.inc("VW", v.reciprocal(g11[0:1, 2:3], g11[0:1, 1:2]))
    C.V(_vm)

    emit_scalar_broadcast(C, g11[0:1, 1:2], s2bc[:, 0:1], ones1)
    emit_scalar_broadcast(C, g11[0:1, 2:3], r2bc[:, 0:1], ones1)

    hq = C.sb("hq", [128, NIC * 512], BF16)
    gchk = C.sb("gchk", [128, 512], F32)
    uchk = C.sb("uchk", [128, 512], F32)
    wdh = C.sb("wdh", [128, NIC * 512], BF16)

    for tc in range(4):
        evac_sw = [0, 0]
        chain_vw = [0]
        g_tw = [0, 0]

        for ic in range(NIC):
            for wi, (wT, dst) in enumerate(((wgqT, gchk), (wuqT, uchk))):
                ps = C.bufs[f"pf{wi}"]

                def _t(t, tc=tc, ic=ic, wi=wi, wT=wT, ps=ps):
                    if tc == 0 and ic == 0 and wi == 0:
                        C.barrier_all(t)
                    elif evac_sw[wi]:
                        t.wait_ge(C.s["SW"].h, evac_sw[wi])
                    for hc in range(DC):
                        mm = t.matmul(
                            ps[:, 0:512],
                            wT[:, hc * NI + ic * 128: hc * NI + (ic + 1) * 128],
                            xqT[:, hc * T + tc * 512: hc * T + (tc + 1) * 512],
                            start=(hc == 0), stop=(hc == DC - 1))
                    C.inc("TW", mm)
                    g_tw[wi] = C.s["TW"].n
                C.T(_t)

                def _s(s, wi=wi, ps=ps, dst=dst):
                    s.wait_ge(C.s["TW"].h, g_tw[wi])
                    if chain_vw[0]:
                        s.wait_ge(C.s["VW"].h, chain_vw[0])
                    C.inc("SW", s.tensor_copy(dst[:, :], ps[:, 0:512]))
                    evac_sw[wi] = C.s["SW"].n
                C.V(_s)

            def _s2(s):
                C.waits(s, "SW")
                C.inc("SW", s.activation(gchk[:, :], gchk[:, :], AF.Silu))
            C.S(_s2)

            def _v2(v, ic=ic):
                C.waits(v, "SW", "TW")
                C.edge(v, "VW", v.tensor_mul(gchk[:, :], gchk[:, :], uchk[:, :]))
                C.edge(v, "VW", v.tensor_scalar(gchk[:, :], gchk[:, :],
                                                r2bc[:, 0:1], None, ALU.mult))
                C.edge(v, "VW", v.tensor_scalar(gchk[:, :], gchk[:, :],
                                                MAGIC, -MAGIC, ALU.add, ALU.add))
                C.edge(v, "VW", v.tensor_scalar(gchk[:, :], gchk[:, :],
                                                QMAX, -QMAX - 1.0, ALU.min,
                                                ALU.max))
                C.inc("VW", v.tensor_scalar(hq[:, ic * 512:(ic + 1) * 512],
                                            gchk[:, :], s2bc[:, 0:1], None,
                                            ALU.mult))
                chain_vw[0] = C.s["VW"].n
            C.V(_v2)

        for mh in range(2):
            def _yw(y, mh=mh):
                C.barrier_all(y)
                for ic in range(NIC):
                    C.inc("LD", y.dma_start(
                        out=wdh[:, ic * 512:(ic + 1) * 512],
                        in_=wdqT_dram[ic * 128:(ic + 1) * 128,
                                      mh * 512: mh * 512 + 512]), 16)
            C.Y(_yw)

            for lt in range(4):
                tt = tc * 4 + lt
                ps = C.bufs[f"pf{2 + lt % 2}"]

                def _yx(y, tt=tt, mh=mh):
                    C.barrier_all(y)
                    C.inc("LD", y.dma_start(
                        out=x0t[:, :],
                        in_=x0[tt * 128:(tt + 1) * 128, mh * 512:mh * 512 + 512]), 16)
                    C.inc("LD", y.dma_start(
                        out=a0t[:, :],
                        in_=a0[tt * 128:(tt + 1) * 128, mh * 512:mh * 512 + 512]), 16)
                C.Y(_yx)

                def _t2(t, lt=lt, ps=ps):
                    C.barrier_all(t)
                    for ic in range(NIC):
                        mm = t.matmul(
                            ps[:, 0:512],
                            hq[:, ic * 512 + lt * 128: ic * 512 + (lt + 1) * 128],
                            wdh[:, ic * 512:(ic + 1) * 512],
                            start=(ic == 0), stop=(ic == NIC - 1))
                    C.inc("TW", mm)
                C.T(_t2)

                def _v3(v, ps=ps):
                    C.waits(v, "TW", "LD", "ST")
                    C.edge(v, "VW", v.tensor_add(x1t[:, 0:512], x0t[:, :],
                                                 ps[:, 0:512]))
                    C.inc("VW", v.tensor_add(x1t[:, 0:512], x1t[:, 0:512],
                                             a0t[:, :]))
                C.V(_v3)

                def _yo(y, tt=tt, mh=mh):
                    C.waits(y, "VW")
                    C.inc("ST", y.dma_start(
                        out=o_x1[tt * 128:(tt + 1) * 128, mh * 512:mh * 512 + 512],
                        in_=x1t[:, 0:512]), 16)
                C.Y(_yo)

    C.free("wdh", "uchk", "gchk", "hq", "cdr", "r2bc", "s2bc", "g11",
           "maxrow", "xqb", "xqT", "wuqT", "wgqT")
    xn1T = C.sb("xn1T", [128, DC * T], BF16)
    stats = C.sb("stats", [128, NT * 2], F32)
    xbt = C.sb("xbt", [128, D], BF16)
    k1t = C.sb("k1t", [128, D], BF16)
    browk1 = C.sb("brow_k1", [1, D], F32)

    def _yk(y):
        C.inc("LD", y.dma_start(out=browk1[:], in_=bk1), 16)
    C.Y(_yk)

    wk1n = C.sb("wk1n", [128, DC * D], BF16)

    def _gk(g):
        C.barrier_all(g)
        for dc in range(DC):
            C.inc("GD", g.dma_start(out=wk1n[:, dc * D:(dc + 1) * D],
                                    in_=wk1[dc * 128:(dc + 1) * 128, :]), 16)
    C.G(_gk)

    for tt in range(NT):
        def _yx(y, tt=tt):
            C.barrier_all(y)
            C.inc("LD", y.dma_start(out=x1t[:, :],
                                    in_=o_x1[tt * 128:(tt + 1) * 128, :]), 16)
        C.Y(_yx)
        emit_ln_tiles(C, [x1t[:, :]], [x1t[:, :]], stats[:, 2 * tt:2 * tt + 2])

        def _yo(y, tt=tt):
            C.waits(y, "VW")
            C.inc("ST", y.dma_start(out=o_xn1[tt * 128:(tt + 1) * 128, :],
                                    in_=x1t[:, :]), 16)
        C.Y(_yo)

        def _vb(v, tt=tt):
            C.waits(v, "VW", "TW", "ST")
            C.inc("VW", v.tensor_copy(xbt[:, :], x1t[:, :]))
        C.V(_vb)

        pairs = [(xbt[:, dc * 128:(dc + 1) * 128],
                  xn1T[:, dc * T + tt * 128: dc * T + (tt + 1) * 128])
                 for dc in range(DC)]
        emit_transposes(C, pairs, idb)

    for tt in range(NT):
        for mh in range(2):
            ps = C.bufs[f"pf{mh}"]

            def _t(t, tt=tt, mh=mh, ps=ps):
                C.barrier_all(t)
                for dc in range(DC):
                    t.matmul(ps[:, 0:512],
                             xn1T[:, dc * T + tt * 128: dc * T + (tt + 1) * 128],
                             wk1n[:, dc * D + mh * 512: dc * D + mh * 512 + 512],
                             start=(dc == 0), stop=False)
                C.inc("TW", t.matmul(ps[:, 0:512], ones1[0:1, 0:128],
                                     browk1[0:1, mh * 512: mh * 512 + 512],
                                     start=False, stop=True))
            C.T(_t)

            def _s(s, mh=mh, ps=ps):
                C.waits(s, "TW", "ST")
                C.inc("SW", s.copy(k1t[:, mh * 512:mh * 512 + 512], ps[:, 0:512]))
            C.S(_s)

        def _yk2(y, tt=tt):
            C.waits(y, "SW")
            C.inc("ST", y.dma_start(out=o_k1[tt * 128:(tt + 1) * 128, :],
                                    in_=k1t[:, :]), 16)
        C.Y(_yk2)

    if own:
        def _yfin(y):
            y.wait_ge(C.s["ST"].h, C.s["ST"].n)
        C.Y(_yfin)

    if own:
        C.close()
    return C


def build_D(C=None):
    own = C is None
    if own:
        C = Ctx("D").start()
    KTOT = 2 * T
    NKT = KTOT // 128
    xn1 = C.din("xn1", [T, D], F32)
    k_own = C.din("k_own", [T, D], BF16)
    k_part = C.din("k_part", [T, D], BF16)
    first = C.din("first", [1, 1], F32)
    wq1 = C.din("Wq1", [D, D], F32)
    bq1 = C.din("bq1", [1, D], F32)

    o_a1 = C.dout("a1", [T, D], F32)
    o_dbg = C.dout("dbg", [128, 1024 + 512], F32) if DBG_D else None

    idb, idf = emit_identity(C)
    ones1 = C.sb("ones1", [1, 128], F32)
    onecol = C.sb("onecol", [128, 1], BF16)

    def _v0(v):
        v.memset(ones1[:], 1.0)
        C.inc("VW", v.memset(onecol[:], 1.0))
    C.V(_v0)

    qT = C.sb("qT", [128, DC * T], BF16)
    xbt = C.sb("xbt", [128, D], BF16)
    xtile = C.sb("xtile", [128, D], F32)

    xn1T = C.sb("xn1T", [128, DC * T], BF16, side="right")
    wq1n = C.sb("wq1n", [128, DC * D], BF16, side="right")
    bqc, _ = emit_bias_cols(C, bq1, D, "bq1")

    for tt in range(NT):
        def _y(y, tt=tt):
            C.waits(y, "VW", "ST", "GD")
            C.inc("LD", y.dma_start(out=xtile[:, :],
                                    in_=xn1[tt * 128:(tt + 1) * 128, :]), 16)
        C.Y(_y)

        def _v(v, tt=tt):
            C.waits(v, "LD", "TW")
            C.inc("VW", v.tensor_copy(xbt[:, :], xtile[:, :]))
        C.V(_v)

        pairs = [(xbt[:, dc * 128:(dc + 1) * 128],
                  xn1T[:, dc * T + tt * 128: dc * T + (tt + 1) * 128])
                 for dc in range(DC)]
        emit_transposes(C, pairs, idb)

    def _g(g):
        C.barrier_all(g)
        for dc in range(DC):
            C.inc("GD", g.dma_start(out=wq1n[:, dc * D:(dc + 1) * D],
                                    in_=wq1[dc * 128:(dc + 1) * 128, :]), 16)
    C.G(_g)

    for jc in range(DC):
        for tc in range(4):
            ps = C.bufs[f"pf{tc % 2}"]

            def _t(t, jc=jc, tc=tc, ps=ps):
                C.barrier_all(t)
                for dc in range(DC):
                    mm = t.matmul(
                        ps[:, 0:512],
                        wq1n[:, dc * D + jc * 128: dc * D + (jc + 1) * 128],
                        xn1T[:, dc * T + tc * 512: dc * T + (tc + 1) * 512],
                        start=(dc == 0), stop=(dc == DC - 1))
                C.inc("TW", mm)
            C.T(_t)

            def _s(s, jc=jc, tc=tc, ps=ps):
                C.waits(s, "TW")
                C.inc("VW", s.tensor_scalar(
                    qT[:, jc * T + tc * 512: jc * T + (tc + 1) * 512],
                    ps[:, 0:512], bqc[:, jc:jc + 1], None, ALU.add))
            C.V(_s)
    C.free("wq1n", "xn1T")

    ktokf = C.sb("ktokf", [128, NKT * D], BF16)
    kTf = C.sb("kTf", [128, DC * KTOT], BF16)

    def _yk(y):
        C.barrier_all(y)
        for tt in range(NT):
            C.inc("LD", y.dma_start(
                out=ktokf[:, tt * D:(tt + 1) * D],
                in_=k_own[tt * 128:(tt + 1) * 128, :]), 16)
            C.inc("LD", y.dma_start(
                out=ktokf[:, (NT + tt) * D:(NT + tt + 1) * D],
                in_=k_part[tt * 128:(tt + 1) * 128, :]), 16)
    C.Y(_yk)

    for kt in range(NKT):
        pairs = [(ktokf[:, kt * D + dc * 128: kt * D + (dc + 1) * 128],
                  kTf[:, dc * KTOT + kt * 128: dc * KTOT + (kt + 1) * 128])
                 for dc in range(DC)]
        emit_transposes(C, pairs, idb)

    if DBG_D:
        qdump = C.sb("qdump", [128, 1024], F32)

        def _vq(v):
            C.barrier_all(v)
            C.inc("VW", v.tensor_copy(qdump[:, :], qT[:, 0:1024]))
        C.V(_vq)

        def _yq(y):
            C.waits(y, "VW")
            C.inc("ST", y.dma_start(out=o_dbg[:, 0:1024], in_=qdump[:, :]), 16)
        C.Y(_yq)

    expt = C.sb("expt", [128, 4 * 512], BF16)
    obuf = C.sb("obuf", [128, 512], F32)
    rcol4 = C.sb("rcol4", [128, 4], F32)
    a1buf = C.sb("a1buf", [128, 4 * D], F32)
    SCALE = 1.0 / math.sqrt(1024.0)

    pfA = C.bufs["pfA"]
    for qc in range(4):
        for h in range(NH):
            hp, r = h // 2, h % 2
            rows = slice(r * 64, r * 64 + 64)
            pout = C.bufs["pf4"]
            NG2 = NKT // 2
            sim_tw = [0] * NG2
            v_tw = [0] * NKT
            exp_sw = [0] * NG2

            def _t_sim(t, g):
                gb = g % 2
                if g == 0:
                    C.barrier_all(t)
                if g >= 2:
                    t.wait_ge(C.s["SW"].h, exp_sw[g - 2])
                for kk in range(2):
                    kt = 2 * g + kk
                    C.inc("TW", t.matmul(
                        pfA[:, gb * 1024 + kk * 512: gb * 1024 + kk * 512 + 512],
                        kTf[rows, hp * KTOT + kt * 128: hp * KTOT + (kt + 1) * 128],
                        qT[rows, hp * T + qc * 512: hp * T + (qc + 1) * 512],
                        start=True, stop=True))
                sim_tw[g] = C.s["TW"].n

            def _s_exp(s, g):
                gb = g % 2
                s.wait_ge(C.s["TW"].h, sim_tw[g])
                if g >= 2:
                    s.wait_ge(C.s["TW"].h, v_tw[2 * (g - 2) + 1])
                C.inc("SW", s.activation(
                    expt[:, gb * 1024: gb * 1024 + 1024],
                    pfA[:, gb * 1024: gb * 1024 + 1024], AF.Exp, scale=SCALE))
                exp_sw[g] = C.s["SW"].n

            def _t_v(t, kt):
                g = kt // 2
                t.wait_ge(C.s["SW"].h, exp_sw[g])
                esl = expt[:, (kt % 4) * 512:(kt % 4) * 512 + 512]
                t.matmul(pout[0:64, 0:512],
                         ktokf[:, kt * D + h * 64: kt * D + (h + 1) * 64],
                         esl, start=(kt == 0), stop=(kt == NKT - 1))
                C.inc("TW", t.matmul(C.bufs["pf5"][0:1, 0:512],
                                     onecol[:, 0:1], esl,
                                     start=(kt == 0), stop=(kt == NKT - 1)))
                v_tw[kt] = C.s["TW"].n

            for g in range(NG2 + 1):
                if g < NG2:
                    C.T(lambda t, g=g: _t_sim(t, g))
                    C.S(lambda s, g=g: _s_exp(s, g))
                if g >= 1:
                    C.T(lambda t, g=g: _t_v(t, 2 * (g - 1)))
                    C.T(lambda t, g=g: _t_v(t, 2 * (g - 1) + 1))

            def _s2(s):
                C.waits(s, "TW", "ST")
                s.copy(obuf[0:64, 0:512], pout[0:64, 0:512])
                C.inc("SW", s.copy(obuf[64:65, 0:512],
                                   C.bufs["pf5"][0:1, 0:512]))
            C.S(_s2)

            def _t3(t):
                C.waits(t, "SW", "VW")
                for ci in range(4):
                    C.inc("TW", t.transpose(
                        C.bufs["pf5"][0:128, ci * 65: ci * 65 + 65],
                        obuf[0:65, ci * 128:(ci + 1) * 128], idf[0:65, 0:65]))
            C.T(_t3)

            def _v3(v, h=h):
                C.waits(v, "TW")
                for ci in range(4):
                    v.reciprocal(rcol4[:, ci:ci + 1],
                                 C.bufs["pf5"][:, ci * 65 + 64: ci * 65 + 65])
                C.edge(v, "VW", v.engine_nop())
                for ci in range(4):
                    mm = v.tensor_scalar(
                        a1buf[:, ci * D + h * 64: ci * D + (h + 1) * 64],
                        C.bufs["pf5"][:, ci * 65: ci * 65 + 64],
                        rcol4[:, ci:ci + 1], None, ALU.mult)
                C.inc("VW", mm)
            C.V(_v3)

        def _yo(y, qc=qc):
            C.waits(y, "VW")
            for ci in range(4):
                C.inc("ST", y.dma_start(
                    out=o_a1[(qc * 4 + ci) * 128:(qc * 4 + ci + 1) * 128, :],
                    in_=a1buf[:, ci * D:(ci + 1) * D]), 16)
        C.Y(_yo)

    if own:
        def _yfin(y):
            y.wait_ge(C.s["ST"].h, C.s["ST"].n)
        C.Y(_yfin)

    if own:
        C.close()
    return C


def run_D(inputs, resC):
    C = build_D()
    im = []
    for c in range(8):
        b, h = c // 2, c % 2
        part = 2 * b + (1 - h)
        im.append({
            "i_xn1": resC[c]["o_xn1"],
            "i_k_own": resC[c]["o_k1"],
            "i_k_part": resC[part]["o_k1"],
            "i_first": np.array([[1.0 - h]], np.float32),
            "i_Wq1": inputs["L1_attn_Wq"],
            "i_bq1": inputs["L1_attn_bq"].reshape(1, D),
        })
    res = run_bass_kernel_spmd(C.nc, im, core_ids=list(range(8)))
    return res.results


WIN = 512
IHEADS = 8
INNER = 512
NG = 20


def build_E1(C=None):
    own = C is None
    if own:
        C = Ctx("E1").start()
    z_in = C.din("z", [T, D], F32)
    ztail = C.din("ztail", [640, D], F32)
    wqkv = C.din("wqkv", [D, 3 * INNER], F32)
    qsr = C.din("qs", [1, HD], F32)
    ksr = C.din("ks", [1, HD], F32)
    wo_in = C.din("wo", [INNER, D], F32)
    tR1 = C.din("tR1", [T, INNER], F32)
    tR2 = C.din("tR2", [T, INNER], F32)
    tR3 = C.din("tR3", [T, INNER], F32)
    tR4 = C.din("tR4", [T, INNER], F32)
    tR3t = C.din("tR3t", [WIN, INNER], F32)
    tR4t = C.din("tR4t", [WIN, INNER], F32)
    mask0 = C.din("mask0", [1024, WIN], BF16)
    maskr = C.din("maskr", [1024, WIN], BF16)

    o_zp = C.dout("zp", [T, D], F32)

    idb, idf = emit_identity(C)
    ones1 = C.sb("ones1", [1, 128], F32)
    onecol = C.sb("onecol", [128, 1], BF16)

    def _v0(v):
        v.memset(ones1[:], 1.0)
        C.inc("VW", v.memset(onecol[:], 1.0))
    C.V(_v0)

    zT = C.sb("zT", [128, DC * (T + 1)], BF16, side="right")
    zTt = C.sb("zTt", [128, DC * 640], BF16, side="right")
    ztile = C.sb("ztile", [128, D], F32)
    zbt = C.sb("zbt", [128, D], BF16)

    for tt in range(NT):
        def _y(y, tt=tt):
            C.waits(y, "VW", "ST", "GD")
            C.inc("LD", y.dma_start(out=ztile[:, :],
                                    in_=z_in[tt * 128:(tt + 1) * 128, :]), 16)
        C.Y(_y)

        def _v(v):
            C.waits(v, "LD", "TW")
            C.inc("VW", v.tensor_copy(zbt[:, :], ztile[:, :]))
        C.V(_v)

        pairs = [(zbt[:, dc * 128:(dc + 1) * 128],
                  zT[:, dc * (T + 1) + 1 + tt * 128: dc * (T + 1) + 1 + (tt + 1) * 128])
                 for dc in range(DC)]
        emit_transposes(C, pairs, idb)

    useScE = C.din("useSc", [1, 1], F32)
    uscE = C.sb("uscE", [128, 1], F32)
    urowE = C.sb("urowE", [1, 1], F32)

    def _yu(y):
        C.inc("LD", y.dma_start(out=urowE[:], in_=useScE), 16)
    C.Y(_yu)
    emit_scalar_broadcast(C, urowE[0:1, 0:1], uscE[:, 0:1], ones1)

    for tt in range(5):
        def _y(y, tt=tt):
            C.waits(y, "VW", "ST", "GD")
            C.inc("LD", y.dma_start(out=ztile[:, :],
                                    in_=ztail[tt * 128:(tt + 1) * 128, :]), 16)
        C.Y(_y)

        def _v(v):
            C.waits(v, "LD", "TW")
            C.edge(v, "VW", v.tensor_scalar(ztile[:, :], ztile[:, :],
                                            uscE[:, 0:1], None, ALU.mult))
            C.inc("VW", v.tensor_copy(zbt[:, :], ztile[:, :]))
        C.V(_v)

        pairs = [(zbt[:, dc * 128:(dc + 1) * 128],
                  zTt[:, dc * 640 + tt * 128: dc * 640 + (tt + 1) * 128])
                 for dc in range(DC)]
        emit_transposes(C, pairs, idb)

    def _vh(v):
        C.barrier_all(v)
        for dc in range(DC):
            mm = v.tensor_copy(zT[:, dc * (T + 1): dc * (T + 1) + 1],
                               zTt[:, dc * 640 + 639: dc * 640 + 640])
        C.inc("VW", mm)
    C.V(_vh)

    def hT_view(dc, lo, n):
        base = dc * (T + 1) + 1
        if dc >= 4:
            base -= 1
        return zT[:, base + lo: base + lo + n]

    def hTt_view(dc, lo, n):
        base = dc * 640 + 128
        if dc >= 4:
            base -= 1
        return zTt[:, base + lo: base + lo + n]

    if E1_STOP == 1:
        return C
    wqkvn = C.sb("wqkvn", [128, DC * 3 * INNER], BF16, side="right")

    def _gw(g):
        C.barrier_all(g)
        for dc in range(DC):
            C.inc("GD", g.dma_start(
                out=wqkvn[:, dc * 3 * INNER:(dc + 1) * 3 * INNER],
                in_=wqkv[dc * 128:(dc + 1) * 128, :]), 16)
    C.G(_gw)

    qb = C.sb("qb", [128, NT * INNER], BF16)
    kb = C.sb("kb", [128, NG * INNER], BF16)
    vb = C.sb("vb", [128, NG * INNER], BF16)

    for g in range(NG):
        is_own = g >= 4
        tt = g - 4
        outs = ((qb, 0), (kb, INNER), (vb, 2 * INNER)) if is_own else                ((kb, INNER), (vb, 2 * INNER))
        for buf, joff in outs:
            ps = C.bufs["pf0"] if joff != 2 * INNER else C.bufs["pf1"]

            def _t(t, g=g, is_own=is_own, tt=tt, joff=joff, ps=ps):
                C.barrier_all(t)
                for dc in range(DC):
                    view = hT_view(dc, tt * 128, 128) if is_own else                         hTt_view(dc, (g % 4) * 128, 128)
                    mm = t.matmul(ps[:, 0:512], view,
                                  wqkvn[:, dc * 3 * INNER + joff:
                                        dc * 3 * INNER + joff + INNER],
                                  start=(dc == 0), stop=(dc == DC - 1))
                C.inc("TW", mm)
            C.T(_t)

            def _s(s, g=g, tt=tt, buf=buf, joff=joff, ps=ps):
                C.waits(s, "TW")
                dst = buf[:, (tt if buf is qb else g) * INNER:
                          (tt if buf is qb else g) * INNER + INNER]
                C.inc("SW", s.copy(dst, ps[:, 0:512]))
            C.S(_s)

    if E1_STOP == 2:
        return C
    C.free("wqkvn", "zTt", "zT")

    qsb = C.sb("qsb", [128, INNER], F32)
    ksb = C.sb("ksb", [128, INNER], F32)
    qsrow = C.sb("qsrow", [1, INNER], F32)
    ksrow = C.sb("ksrow", [1, INNER], F32)

    def _yq(y):
        for h8 in range(IHEADS):
            C.inc("LD", y.dma_start(out=qsrow[0:1, h8 * HD:(h8 + 1) * HD],
                                    in_=qsr), 16)
            C.inc("LD", y.dma_start(out=ksrow[0:1, h8 * HD:(h8 + 1) * HD],
                                    in_=ksr), 16)
    C.Y(_yq)

    def _tb(t):
        C.barrier_all(t)
        t.matmul(C.bufs["pf4"][:, 0:INNER], ones1[0:1, 0:128], qsrow[0:1, :],
                 start=True, stop=True)
        C.inc("TW", t.matmul(C.bufs["pf5"][:, 0:INNER], ones1[0:1, 0:128],
                             ksrow[0:1, :], start=True, stop=True))
    C.T(_tb)

    def _vb(v):
        C.waits(v, "TW")
        v.tensor_copy(qsb[:, :], C.bufs["pf4"][:, 0:INNER])
        C.inc("VW", v.tensor_copy(ksb[:, :], C.bufs["pf5"][:, 0:INNER]))
    C.V(_vb)

    nrm = C.sb("nrm", [128, IHEADS], F32)
    sqs = C.sb("sqs", [128, HD], F32)
    rtile = C.sb("rtile", [128, 2 * INNER], F32)
    tmp1 = C.sb("tmp1", [128, INNER], F32)
    tmp2 = C.sb("tmp2", [128, INNER], F32)

    def emit_rotnorm(buf, g, sb_scale, t1_ap, t2_ap):
        tile = buf[:, g * INNER:(g + 1) * INNER]

        def _y(y, t1_ap=t1_ap, t2_ap=t2_ap):
            C.barrier_all(y)
            C.inc("LD", y.dma_start(out=rtile[:, 0:INNER], in_=t1_ap), 16)
            C.inc("LD", y.dma_start(out=rtile[:, INNER:2 * INNER], in_=t2_ap), 16)
        C.Y(_y)

        for h8 in range(IHEADS):
            def _s(s, h8=h8, tile=tile):
                C.waits(s, "VW", "SW")
                C.inc("SW", s.activation(sqs[:, :],
                                         tile[:, h8 * HD:(h8 + 1) * HD],
                                         AF.Square,
                                         accum_out=nrm[:, h8:h8 + 1]))
            C.S(_s)

        def _s2(s):
            C.waits(s, "SW")
            C.inc("SW", s.activation(nrm[:, :], nrm[:, :], AF.Sqrt))
        C.S(_s2)

        def _v(v, tile=tile):
            C.waits(v, "SW")
            C.edge(v, "VW", v.tensor_scalar(nrm[:, :], nrm[:, :], 1e-12, None,
                                            ALU.max))
            C.edge(v, "VW", v.reciprocal(nrm[:, :], nrm[:, :]))
            for h8 in range(IHEADS):
                mm = v.tensor_scalar(tile[:, h8 * HD:(h8 + 1) * HD],
                                     tile[:, h8 * HD:(h8 + 1) * HD],
                                     nrm[:, h8:h8 + 1], None, ALU.mult)
            C.edge(v, "VW", mm)
            C.edge(v, "VW", v.tensor_mul(tmp1[:, :], tile, sb_scale[:, :]))
            C.edge(v, "VW", v.tensor_scalar(
                tmp2.ap().rearrange("p (h x) -> p h x", x=HD)[:, :, 0:32],
                tmp1.ap().rearrange("p (h x) -> p h x", x=HD)[:, :, 32:64],
                -1.0, None, ALU.mult))
            C.edge(v, "VW", v.tensor_copy(
                tmp2.ap().rearrange("p (h x) -> p h x", x=HD)[:, :, 32:64],
                tmp1.ap().rearrange("p (h x) -> p h x", x=HD)[:, :, 0:32]))
            C.edge(v, "VW", v.tensor_mul(tmp1[:, :], tmp1[:, :],
                                         rtile[:, 0:INNER]))
            C.edge(v, "VW", v.tensor_mul(tmp2[:, :], tmp2[:, :],
                                         rtile[:, INNER:2 * INNER]))
            C.inc("VW", v.tensor_add(tile, tmp1[:, :], tmp2[:, :]))
        C.V(_v)

    for tt in range(NT):
        emit_rotnorm(qb, tt, qsb, tR1[tt * 128:(tt + 1) * 128, :],
                     tR2[tt * 128:(tt + 1) * 128, :])
    for g in range(NG):
        if g < 4:
            t3 = tR3t[(g % 4) * 128:((g % 4) + 1) * 128, :]
            t4 = tR4t[(g % 4) * 128:((g % 4) + 1) * 128, :]
        else:
            tt = g - 4
            t3 = tR3[tt * 128:(tt + 1) * 128, :]
            t4 = tR4[tt * 128:(tt + 1) * 128, :]
        emit_rotnorm(kb, g, ksb, t3, t4)

    if E1_STOP == 3:
        return C
    qTl = C.sb("qTl", [128, NT * INNER], BF16)
    kTl = C.sb("kTl", [128, NG * INNER], BF16)
    qbb = C.sb("qbb", [128, INNER], BF16)

    for tt in range(NT):
        def _v(v, tt=tt):
            C.waits(v, "VW", "TW")
            C.inc("VW", v.tensor_copy(qbb[:, :], qb[:, tt * INNER:(tt + 1) * INNER]))
        C.V(_v)
        pairs = [(qbb[:, jc * 128:(jc + 1) * 128],
                  qTl[:, (jc * NT + tt) * 128:(jc * NT + tt + 1) * 128])
                 for jc in range(4)]
        emit_transposes(C, pairs, idb)
    for g in range(NG):
        def _v(v, g=g):
            C.waits(v, "VW", "TW")
            C.inc("VW", v.tensor_copy(qbb[:, :], kb[:, g * INNER:(g + 1) * INNER]))
        C.V(_v)
        pairs = [(qbb[:, jc * 128:(jc + 1) * 128],
                  kTl[:, (jc * NG + g) * 128:(jc * NG + g + 1) * 128])
                 for jc in range(4)]
        emit_transposes(C, pairs, idb)

    if E1_STOP == 4:
        return C
    mbuf = C.sb("mbuf", [128, 2 * 8 * WIN], BF16)

    def _ym(y):
        for kt in range(8):
            C.inc("LD", y.dma_start(
                out=mbuf[:, kt * WIN:(kt + 1) * WIN],
                in_=mask0[kt * 128:(kt + 1) * 128, :]), 16)
            C.inc("LD", y.dma_start(
                out=mbuf[:, (8 + kt) * WIN:(8 + kt + 1) * WIN],
                in_=maskr[kt * 128:(kt + 1) * 128, :]), 16)
    C.Y(_ym)

    expt = C.sb("expt", [128, 4 * WIN], BF16)
    simb = C.sb("simb", [128, 4 * WIN], F32)
    obuf = C.sb("obuf", [128, WIN], F32)
    rcolE = C.sb("rcolE", [128, 4], F32)
    ob = C.sb("ob", [128, NT * INNER], BF16)

    for w in range(4):
        for h8 in range(IHEADS):
            hp, r = h8 // 2, h8 % 2
            rows = slice(r * 64, r * 64 + 64)
            pout = C.bufs["pf2"]
            sim_tw = [0] * 8
            mask_vw = [0] * 8
            exp_sw = [0] * 8
            v_tw = [0] * 8

            def _t_sim(t, w, kt, g, psim):
                if kt == 0:
                    C.barrier_all(t)
                if kt >= 2:
                    t.wait_ge(C.s["VW"].h, mask_vw[kt - 2])

                C.inc("TW", t.matmul(
                    psim[:, 0:WIN],
                    kTl[rows, (hp * NG + g) * 128:(hp * NG + g + 1) * 128],
                    qTl[rows, (hp * NT + 4 * w) * 128:
                        (hp * NT + 4 * w) * 128 + WIN],
                    start=True, stop=True))
                sim_tw[kt] = C.s["TW"].n

            def _v_mask(v, w, kt, psim):
                v.wait_ge(C.s["TW"].h, sim_tw[kt])
                if kt >= 4:
                    g = kt // 2 - 2
                    v.wait_ge(C.s["SW"].h, exp_sw[2 * g + 1])
                moff = (0 if w == 0 else 8 * WIN) + kt * WIN
                C.inc("VW", v.tensor_add(
                    simb[:, (kt % 4) * WIN:(kt % 4) * WIN + WIN],
                    psim[:, 0:WIN], mbuf[:, moff:moff + WIN]))
                mask_vw[kt] = C.s["VW"].n

            def _s_exp(s, kt):
                s.wait_ge(C.s["VW"].h, mask_vw[kt])
                if kt >= 4:
                    s.wait_ge(C.s["TW"].h, v_tw[kt - 4])
                gb = ((kt - 1) % 4) * WIN
                C.inc("SW", s.activation(
                    expt[:, gb: gb + 2 * WIN],
                    simb[:, gb: gb + 2 * WIN],
                    AF.Exp, scale=8.0))
                exp_sw[kt - 1] = C.s["SW"].n
                exp_sw[kt] = C.s["SW"].n

            def _t_v(t, kt, g, h8):
                t.wait_ge(C.s["SW"].h, exp_sw[kt])
                esl = expt[:, (kt % 4) * WIN:(kt % 4) * WIN + WIN]
                t.matmul(pout[0:64, 0:WIN],
                         vb[:, g * INNER + h8 * HD: g * INNER + (h8 + 1) * HD],
                         esl, start=(kt == 0), stop=(kt == 7))
                C.inc("TW", t.matmul(C.bufs["pf3"][0:1, 0:WIN],
                                     onecol[:, 0:1], esl,
                                     start=(kt == 0), stop=(kt == 7)))
                v_tw[kt] = C.s["TW"].n

            for kt in range(10):
                if kt < 8:
                    g = 4 * w + kt
                    psim = C.bufs[f"pf{kt % 2}"]
                    C.T(lambda t, w=w, kt=kt, g=g, psim=psim:
                        _t_sim(t, w, kt, g, psim))
                    C.V(lambda v, w=w, kt=kt, psim=psim:
                        _v_mask(v, w, kt, psim))
                    if kt % 2 == 1:
                        C.S(lambda s, kt=kt: _s_exp(s, kt))
                if kt >= 2:
                    g = 4 * w + kt - 2
                    C.T(lambda t, kt=kt, g=g, h8=h8: _t_v(t, kt - 2, g, h8))

            def _s2(s):
                C.waits(s, "TW", "ST")
                s.copy(obuf[0:64, 0:WIN], pout[0:64, 0:WIN])
                C.inc("SW", s.copy(obuf[64:65, 0:WIN], C.bufs["pf3"][0:1, 0:WIN]))
            C.S(_s2)

            def _t3(t):
                C.waits(t, "SW", "VW")
                for ci in range(4):
                    C.inc("TW", t.transpose(
                        C.bufs["pf4"][0:128, ci * 65: ci * 65 + 65],
                        obuf[0:65, ci * 128:(ci + 1) * 128], idf[0:65, 0:65]))
            C.T(_t3)

            def _v3(v, w=w, h8=h8):
                C.waits(v, "TW")
                for ci in range(4):
                    v.reciprocal(rcolE[:, ci:ci + 1],
                                 C.bufs["pf4"][:, ci * 65 + 64: ci * 65 + 65])
                C.edge(v, "VW", v.engine_nop())
                for ci in range(4):
                    mm = v.tensor_scalar(
                        ob[:, (w * 4 + ci) * INNER + h8 * HD:
                           (w * 4 + ci) * INNER + (h8 + 1) * HD],
                        C.bufs["pf4"][:, ci * 65: ci * 65 + 64],
                        rcolE[:, ci:ci + 1], None, ALU.mult)
                C.inc("VW", mm)
            C.V(_v3)

    won = C.sb("won", [128, 4 * D], BF16)
    oT = C.sb("oT", [128, 4 * 128], BF16)
    zpt = C.sb("zpt", [128, D], F32)

    def _gw2(g):
        C.barrier_all(g)
        for ic in range(4):
            C.inc("GD", g.dma_start(out=won[:, ic * D:(ic + 1) * D],
                                    in_=wo_in[ic * 128:(ic + 1) * 128, :]), 16)
    C.G(_gw2)
    if E1_STOP == 6:
        return C

    for tt in range(NT):
        pairs = [(ob[:, tt * INNER + jc * 128: tt * INNER + (jc + 1) * 128],
                  oT[:, jc * 128:(jc + 1) * 128]) for jc in range(4)]
        emit_transposes(C, pairs, idb)
        if E1_STOP == 61:
            return C

        def _yz(y, tt=tt):
            C.barrier_all(y)
            C.inc("LD", y.dma_start(out=ztile[:, :],
                                    in_=z_in[tt * 128:(tt + 1) * 128, :]), 16)
        C.Y(_yz)
        if E1_STOP == 611:
            return C

        for mh in range(2):
            ps = C.bufs[f"pf{mh}"]

            def _t(t, mh=mh, ps=ps):
                C.barrier_all(t)
                for ic in range(4):
                    mm = t.matmul(ps[:, 0:512], oT[:, ic * 128:(ic + 1) * 128],
                                  won[:, ic * D + mh * 512: ic * D + mh * 512 + 512],
                                  start=(ic == 0), stop=(ic == 3))
                C.inc("TW", mm)
            C.T(_t)
            if E1_STOP == 612:
                return C

            def _v(v, mh=mh, ps=ps):
                C.waits(v, "TW", "LD", "ST")
                C.inc("VW", v.tensor_add(zpt[:, mh * 512:mh * 512 + 512],
                                         ztile[:, mh * 512:mh * 512 + 512],
                                         ps[:, 0:512]))
            C.V(_v)
            if E1_STOP == 613:
                return C
            if E1_STOP == 614 and mh == 1:
                return C

        def _yo(y, tt=tt):
            C.waits(y, "VW")
            C.inc("ST", y.dma_start(out=o_zp[tt * 128:(tt + 1) * 128, :],
                                    in_=zpt[:, :]), 16)
        C.Y(_yo)
        if E1_STOP == 62 and tt == 0:
            return C

    if E1_STOP == 615:
        return C

    if own:
        def _yfin(y):
            y.wait_ge(C.s["ST"].h, C.s["ST"].n)
        C.Y(_yfin)

    if own:
        C.close()
    return C


HID = 2730
NOC = 6
OCW = 455
NIC2 = 22


def build_E2(final, C=None):
    own = C is None
    if own:
        C = Ctx("E2f" if final else "E2").start()
    zp = C.din("zp", [T, D], F32)
    zptail = C.din("zptail", [1, D], F32)
    lg = C.din("lg", [1, D], F32)
    lb = C.din("lb", [1, D], F32)
    w1 = C.din("w1", [D, 2 * HID], F32)
    b1 = C.din("b1", [1, 2 * HID], F32)
    w2 = C.din("w2", [HID, D], F32)
    b2 = C.din("b2", [1, D], F32)
    if final:
        x1 = C.din("x1", [T, D], F32)
        a1 = C.din("a1", [T, D], F32)
    o_z = C.dout("z", [T, D], F32)

    idb, idf = emit_identity(C)
    ones1 = C.sb("ones1", [1, 128], F32)

    def _v0(v):
        C.inc("VW", v.memset(ones1[:], 1.0))
    C.V(_v0)

    zpT = C.sb("zpT", [128, DC * (T + 1)], BF16, side="right")
    ztile = C.sb("ztile", [128, D], F32)
    zbt = C.sb("zbt", [128, D], BF16)
    tailr = C.sb("tailr", [1, D], F32, side="right")
    tailb = C.sb("tailb", [1, D], BF16, side="right")

    def _yt(y):
        C.barrier_all(y)
        C.inc("LD", y.dma_start(out=tailr[:], in_=zptail), 16)
    C.Y(_yt)

    useScF = C.din("useSc", [1, 1], F32)
    urowF = C.sb("urowF", [1, 1], F32)

    def _yu(y):
        C.inc("LD", y.dma_start(out=urowF[:], in_=useScF), 16)
    C.Y(_yu)

    def _vt(v):
        C.waits(v, "LD")
        C.edge(v, "VW", v.tensor_scalar(tailr[:, :], tailr[:, :],
                                        urowF[0:1, 0:1], None, ALU.mult))
        C.inc("VW", v.tensor_copy(tailb[:, :], tailr[:, :]))
    C.V(_vt)

    for dc in range(DC):
        def _t(t, dc=dc):
            C.barrier_all(t)
            C.inc("TW", t.transpose(C.bufs["pb0"][0:128, 0:1],
                                    tailb[0:1, dc * 128:(dc + 1) * 128],
                                    idb[0:1, 0:1]))
        C.T(_t)

        def _v(v, dc=dc):
            C.waits(v, "TW")
            C.inc("VW", v.tensor_copy(zpT[:, dc * (T + 1): dc * (T + 1) + 1],
                                      C.bufs["pb0"][0:128, 0:1]))
        C.V(_v)

    s1b = C.sb("s1b", [128, NT], F32)
    s2b = C.sb("s2b", [128, NT], F32)
    q1b = C.sb("q1b", [128, NT], F32)
    q2b = C.sb("q2b", [128, NT], F32)
    s2s = C.sb("s2s", [128, NT], F32)
    q2s = C.sb("q2s", [128, NT], F32)
    sqs = C.sb("sqs2", [128, 512], F32)
    hstat = C.sb("hstat", [1, 2], F32)

    def _vhs(v):
        C.waits(v, "LD")
        C.inc("VW", v.reduce_sum(hstat[0:1, 0:1], tailr[0:1, 512:1024], AX.X))
    C.V(_vhs)

    def _shs(s):
        C.waits(s, "VW")
        C.inc("SW", s.activation(sqs[0:1, :], tailr[0:1, 512:1024], AF.Square,
                                 accum_out=hstat[0:1, 1:2]))
    C.S(_shs)
    C.free("tailb", "tailr")

    for tt in range(NT):
        def _y(y, tt=tt):
            C.waits(y, "VW", "SW", "TW")
            C.inc("LD", y.dma_start(out=ztile[:, :],
                                    in_=zp[tt * 128:(tt + 1) * 128, :]), 16)
        C.Y(_y)

        def _v(v, tt=tt):
            C.waits(v, "LD", "TW")
            v.reduce_sum(s1b[:, tt:tt + 1], ztile[:, 0:512], AX.X)
            v.reduce_sum(s2b[:, tt:tt + 1], ztile[:, 512:1024], AX.X)
            C.inc("VW", v.tensor_copy(zbt[:, :], ztile[:, :]))
        C.V(_v)

        def _s(s, tt=tt):
            C.waits(s, "VW")
            s.activation(sqs[:, :], ztile[:, 0:512], AF.Square,
                         accum_out=q1b[:, tt:tt + 1])
            C.inc("SW", s.activation(sqs[:, :], ztile[:, 512:1024], AF.Square,
                                     accum_out=q2b[:, tt:tt + 1]))
        C.S(_s)

        pairs = [(zbt[:, dc * 128:(dc + 1) * 128],
                  zpT[:, dc * (T + 1) + 1 + tt * 128:
                      dc * (T + 1) + 1 + (tt + 1) * 128])
                 for dc in range(DC)]
        emit_transposes(C, pairs, idb)

    skd = C.nc.dram_tensor(f"skd{C.sec}" if C.sec else "skd",
                           [128, 2 * NT], F32).ap()

    def _ysko(y):
        C.barrier_all(y)
        C.inc("ST", y.dma_start(out=skd[:, 0:NT], in_=s2b[:, :]), 16)
        C.inc("ST", y.dma_start(out=skd[:, NT:2 * NT], in_=q2b[:, :]), 16)
    C.Y(_ysko)

    def _yski(y):
        C.waits(y, "ST")
        C.inc("LD", y.dma_start(out=s2s[1:128, :], in_=skd[0:127, 0:NT]), 16)
        C.inc("LD", y.dma_start(out=s2s[0:1, 1:NT], in_=skd[127:128, 0:NT - 1]), 16)
        C.inc("LD", y.dma_start(out=q2s[1:128, :], in_=skd[0:127, NT:2 * NT]), 16)
        C.inc("LD", y.dma_start(out=q2s[0:1, 1:NT],
                                in_=skd[127:128, NT:2 * NT - 1]), 16)
    C.Y(_yski)

    def _vsk(v):
        C.waits(v, "LD", "SW")
        v.tensor_copy(s2s[0:1, 0:1], hstat[0:1, 0:1])
        C.inc("VW", v.tensor_copy(q2s[0:1, 0:1], hstat[0:1, 1:2]))
    C.V(_vsk)

    mu = C.sb("mu", [128, NT], F32)
    var = C.sb("var", [128, NT], F32)
    rstd = C.sb("rstdc", [128, NT], F32)
    stdv = C.sb("stdc", [128, NT], F32)
    negmu = C.sb("negmu", [128, NT], F32)

    def _vst(v):
        C.waits(v, "VW", "LD", "SW")
        C.edge(v, "VW", v.tensor_add(mu[:, :], s1b[:, :], s2s[:, :]))
        C.edge(v, "VW", v.tensor_scalar(mu[:, :], mu[:, :], 1.0 / 1024.0, None,
                                        ALU.mult))
        C.edge(v, "VW", v.tensor_add(var[:, :], q1b[:, :], q2s[:, :]))
        C.edge(v, "VW", v.tensor_scalar(var[:, :], var[:, :], 1.0 / 1024.0,
                                        None, ALU.mult))
        C.edge(v, "VW", v.tensor_mul(s1b[:, :], mu[:, :], mu[:, :]))
        C.edge(v, "VW", v.tensor_sub(var[:, :], var[:, :], s1b[:, :]))
        C.edge(v, "VW", v.tensor_scalar(var[:, :], var[:, :], 1e-5, None,
                                        ALU.add))
        C.inc("VW", v.tensor_scalar(negmu[:, :], mu[:, :], -1.0, None,
                                    ALU.mult))
    C.V(_vst)

    def _sst(s):
        C.waits(s, "VW")
        C.inc("SW", s.activation(stdv[:, :], var[:, :], AF.Sqrt))
    C.S(_sst)

    def _vst2(v):
        C.waits(v, "SW")
        C.inc("VW", v.reciprocal(rstd[:, :], stdv[:, :]))
    C.V(_vst2)

    if DBG_D:
        def _yds(y):
            C.barrier_all(y)
            dbg = C.nc.dram_tensor("o_dbgs", [8, 128, 1], F32,
                                   kind="ExternalOutput").ap()
            for di, src in enumerate((s1b, s2s, q1b, q2s, mu, var, rstd, stdv)):
                C.inc("ST", y.dma_start(out=dbg[di], in_=src[:, 0:1]), 16)
        C.Y(_yds)

    negmurow = C.sb("negmurow", [1, T], BF16)
    stdrow = C.sb("stdrow", [1, T], BF16)

    for tt in range(NT):
        for src, dstrow in ((negmu, negmurow), (stdv, stdrow)):
            def _t(t, tt=tt, src=src):
                C.barrier_all(t)
                C.inc("TW", t.transpose(C.bufs["pf4"][0:1, 0:128],
                                        src[:, tt:tt + 1], idf[:, :]))
            C.T(_t)

            def _v(v, tt=tt, dstrow=dstrow):
                C.waits(v, "TW")
                C.inc("VW", v.tensor_copy(dstrow[0:1, tt * 128:(tt + 1) * 128],
                                          C.bufs["pf4"][0:1, 0:128]))
            C.V(_v)

    w1b = C.sb("w1b", [128, DC * 2 * HID], BF16, side="right")

    def _gw1(g):
        C.barrier_all(g)
        for dc in range(DC):
            C.inc("GD", g.dma_start(out=w1b[:, dc * 2 * HID:(dc + 1) * 2 * HID],
                                    in_=w1[dc * 128:(dc + 1) * 128, :]), 16)
    C.G(_gw1)

    lgc, _ = emit_bias_cols(C, lg, D, "lg")
    lbc, _ = emit_bias_cols(C, lb, D, "lb")
    lbcb = C.sb("lbcb", [128, DC], BF16)
    onecb = C.sb("onecb", [128, 1], BF16)
    c1row = C.sb("c1row", [1, 2 * HID], BF16)
    c2row = C.sb("c2row", [1, 2 * HID], BF16)
    b1row = C.sb("b1row", [1, 2 * HID], F32, side="right")

    def _yb1(y):
        C.inc("LD", y.dma_start(out=b1row[:], in_=b1), 16)
    C.Y(_yb1)

    def _vlb(v):
        C.barrier_all(v)
        v.tensor_copy(lbcb[:, :], lbc[:, :])
        C.inc("VW", v.memset(onecb[:], 1.0))
    C.V(_vlb)

    NSL = (2 * HID + 511) // 512
    for sl in range(NSL):
        o0 = sl * 512
        ow = min(512, 2 * HID - o0)

        def _t(t, o0=o0, ow=ow):
            C.barrier_all(t)
            for dc in range(DC):
                mm = t.matmul(C.bufs["pf4"][0:1, 0:ow], lbcb[:, dc:dc + 1],
                              w1b[:, dc * 2 * HID + o0: dc * 2 * HID + o0 + ow],
                              start=(dc == 0), stop=(dc == DC - 1))
            C.inc("TW", mm)
        C.T(_t)

        def _v(v, o0=o0, ow=ow):
            C.waits(v, "TW")
            C.inc("VW", v.tensor_add(c2row[0:1, o0:o0 + ow],
                                     C.bufs["pf4"][0:1, 0:ow],
                                     b1row[0:1, o0:o0 + ow]))
        C.V(_v)

    C.free("b1row")

    def _vw1(v):
        C.barrier_all(v)
        for dc in range(DC):
            mm = v.tensor_scalar(w1b[:, dc * 2 * HID:(dc + 1) * 2 * HID],
                                 w1b[:, dc * 2 * HID:(dc + 1) * 2 * HID],
                                 lgc[:, dc:dc + 1], None, ALU.mult)
        C.inc("VW", mm)
    C.V(_vw1)

    for sl in range(NSL):
        o0 = sl * 512
        ow = min(512, 2 * HID - o0)

        def _t(t, o0=o0, ow=ow):
            C.barrier_all(t)
            for dc in range(DC):
                mm = t.matmul(C.bufs["pf4"][0:1, 0:ow], onecb[:, 0:1],
                              w1b[:, dc * 2 * HID + o0: dc * 2 * HID + o0 + ow],
                              start=(dc == 0), stop=(dc == DC - 1))
            C.inc("TW", mm)
        C.T(_t)

        def _v(v, o0=o0, ow=ow):
            C.waits(v, "TW")
            C.inc("VW", v.tensor_copy(c1row[0:1, o0:o0 + ow],
                                      C.bufs["pf4"][0:1, 0:ow]))
        C.V(_v)

    def h2T_view(dc, lo, n):
        base = dc * (T + 1) + 1
        if dc >= 4:
            base -= 1
        return zpT[:, base + lo: base + lo + n]

    if DBG_D:
        def _ydo(y):
            C.barrier_all(y)
            dbg2 = C.nc.dram_tensor("o_dbgo", [4, 1, 512], BF16,
                                    kind="ExternalOutput").ap()
            C.inc("ST", y.dma_start(out=dbg2[0], in_=w1b[0:1, 0:512]), 16)
            C.inc("ST", y.dma_start(out=dbg2[1], in_=zpT[0:1, 0:512]), 16)
            C.inc("ST", y.dma_start(out=dbg2[2], in_=c1row[0:1, 0:512]), 16)
            C.inc("ST", y.dma_start(out=dbg2[3], in_=c2row[0:1, 0:512]), 16)
        C.Y(_ydo)

    ya = C.sb("ya", [128, OCW], F32)
    yg = C.sb("yg", [128, OCW], F32)
    h3buf = C.sb("h3buf", [128, HID], BF16)
    h3T = C.sb("h3T", [128, NIC2 * 128], BF16)
    w2r = C.sb("w2r", [128, NIC2 * 512], BF16)
    b2row = C.sb("b2row", [1, D], F32)
    outt = C.sb("outt", [128, D], F32)
    xt1 = C.sb("xt1", [128, 512], F32) if final else None
    at1 = C.sb("at1", [128, 512], F32) if final else None

    def _yb2(y):
        C.inc("LD", y.dma_start(out=b2row[:], in_=b2), 16)
    C.Y(_yb2)

    w1_evac = [0, 0]
    w1_gtw = [0, 0]
    w1_h3vw = [0]
    for tt in range(NT):
        trow = slice(tt * 128, (tt + 1) * 128)
        for i in range(NOC):
            for half, (psn, ybuf) in enumerate((("pf0", ya), ("pf1", yg))):
                o0 = i * OCW + half * HID
                ps = C.bufs[psn]

                def _t(t, tt=tt, i=i, half=half, o0=o0, ps=ps):
                    if tt == 0 and i == 0 and half == 0:
                        C.barrier_all(t)
                    elif w1_evac[half]:
                        t.wait_ge(C.s["VW"].h, w1_evac[half])
                    for dc in range(DC):
                        t.matmul(ps[:, 0:OCW], h2T_view(dc, tt * 128, 128),
                                 w1b[:, dc * 2 * HID + o0: dc * 2 * HID + o0 + OCW],
                                 start=(dc == 0), stop=False)
                    t.matmul(ps[:, 0:OCW],
                             negmurow[0:1, tt * 128:(tt + 1) * 128],
                             c1row[0:1, o0:o0 + OCW], start=False, stop=False)
                    C.inc("TW", t.matmul(
                        ps[:, 0:OCW], stdrow[0:1, tt * 128:(tt + 1) * 128],
                        c2row[0:1, o0:o0 + OCW], start=False, stop=True))
                    w1_gtw[half] = C.s["TW"].n
                C.T(_t)

                def _v(v, tt=tt, half=half, ps=ps, ybuf=ybuf):
                    v.wait_ge(C.s["TW"].h, w1_gtw[half])
                    if w1_h3vw[0]:
                        v.wait_ge(C.s["SW"].h, w1_h3vw[0])
                    C.inc("VW", v.tensor_scalar(ybuf[:, :], ps[:, 0:OCW],
                                                rstd[:, tt:tt + 1], None,
                                                ALU.mult))
                    w1_evac[half] = C.s["VW"].n
                C.V(_v)

            if DBG_D and tt == 0 and i == 0:
                def _yd(y):
                    C.waits(y, "VW")
                    C.inc("ST", y.dma_start(out=C.nc.dram_tensor(
                        "o_dbgy", [128, 2 * OCW], F32,
                        kind="ExternalOutput").ap()[:, 0:OCW], in_=ya[:, :]), 16)
                    C.inc("ST", y.dma_start(
                        out=C.nc.dram_tensor("o_dbgy2", [128, OCW], F32,
                                             kind="ExternalOutput").ap(),
                        in_=yg[:, :]), 16)
                C.Y(_yd)

            def _sg(s):
                C.waits(s, "VW")
                C.inc("SW", s.activation(yg[:, :], yg[:, :], AF.Gelu))
            C.S(_sg)

            def _vh3(v, i=i):
                C.waits(v, "SW", "TW")
                C.inc("VW", v.tensor_mul(h3buf[:, i * OCW:(i + 1) * OCW],
                                         ya[:, :], yg[:, :]))
                w1_h3vw[0] = C.s["SW"].n
            C.V(_vh3)

        pairs = []
        for ic in range(NIC2):
            wdt = min(128, HID - ic * 128)
            pairs.append((h3buf[:, ic * 128: ic * 128 + wdt],
                          h3T[0:wdt, ic * 128:(ic + 1) * 128]))
        emit_transposes(C, pairs, idb)

        def _yz(y, tt=tt, trow=trow):
            C.barrier_all(y)
            C.inc("LD", y.dma_start(out=ztile[:, :], in_=zp[trow, :]), 16)
        C.Y(_yz)

        for mh in range(2):
            if final:
                def _yxa(y, tt=tt, trow=trow, mh=mh):
                    C.barrier_all(y)
                    C.inc("LD", y.dma_start(
                        out=xt1[:, :], in_=x1[trow, mh * 512:mh * 512 + 512]), 16)
                    C.inc("LD", y.dma_start(
                        out=at1[:, :], in_=a1[trow, mh * 512:mh * 512 + 512]), 16)
                C.Y(_yxa)
            def _yw2(y, mh=mh):
                C.waits(y, "TW")
                for ic in range(NIC2):
                    rw = min(128, HID - ic * 128)
                    C.inc("GD", y.dma_start(
                        out=w2r[:rw, ic * 512:(ic + 1) * 512],
                        in_=w2[ic * 128: ic * 128 + rw,
                               mh * 512: mh * 512 + 512]), 16)
            C.G(_yw2)

            ps = C.bufs[f"pf{2 + mh}"]

            def _t2(t, mh=mh, ps=ps):
                C.waits(t, "GD", "VW", "LD")
                for ic in range(NIC2):
                    rw = min(128, HID - ic * 128)
                    t.matmul(ps[:, 0:512], h3T[0:rw, ic * 128:(ic + 1) * 128],
                             w2r[0:rw, ic * 512:(ic + 1) * 512],
                             start=(ic == 0), stop=False)
                C.inc("TW", t.matmul(ps[:, 0:512], ones1[0:1, 0:128],
                                     b2row[0:1, mh * 512: mh * 512 + 512],
                                     start=False, stop=True))
            C.T(_t2)

            def _v2(v, mh=mh, ps=ps):
                C.waits(v, "TW", "LD", "ST")
                C.edge(v, "VW", v.tensor_add(outt[:, mh * 512:mh * 512 + 512],
                                             ztile[:, mh * 512:mh * 512 + 512],
                                             ps[:, 0:512]))
                if final:
                    C.edge(v, "VW", v.tensor_add(
                        outt[:, mh * 512:mh * 512 + 512],
                        outt[:, mh * 512:mh * 512 + 512], xt1[:, :]))
                    C.edge(v, "VW", v.tensor_add(
                        outt[:, mh * 512:mh * 512 + 512],
                        outt[:, mh * 512:mh * 512 + 512], at1[:, :]))
            C.V(_v2)

        def _yo(y, trow=trow):
            C.waits(y, "VW")
            C.inc("ST", y.dma_start(out=o_z[trow, :], in_=outt[:, :]), 16)
        C.Y(_yo)

    if own:
        def _yfin(y):
            y.wait_ge(C.s["ST"].h, C.s["ST"].n)
        C.Y(_yfin)

    if own:
        C.close()
    return C


def run_E2(inputs, layer, zp_shards, final=False, x1_shards=None, a1_shards=None):
    C = build_E2(final)
    im = []
    for c in range(8):
        b, h = c // 2, c % 2
        part = 2 * b + (1 - h)
        zptail = np.zeros((1, D), np.float32)
        if h == 1:
            zptail[0] = zp_shards[part][2047]
        m = {
            "i_zp": zp_shards[c],
            "i_zptail": zptail,
            "i_useSc": np.ones((1, 1), np.float32),
            "i_lg": inputs["L1_ffn_ln_g"][layer].reshape(1, D),
            "i_lb": inputs["L1_ffn_ln_b"][layer].reshape(1, D),
            "i_w1": np.ascontiguousarray(inputs["L1_ffn_w1"][layer]),
            "i_b1": inputs["L1_ffn_b1"][layer].reshape(1, 2 * HID),
            "i_w2": np.ascontiguousarray(inputs["L1_ffn_w2"][layer]),
            "i_b2": inputs["L1_ffn_b2"][layer].reshape(1, D),
        }
        if final:
            m["i_x1"] = x1_shards[c]
            m["i_a1"] = a1_shards[c]
        im.append(m)
    res = run_bass_kernel_spmd(C.nc, im, core_ids=list(range(8)))
    return res.results


def run_B(inputs, resA):
    C = build_B()
    im = []
    for c in range(8):
        b, h = c // 2, c % 2
        part = 2 * b + (1 - h)
        Sprev = resA[part]["o_S"] if h == 1 else np.zeros((D, HD), np.float32)
        maxes = np.concatenate([resA[j]["o_maxx"].reshape(1, 1)
                                for j in range(8)], axis=1)
        idx = np.arange(h * T, (h + 1) * T, dtype=np.float32).reshape(1, T)
        im.append({
            "i_xn": resA[c]["o_xn"],
            "i_Sprev": np.ascontiguousarray(Sprev),
            "i_useSc": np.ones((1, 1), np.float32),
            "i_maxes": maxes,
            "i_clipx": inputs["L0_ffn_clip_x"].reshape(1, 1),
            "i_idx": idx,
            "i_gamma": inputs["L0_attn_gamma"].reshape(1, NH),
            "i_Wq": inputs["L0_attn_Wq"],
            "i_bq": inputs["L0_attn_bq"].reshape(1, D),
            "i_Wk": inputs["L0_attn_Wk"],
            "i_bk": inputs["L0_attn_bk"].reshape(1, D),
            "i_Wv": inputs["L0_attn_Wv"],
            "i_bv": inputs["L0_attn_bv"].reshape(1, D),
            "i_Wo": inputs["L0_attn_Wo"],
            "i_bo": inputs["L0_attn_bo"].reshape(1, D),
            "i_wgate": inputs["L0_ffn_w_gate"],
            "i_wup": inputs["L0_ffn_w_up"],
        })
    res = run_bass_kernel_spmd(C.nc, im, core_ids=list(range(8)))
    return res.results


def run_C(inputs, resB):
    C = build_C()
    xs = shard_x(inputs["L0_attn_x"])
    hmaxes = np.concatenate([resB[j]["o_hmax"].reshape(1, 1)
                             for j in range(8)], axis=1)
    im = []
    for c in range(8):
        im.append({
            "i_x0": xs[c],
            "i_a0": resB[c]["o_a0"],
            "i_xq": resB[c]["o_xq"],
            "i_hmaxes": hmaxes,
            "i_clipd": inputs["L0_ffn_clip_down"].reshape(1, 1),
            "i_wgate": inputs["L0_ffn_w_gate"],
            "i_wup": inputs["L0_ffn_w_up"],
            "i_wdown": inputs["L0_ffn_w_down"],
            "i_Wk1": inputs["L1_attn_Wk"],
            "i_bk1": inputs["L1_attn_bk"].reshape(1, D),
        })
    res = run_bass_kernel_spmd(C.nc, im, core_ids=list(range(8)))
    return res.results


def _rot_tables(pos, kind):
    half = np.arange(0, HD, 2, dtype=np.float32)
    invf = 1.0 / 10000.0 ** (half / HD)
    t = pos.astype(np.float32)
    fr = t[:, None] * invf[None, :]
    fr = np.concatenate([fr, fr], 1)
    base = (half + 0.4 * HD) / (1.4 * HD)
    power = (t - (2 * T) // 2) / float(WIN // 2)
    sc = base[None, :] ** power[:, None]
    sc = np.concatenate([sc, sc], 1)
    cos, sin = np.cos(fr), np.sin(fr)
    if kind == "q":
        a, b = cos * sc, sin * sc
    else:
        a, b = cos / sc, sin / sc
    return (np.tile(a, (1, IHEADS)).astype(np.float32),
            np.tile(b, (1, IHEADS)).astype(np.float32))


def _inner_masks(first_blocked):
    r = np.arange(1024)[:, None]
    qi = np.arange(512)[None, :]
    allowed = (qi >= r - 512) & (qi <= r)
    m = np.where(allowed, 0.0, -30000.0).astype(np.float32)
    if first_blocked:
        m0 = m.copy()
        m0[0:512, :] = -30000.0
        return m0
    return m


def run_E1(inputs, layer, z_shards):
    C = build_E1()
    im = []
    for c in range(8):
        b, h = c // 2, c % 2
        part = 2 * b + (1 - h)
        ztail = np.zeros((640, D), np.float32)
        if h == 1:
            ztail[127:640] = z_shards[part][1535:2048]
        pos = np.arange(h * T, (h + 1) * T)
        tpos = np.arange(1536, 2048) if h == 1 else np.arange(512)
        q1, q2 = _rot_tables(pos, "q")
        k1, k2 = _rot_tables(pos, "k")
        kt1, kt2 = _rot_tables(tpos, "k")
        im.append({
            "i_z": z_shards[c],
            "i_ztail": ztail,
            "i_wqkv": np.ascontiguousarray(inputs["L1_ffn_wqkv"][layer]),
            "i_qs": inputs["L1_ffn_q_scale"][layer].reshape(1, HD),
            "i_ks": inputs["L1_ffn_k_scale"][layer].reshape(1, HD),
            "i_wo": np.ascontiguousarray(inputs["L1_ffn_wo"][layer]),
            "i_tR1": q1, "i_tR2": q2, "i_tR3": k1, "i_tR4": k2,
            "i_tR3t": kt1, "i_tR4t": kt2,
            "i_useSc": np.ones((1, 1), np.float32),
            "i_mask0": _inner_masks(h == 0).astype(ml_dtypes.bfloat16),
            "i_maskr": _inner_masks(False).astype(ml_dtypes.bfloat16),
        })
    res = run_bass_kernel_spmd(C.nc, im, core_ids=list(range(8)))
    return res.results




def emit_pair_send(C, slot, gen, payloads):
    def _g(g):
        C.barrier_all(g)
        with g.register(f"s{slot}h") as rh:
            g.reg_load(rh, C.bufs["myh_sb"][0:1, 0:1])
            for m in range(2):
                guard = g.If_eq(rh, 0) if m == 0 else g.Else()
                with guard:
                    for dram_t, rpm, row_lo, row_n, src in payloads:
                        g.dma_start(
                            out=dram_t.ap()[m * rpm: m * rpm + row_n, :],
                            in_=src).then_inc(C.s["GD"].h, 16)
            C.s["GD"].n += 16 * len(payloads)
            g.wait_ge(C.s["GD"].h, C.s["GD"].n)
            for m in range(2):
                guard = g.If_eq(rh, 0) if m == 0 else g.Else()
                with guard:
                    g.dma_start(
                        out=C.bufs["xflg_dram"].ap()[m:m + 1, slot:slot + 1],
                        in_=C.bufs["gens_sb"][0:1, gen:gen + 1]).then_inc(
                            C.s["GD"].h, 16)
            C.s["GD"].n += 16
            g.wait_ge(C.s["GD"].h, C.s["GD"].n)
    C.G(_g)


def emit_pair_recv(C, slot, gen, payloads):
    def _g(g):
        C.barrier_all(g)
        with g.register(f"r{slot}h") as rh, g.register(f"r{slot}f") as rf, \
             g.register(f"r{slot}v") as rv, g.register(f"r{slot}t") as rt:
            g.reg_load(rh, C.bufs["prh_sb"][0:1, 0:1])
            g.reg_mov(rt, 0)
            g.reg_mov(rf, 1)
            with g.While(rf):
                g.dma_start(
                    C.bufs["pollb"][0:1, 0:2],
                    bass.AP(C.bufs["xflg_dram"], slot,
                            [[64, 2], [1, 1], [1, 1]]),
                    allow_slow_non_contiguous=True).then_inc(
                        C.s["PL"].h, 16)
                g.reg_add(rt, rt, 16)
                g.wait_ge(C.s["PL"].h, rt)
                with g.If_eq(rh, 0):
                    g.reg_load(rv, C.bufs["pollb"][0:1, 0:1])
                with g.Else():
                    g.reg_load(rv, C.bufs["pollb"][0:1, 1:2])
                g.reg_sub(rf, rv, gen)
            for m in range(2):
                guard = g.If_eq(rh, 0) if m == 0 else g.Else()
                with guard:
                    for dst, dram_t, rpm, row_lo, row_n in payloads:
                        g.dma_start(
                            out=dst,
                            in_=dram_t.ap()[m * rpm: m * rpm + row_n, :]
                            ).then_inc(C.s["GD"].h, 16)
            C.s["GD"].n += 16 * len(payloads)
            g.wait_ge(C.s["GD"].h, C.s["GD"].n)
    C.G(_g)

    def _v(v):
        C.waits(v, "GD")
        C.inc("VW", v.engine_nop())
    C.V(_v)


def emit_FA(C, x0f, fstats, gmax_sb, maxesD):
    C.section("FA")
    NF = 48
    ft = C.sb("fa_t", [128, D], F32)
    rs = C.sb("fa_rs", [128, 6], F32)
    sq = C.sb("fa_sq", [128, D], F32)
    ssq = C.sb("fa_ssq", [128, 1], F32)
    fmx = C.sb("fa_fmx", [128, NF], F32)
    half = C.sb("fa_half", [1, 2], F32)

    for i in range(NF):
        def _y(y, i=i):
            C.barrier_all(y)
            C.inc("LD", y.dma_start(out=ft[:, :],
                                    in_=x0f[i * 128:(i + 1) * 128, :]), 16)
        C.Y(_y)

        def _v(v, i=i):
            C.waits(v, "LD")
            C.edge(v, "VW", v.reduce_sum(rs[:, 0:1], ft[:, :], AX.X))
            C.edge(v, "VW", v.reduce_max(rs[:, 1:2], ft[:, :], AX.X))
            C.edge(v, "VW", v.tensor_reduce(rs[:, 2:3], ft[:, :], AX.X,
                                            ALU.min))
            C.inc("VW", v.tensor_scalar(fstats[:, 2 * i:2 * i + 1], rs[:, 0:1],
                                        1.0 / 1024.0, None, ALU.mult))
        C.V(_v)

        def _s(s, i=i):
            C.waits(s, "VW")
            C.inc("SW", s.activation(sq[:, :], ft[:, :], AF.Square,
                                     accum_out=ssq[:, 0:1]))
        C.S(_s)

        def _v2(v, i=i):
            C.waits(v, "SW")
            mu = fstats[:, 2 * i:2 * i + 1]
            rstd = fstats[:, 2 * i + 1:2 * i + 2]
            C.edge(v, "VW", v.tensor_scalar(rs[:, 4:5], ssq[:, 0:1],
                                            1.0 / 1024.0, None, ALU.mult))
            C.edge(v, "VW", v.tensor_mul(rs[:, 5:6], mu, mu))
            C.edge(v, "VW", v.tensor_sub(rs[:, 4:5], rs[:, 4:5], rs[:, 5:6]))
            C.edge(v, "VW", v.tensor_scalar(rs[:, 4:5], rs[:, 4:5], 1e-5,
                                            None, ALU.add))
            C.inc("VW", v.engine_nop())
        C.V(_v2)

        def _s2(s, i=i):
            C.waits(s, "VW")
            C.inc("SW", s.activation(rs[:, 4:5], rs[:, 4:5], AF.Sqrt))
        C.S(_s2)

        def _v3(v, i=i):
            C.waits(v, "SW")
            mu = fstats[:, 2 * i:2 * i + 1]
            rstd = fstats[:, 2 * i + 1:2 * i + 2]
            C.edge(v, "VW", v.reciprocal(rstd, rs[:, 4:5]))
            C.edge(v, "VW", v.tensor_sub(rs[:, 1:2], rs[:, 1:2], mu))
            C.edge(v, "VW", v.tensor_sub(rs[:, 2:3], mu, rs[:, 2:3]))
            C.edge(v, "VW", v.tensor_max(rs[:, 1:2], rs[:, 1:2], rs[:, 2:3]))
            C.inc("VW", v.tensor_mul(fmx[:, i:i + 1], rs[:, 1:2], rstd))
        C.V(_v3)

    emit_absmax_tiles(C, [fmx[:, 0:NF]], half[0:1, 0:1])

    def _vc(v):
        C.barrier_all(v)
        C.inc("VW", v.tensor_max(half[0:1, 0:1], half[0:1, 0:1],
                                 gmax_sb[0:1, 0:1]))
    C.V(_vc)

    hd = C.nc.dram_tensor("fa_halfD", [2, 1], F32)

    def _yh(y):
        C.waits(y, "VW")
        C.inc("ST", y.dma_start(out=C.bufs["scratch1"].ap()[0:1, :],
                                in_=half[0:1, 0:1]), 16)
    C.Y(_yh)
    emit_pair_send(C, 2, 2, [(C.bufs["xMX1"], 1, 0, 1,
                              C.bufs["scratch1"].ap()[0:1, :])])
    emit_pair_recv(C, 2, 2, [(half[0:1, 1:2], C.bufs["xMX1"], 1, 0, 1)])

    def _vg(v):
        C.waits(v, "GD", "GW")
        C.inc("VW", v.tensor_max(half[0:1, 0:1], half[0:1, 0:1],
                                 half[0:1, 1:2]))
    C.V(_vg)

    gm8 = C.sb("fa_gm8", [1, 8], F32)

    def _vb(v):
        C.waits(v, "VW")
        for i in range(8):
            mm = v.tensor_copy(gm8[0:1, i:i + 1], half[0:1, 0:1])
        C.inc("VW", mm)
    C.V(_vb)

    def _yo(y):
        C.waits(y, "VW")
        C.inc("ST", y.dma_start(out=maxesD, in_=gm8[0:1, :]), 16)
    C.Y(_yo)


def emit_FB(C, x0f, fstats, hmaxesD):
    C.section("FB")
    NF = 48
    wgqT = C.bufs["wgqT"]
    wuqT = C.bufs["wuqT"]
    s1bc = C.bufs["s1bc"]
    r1bc = C.bufs["r1bc"]
    hgm = C.bufs["hgm"]
    NI = 2816
    ft = C.sb("fb_t", [128, D], F32)
    xqfb = C.sb("fb_xqb", [128, D], BF16)
    xqfT = C.sb("fb_xqT", [128, DC * 512], BF16)
    gchf = C.bufs["gch"]
    uchf = C.bufs["uch"]
    fmx = C.sb("fb_fmx", [128, 12], F32)
    half = C.sb("fb_half", [1, 2], F32)
    idb = C.bufs["ident_bf"][:, :]

    for fc in range(12):
        for lt in range(4):
            i = fc * 4 + lt

            def _y(y, i=i):
                C.barrier_all(y)
                C.inc("LD", y.dma_start(out=ft[:, :],
                                        in_=x0f[i * 128:(i + 1) * 128, :]), 16)
            C.Y(_y)

            def _v(v, i=i):
                C.waits(v, "LD", "TW")
                mu = fstats[:, 2 * i:2 * i + 1]
                rstd = fstats[:, 2 * i + 1:2 * i + 2]
                C.edge(v, "VW", v.tensor_scalar(ft[:, :], ft[:, :], mu, None,
                                                ALU.subtract))
                C.edge(v, "VW", v.tensor_scalar(ft[:, :], ft[:, :], rstd,
                                                None, ALU.mult))
                C.edge(v, "VW", v.tensor_scalar(ft[:, :], ft[:, :],
                                                r1bc[:, 0:1], None, ALU.mult))
                C.edge(v, "VW", v.tensor_scalar(ft[:, :], ft[:, :], MAGIC,
                                                -MAGIC, ALU.add, ALU.add))
                C.edge(v, "VW", v.tensor_scalar(ft[:, :], ft[:, :], QMAX,
                                                -QMAX - 1.0, ALU.min, ALU.max))
                C.inc("VW", v.tensor_scalar(xqfb[:, :], ft[:, :], s1bc[:, 0:1],
                                            None, ALU.mult))
            C.V(_v)

            pairs = [(xqfb[:, dc * 128:(dc + 1) * 128],
                      xqfT[:, dc * 512 + lt * 128: dc * 512 + (lt + 1) * 128])
                     for dc in range(DC)]
            emit_transposes(C, pairs, idb)

        fb_evac = [0, 0]
        fb_gtw = [0, 0]
        fb_vw = [0]
        for ic in range(NI // 128):
            for wi, (wT, dstn) in enumerate(((wgqT, "gch"), (wuqT, "uch"))):
                ps = C.bufs[f"pf{wi}"]

                def _t(t, ic=ic, wi=wi, wT=wT, ps=ps):
                    if ic == 0 and wi == 0:
                        C.barrier_all(t)
                    elif fb_evac[wi]:
                        t.wait_ge(C.s["SW"].h, fb_evac[wi])
                    for hc in range(DC):
                        mm = t.matmul(
                            ps[:, 0:512],
                            wT[:, hc * NI + ic * 128: hc * NI + (ic + 1) * 128],
                            xqfT[:, hc * 512:(hc + 1) * 512],
                            start=(hc == 0), stop=(hc == DC - 1))
                    C.inc("TW", mm)
                    fb_gtw[wi] = C.s["TW"].n
                C.T(_t)

                def _s(s, wi=wi, ps=ps, dstn=dstn):
                    s.wait_ge(C.s["TW"].h, fb_gtw[wi])
                    if fb_vw[0]:
                        s.wait_ge(C.s["VW"].h, fb_vw[0])
                    C.inc("SW", s.tensor_copy(C.bufs[dstn][:, 0:512],
                                              ps[:, 0:512]))
                    fb_evac[wi] = C.s["SW"].n
                C.V(_s)

            def _s2(s):
                C.waits(s, "SW")
                C.inc("SW", s.activation(gchf[:, 0:512], gchf[:, 0:512],
                                         AF.Silu))
            C.S(_s2)

            def _v2(v, fc=fc, ic=ic):
                C.waits(v, "SW")
                C.edge(v, "VW", v.tensor_mul(gchf[:, 0:512], gchf[:, 0:512],
                                             uchf[:, 0:512]))
                C.edge(v, "VW", v.reduce_max(uchf[:, 0:1], gchf[:, 0:512],
                                             AX.X, apply_absolute_value=True))
                if ic == 0:
                    C.inc("VW", v.tensor_copy(fmx[:, fc:fc + 1],
                                              uchf[:, 0:1]))
                else:
                    C.inc("VW", v.tensor_max(fmx[:, fc:fc + 1],
                                             fmx[:, fc:fc + 1],
                                             uchf[:, 0:1]))
                fb_vw[0] = C.s["VW"].n
            C.V(_v2)

    emit_absmax_tiles(C, [fmx[:, 0:12]], half[0:1, 0:1])

    def _vc(v):
        C.barrier_all(v)
        C.inc("VW", v.tensor_max(half[0:1, 0:1], half[0:1, 0:1],
                                 hgm[0:1, 0:1]))
    C.V(_vc)

    def _yh(y):
        C.waits(y, "VW")
        C.inc("ST", y.dma_start(out=C.bufs["scratch1"].ap()[0:1, :],
                                in_=half[0:1, 0:1]), 16)
    C.Y(_yh)
    emit_pair_send(C, 3, 3, [(C.bufs["xMX2"], 1, 0, 1,
                              C.bufs["scratch1"].ap()[0:1, :])])
    emit_pair_recv(C, 3, 3, [(half[0:1, 1:2], C.bufs["xMX2"], 1, 0, 1)])

    def _vg(v):
        C.waits(v, "GD", "GW")
        C.edge(v, "VW", v.tensor_max(half[0:1, 0:1], half[0:1, 0:1],
                                     half[0:1, 1:2]))
        for i in range(8):
            mm = v.tensor_copy(C.bufs["fb_gm8"][0:1, i:i + 1], half[0:1, 0:1])
        C.inc("VW", mm)
    C.V(_vg)

    def _yo(y):
        C.waits(y, "VW")
        C.inc("ST", y.dma_start(out=hmaxesD, in_=C.bufs["fb_gm8"][0:1, :]), 16)
    C.Y(_yo)


def build_fused():
    C = Ctx("FUSED").start()
    nc = C.nc
    E = {}

    def ein(name, shape, dt):
        E[name] = nc.dram_tensor("i_" + name, list(shape), dt,
                                 kind="ExternalInput").ap()
        return E[name]

    ein("x0", [T, D], F32)
    ein("x0f", [6144, D], F32)
    for nm, shp in (("Wq0", [D, D]), ("Wk0", [D, D]), ("Wv0", [D, D]),
                    ("Wo0", [D, D])):
        ein(nm, shp, F32)
    for nm in ("bq0", "bk0", "bv0", "bo0"):
        ein(nm, [1, D], F32)
    ein("gamma", [1, NH], F32)
    ein("idx", [1, T], F32)
    ein("clipx", [1, 1], F32)
    ein("clipd", [1, 1], F32)
    ein("wgate", [2816, D], F32)
    ein("wup", [2816, D], F32)
    ein("wdown", [D, 2816], F32)
    ein("Wq1", [D, D], F32)
    ein("Wk1", [D, D], F32)
    ein("bq1", [1, D], F32)
    ein("bk1", [1, D], F32)
    for l in range(2):
        ein(f"wqkv{l}", [D, 3 * INNER], F32)
        ein(f"qs{l}", [1, HD], F32)
        ein(f"ks{l}", [1, HD], F32)
        ein(f"wo{l}", [INNER, D], F32)
        ein(f"lg{l}", [1, D], F32)
        ein(f"lb{l}", [1, D], F32)
        ein(f"w1_{l}", [D, 2 * HID], F32)
        ein(f"b1_{l}", [1, 2 * HID], F32)
        ein(f"w2_{l}", [HID, D], F32)
        ein(f"b2_{l}", [1, D], F32)
    ein("tR1", [T, INNER], F32)
    ein("tR2", [T, INNER], F32)
    ein("tR3", [T, INNER], F32)
    ein("tR4", [T, INNER], F32)
    ein("tR3t", [WIN, INNER], F32)
    ein("tR4t", [WIN, INNER], F32)
    ein("mask0", [1024, WIN], BF16)
    ein("maskr", [1024, WIN], BF16)
    ein("myh", [1, 1], I32)
    ein("prh", [1, 1], I32)
    ein("useS", [1, 1], F32)

    o_out = nc.dram_tensor("o_out", [T, D], F32, kind="ExternalOutput").ap()

    def idr(name, shape, dt):
        if name in DBGF:
            return nc.dram_tensor("o_" + name + "D", list(shape), dt,
                                  kind="ExternalOutput").ap()
        return nc.dram_tensor(name + "D", list(shape), dt).ap()

    xnD = idr("xn", [T, D], F32)
    maxxD = idr("maxx", [1, 1], F32)
    SD = idr("S", [D, HD], F32)
    SprevD = idr("Sprev", [D, HD], F32)
    maxesD = idr("maxes", [1, 8], F32)
    a0D = idr("a0", [T, D], F32)
    xqD = idr("xq", [T, D], BF16)
    hmaxD = idr("hmax", [1, 1], F32)
    hmaxesD = idr("hmaxes", [1, 8], F32)
    x1D = idr("x1", [T, D], F32)
    xn1D = idr("xn1", [T, D], F32)
    k1D = idr("k1", [T, D], BF16)
    k1pD = idr("k1p", [T, D], BF16)
    a1D = idr("a1", [T, D], F32)
    zpD = idr("zp", [T, D], F32)
    z1D = idr("z1", [T, D], F32)
    zp1D = idr("zp1", [T, D], F32)
    ztail0D = idr("ztail0", [640, D], F32)
    ztail1D = idr("ztail1", [640, D], F32)
    zpt0D = idr("zpt0", [1, D], F32)
    zpt1D = idr("zpt1", [1, D], F32)

    C.bufs["xflg_dram"] = nc.dram_tensor("xflg", [2, 64], I32,
                                         addr_space="Shared")
    C.bufs["xMX1"] = nc.dram_tensor("xMX1", [2, 1], F32, addr_space="Shared")
    C.bufs["xMX2"] = nc.dram_tensor("xMX2", [2, 1], F32, addr_space="Shared")
    xS = nc.dram_tensor("xS", [2 * D, HD], F32, addr_space="Shared")
    xK = nc.dram_tensor("xK", [2 * T, D], BF16, addr_space="Shared")
    xT1 = nc.dram_tensor("xT1", [2 * 513, D], F32, addr_space="Shared")
    xZ1 = nc.dram_tensor("xZ1", [2 * 1, D], F32, addr_space="Shared")
    xT2 = nc.dram_tensor("xT2", [2 * 513, D], F32, addr_space="Shared")
    xZ2 = nc.dram_tensor("xZ2", [2 * 1, D], F32, addr_space="Shared")

    C.bufs["myh_sb"] = C.sb("myh_sb", [1, 1], I32)
    C.bufs["prh_sb"] = C.sb("prh_sb", [1, 1], I32)
    C.bufs["pollb"] = C.sb("pollb", [1, 2], I32)
    C.bufs["gens_sb"] = C.sb("gens_sb", [1, 64], I32)
    C.bufs["scratch1"] = nc.dram_tensor("scr1D", [1, 1], F32)
    fstats = C.sb("fstats", [128, 96], F32)
    C.bufs["fb_gm8"] = C.sb("fb_gm8", [1, 8], F32)
    zerot = C.sb("zerot", [128, D], F32)

    def _y0(y):
        C.inc("LD", y.dma_start(out=C.bufs["myh_sb"][:, :], in_=E["myh"]), 16)
        C.inc("LD", y.dma_start(out=C.bufs["prh_sb"][:, :], in_=E["prh"]), 16)
    C.Y(_y0)

    def _g0(g):
        C.inc("GW", g.iota(C.bufs["gens_sb"][:, :], [[1, 64]],
                           channel_multiplier=0,
                           allow_small_or_imprecise_dtypes=True))
    C.G(_g0)

    def _vz(v):
        C.inc("VW", v.memset(zerot[:, :], 0.0))
    C.V(_vz)

    def _yz(y):
        C.waits(y, "VW")
        C.inc("ST", y.dma_start(out=ztail0D[0:127, :], in_=zerot[0:127, :]), 16)
        C.inc("ST", y.dma_start(out=ztail1D[0:127, :], in_=zerot[0:127, :]), 16)
        y.wait_ge(C.s["ST"].h, C.s["ST"].n)
    C.Y(_yz)
    C.free("zerot")

    C.section("A")
    mk = C.push()
    C.io = {"x0": E["x0"], "Wk": E["Wk0"], "bk": E["bk0"], "Wv": E["Wv0"],
            "bv": E["bv0"], "xn": xnD, "maxx": maxxD, "S": SD}
    build_A(C)
    emit_pair_send(C, 1, 1, [(xS, D, 0, D, SD)])
    emit_FA(C, E["x0f"], fstats, C.bufs["gmax"], maxesD)
    C.pop(mk)

    emit_pair_recv(C, 1, 1, [(SprevD, xS, D, 0, D)])

    C.section("B")
    mk = C.push()
    C.io = {"xn": xnD, "Sprev": SprevD, "maxes": maxesD, "clipx": E["clipx"],
            "idx": E["idx"], "gamma": E["gamma"], "Wq": E["Wq0"],
            "bq": E["bq0"], "Wk": E["Wk0"], "bk": E["bk0"], "Wv": E["Wv0"],
            "bv": E["bv0"], "Wo": E["Wo0"], "bo": E["bo0"],
            "wgate": E["wgate"], "wup": E["wup"], "useSc": E["useS"],
            "a0": a0D, "xq": xqD, "hmax": hmaxD}
    build_B(C)
    emit_FB(C, E["x0f"], fstats, hmaxesD)
    C.pop(mk)

    C.section("C")
    mk = C.push()
    C.io = {"x0": E["x0"], "a0": a0D, "xq": xqD, "hmaxes": hmaxesD,
            "clipd": E["clipd"], "wgate": E["wgate"], "wup": E["wup"],
            "wdown": E["wdown"], "Wk1": E["Wk1"], "bk1": E["bk1"],
            "x1": x1D, "xn1": xn1D, "k1": k1D}
    build_C(C)
    C.pop(mk)

    emit_pair_send(C, 4, 4, [(xK, T, 0, T, k1D)])
    emit_pair_send(C, 5, 5, [(xT1, 513, 0, 513, xn1D[1535:2048, :])])
    emit_pair_recv(C, 4, 4, [(k1pD, xK, T, 0, T)])
    emit_pair_recv(C, 5, 5, [(ztail0D[127:640, :], xT1, 513, 0, 513)])

    C.section("D")
    mk = C.push()
    C.io = {"xn1": xn1D, "k_own": k1D, "k_part": k1pD, "first": maxxD,
            "Wq1": E["Wq1"], "bq1": E["bq1"], "a1": a1D}
    build_D(C)
    C.pop(mk)

    C.section("E1a")
    mk = C.push()
    C.io = {"z": xn1D, "ztail": ztail0D, "wqkv": E["wqkv0"], "qs": E["qs0"],
            "ks": E["ks0"], "wo": E["wo0"], "tR1": E["tR1"], "tR2": E["tR2"],
            "tR3": E["tR3"], "tR4": E["tR4"], "tR3t": E["tR3t"],
            "tR4t": E["tR4t"], "mask0": E["mask0"], "maskr": E["maskr"],
            "useSc": E["useS"], "zp": zpD}
    build_E1(C)
    C.pop(mk)
    emit_pair_send(C, 6, 6, [(xZ1, 1, 0, 1, zpD[2047:2048, :])])
    emit_pair_recv(C, 6, 6, [(zpt0D, xZ1, 1, 0, 1)])

    C.section("E2a")
    mk = C.push()
    C.io = {"zp": zpD, "zptail": zpt0D, "lg": E["lg0"], "lb": E["lb0"],
            "w1": E["w1_0"], "b1": E["b1_0"], "w2": E["w2_0"],
            "b2": E["b2_0"], "useSc": E["useS"], "z": z1D}
    build_E2(False, C)
    C.pop(mk)
    emit_pair_send(C, 7, 7, [(xT2, 513, 0, 513, z1D[1535:2048, :])])
    emit_pair_recv(C, 7, 7, [(ztail1D[127:640, :], xT2, 513, 0, 513)])

    C.section("F1a")
    mk = C.push()
    C.io = {"z": z1D, "ztail": ztail1D, "wqkv": E["wqkv1"], "qs": E["qs1"],
            "ks": E["ks1"], "wo": E["wo1"], "tR1": E["tR1"], "tR2": E["tR2"],
            "tR3": E["tR3"], "tR4": E["tR4"], "tR3t": E["tR3t"],
            "tR4t": E["tR4t"], "mask0": E["mask0"], "maskr": E["maskr"],
            "useSc": E["useS"], "zp": zp1D}
    build_E1(C)
    C.pop(mk)
    emit_pair_send(C, 8, 8, [(xZ2, 1, 0, 1, zp1D[2047:2048, :])])
    emit_pair_recv(C, 8, 8, [(zpt1D, xZ2, 1, 0, 1)])

    C.section("F2a")
    mk = C.push()
    C.io = {"zp": zp1D, "zptail": zpt1D, "lg": E["lg1"], "lb": E["lb1"],
            "w1": E["w1_1"], "b1": E["b1_1"], "w2": E["w2_1"],
            "b2": E["b2_1"], "useSc": E["useS"], "x1": x1D, "a1": a1D,
            "z": o_out}
    build_E2(True, C)
    C.pop(mk)

    C.io = {}
    C.section("")

    def _yend(y):
        y.wait_ge(C.s["ST"].h, C.s["ST"].n)
    C.Y(_yend)
    C.close()
    return C


FUSED = True


def kernel_fused(inputs):
    C = build_fused()
    im = []
    for c in range(8):
        b, h = c // 2, c % 2
        xs = shard_x(inputs["L0_attn_x"])
        bb = [(b + 1) % 4, (b + 2) % 4, (b + 3) % 4]
        x0 = inputs["L0_attn_x"]
        if h == 0:
            x0f = np.concatenate([x0[bb[0]], x0[bb[1]][:T]], 0)
        else:
            x0f = np.concatenate([x0[bb[1]][T:], x0[bb[2]]], 0)
        pos = np.arange(h * T, (h + 1) * T)
        tpos = np.arange(1536, 2048) if h == 1 else np.arange(512)
        q1, q2 = _rot_tables(pos, "q")
        k1, k2 = _rot_tables(pos, "k")
        kt1, kt2 = _rot_tables(tpos, "k")
        m = {
            "i_x0": xs[c],
            "i_x0f": np.ascontiguousarray(x0f),
            "i_Wq0": inputs["L0_attn_Wq"], "i_bq0": inputs["L0_attn_bq"].reshape(1, D),
            "i_Wk0": inputs["L0_attn_Wk"], "i_bk0": inputs["L0_attn_bk"].reshape(1, D),
            "i_Wv0": inputs["L0_attn_Wv"], "i_bv0": inputs["L0_attn_bv"].reshape(1, D),
            "i_Wo0": inputs["L0_attn_Wo"], "i_bo0": inputs["L0_attn_bo"].reshape(1, D),
            "i_gamma": inputs["L0_attn_gamma"].reshape(1, NH),
            "i_idx": np.arange(h * T, (h + 1) * T, dtype=np.float32).reshape(1, T),
            "i_clipx": inputs["L0_ffn_clip_x"].reshape(1, 1),
            "i_clipd": inputs["L0_ffn_clip_down"].reshape(1, 1),
            "i_wgate": inputs["L0_ffn_w_gate"],
            "i_wup": inputs["L0_ffn_w_up"],
            "i_wdown": inputs["L0_ffn_w_down"],
            "i_Wq1": inputs["L1_attn_Wq"], "i_bq1": inputs["L1_attn_bq"].reshape(1, D),
            "i_Wk1": inputs["L1_attn_Wk"], "i_bk1": inputs["L1_attn_bk"].reshape(1, D),
            "i_tR1": q1, "i_tR2": q2, "i_tR3": k1, "i_tR4": k2,
            "i_tR3t": kt1, "i_tR4t": kt2,
            "i_mask0": _inner_masks(h == 0).astype(ml_dtypes.bfloat16),
            "i_maskr": _inner_masks(False).astype(ml_dtypes.bfloat16),
            "i_myh": np.array([[h]], np.int32),
            "i_prh": np.array([[1 - h]], np.int32),
            "i_useS": np.array([[float(h)]], np.float32),
        }
        for l in range(2):
            m[f"i_wqkv{l}"] = np.ascontiguousarray(inputs["L1_ffn_wqkv"][l])
            m[f"i_qs{l}"] = inputs["L1_ffn_q_scale"][l].reshape(1, HD)
            m[f"i_ks{l}"] = inputs["L1_ffn_k_scale"][l].reshape(1, HD)
            m[f"i_wo{l}"] = np.ascontiguousarray(inputs["L1_ffn_wo"][l])
            m[f"i_lg{l}"] = inputs["L1_ffn_ln_g"][l].reshape(1, D)
            m[f"i_lb{l}"] = inputs["L1_ffn_ln_b"][l].reshape(1, D)
            m[f"i_w1_{l}"] = np.ascontiguousarray(inputs["L1_ffn_w1"][l])
            m[f"i_b1_{l}"] = inputs["L1_ffn_b1"][l].reshape(1, 2 * HID)
            m[f"i_w2_{l}"] = np.ascontiguousarray(inputs["L1_ffn_w2"][l])
            m[f"i_b2_{l}"] = inputs["L1_ffn_b2"][l].reshape(1, D)
        im.append(m)
    res = run_bass_kernel_spmd(C.nc, im, core_ids=list(range(8)))
    return unshard([res.results[c]["o_out"] for c in range(8)])


def unshard(shards):
    out = np.empty((4, 2 * T, D), np.float32)
    for b in range(4):
        for h in range(2):
            out[b, h * T:(h + 1) * T, :] = shards[2 * b + h]
    return out


def kernel(**inputs):
    if FUSED:
        return kernel_fused(inputs)
    resA = run_A(inputs)
    resB = run_B(inputs, resA)
    resC = run_C(inputs, resB)
    resD = run_D(inputs, resC)
    xn1_sh = [resC[c]["o_xn1"] for c in range(8)]
    zp0 = run_E1(inputs, 0, xn1_sh)
    zp0_sh = [zp0[c]["o_zp"] for c in range(8)]
    z1 = run_E2(inputs, 0, zp0_sh)
    z1_sh = [z1[c]["o_z"] for c in range(8)]
    zp1 = run_E1(inputs, 1, z1_sh)
    zp1_sh = [zp1[c]["o_zp"] for c in range(8)]
    fin = run_E2(inputs, 1, zp1_sh, final=True,
                 x1_shards=[resC[c]["o_x1"] for c in range(8)],
                 a1_shards=[resD[c]["o_a1"] for c in range(8)])
    return unshard([fin[c]["o_z"] for c in range(8)])
